# version 29
# baseline (speedup 1.0000x reference)
import numpy as np
from contextlib import ExitStack

import concourse.bass as bass
import concourse.mybir as mybir
from concourse.bass_utils import run_bass_kernel_spmd

F32 = mybir.dt.float32
BF16 = mybir.dt.bfloat16
AF = mybir.ActivationFunctionType
ALU = mybir.AluOpType

P = 128
D = 1024
NCH = D // P
T = 4096
TT = 256
NT = T // TT
EPS = 1e-5
N_CORES = 8

SV_DL = 0
SV_BATT = 8
SV_CONVB = 16
SV_B1 = 24
SV_B2 = 32
SV_B3 = 36
SV_GB1 = 38
SV_GB2 = 42
SV_CW = 44
SV = 84


class Sched:
    ENGINES = ("sync", "gpsimd", "vector", "scalar", "tensor")

    def __init__(self):
        self.ops = {e: [] for e in self.ENGINES}
        self.cnt = {s: 0 for s in ("di", "dw0", "dw1", "dx0", "dx1", "dx2", "dx3", "dx4", "dx5", "dx6",
                           "dr", "dr2", "do0", "do1", "do2", "dd", "t", "v", "s", "g")}

    def emit(self, eng, fn, waits=(), inc=None, tag=""):
        waits = [(s, v) for (s, v) in waits if v > 0]
        self.ops[eng].append((fn, waits, inc, tag))
        if inc is not None:
            self.cnt[inc[0]] += inc[1]

    def mark(self, sem):
        return self.cnt[sem]

    def simulate(self):
        pos = {e: 0 for e in self.ENGINES}
        val = {s: 0 for s in self.cnt}
        progress = True
        while progress:
            progress = False
            for e in self.ENGINES:
                while pos[e] < len(self.ops[e]):
                    fn, waits, inc, tag = self.ops[e][pos[e]]
                    if all(val[s] >= v for (s, v) in waits):
                        if inc:
                            val[inc[0]] += inc[1]
                        pos[e] += 1
                        progress = True
                    else:
                        break
        stuck = {e: pos[e] for e in self.ENGINES if pos[e] < len(self.ops[e])}
        if stuck:
            lines = []
            for e, p_ in stuck.items():
                fn, waits, inc, tag = self.ops[e][p_]
                lines.append(f"  {e}[{p_}] tag={tag} waits={waits} "
                             f"(now={[(s, val[s]) for s, _ in waits]})")
            raise RuntimeError("DEADLOCK in schedule:\n" + "\n".join(lines))


def build_nc():
    nc = bass.Bass("TRN2", target_bir_lowering=False)

    xt = nc.dram_tensor("x_t", [NT, P, NCH * TT], F32, kind="ExternalInput").ap()
    wts_h = nc.dram_tensor("w_ret_t", [D, D], F32, kind="ExternalInput").ap()
    w1_h = nc.dram_tensor("cn_w1", [D, D], F32, kind="ExternalInput").ap()
    w2_h = nc.dram_tensor("cn_w2", [D, 512], F32, kind="ExternalInput").ap()
    w3_h = nc.dram_tensor("cn_w3", [512, 256], F32, kind="ExternalInput").ap()
    w4_h = nc.dram_tensor("cn_w4", [P, 2], F32, kind="ExternalInput").ap()
    g1_h = nc.dram_tensor("gc_w1", [D, 512], F32, kind="ExternalInput").ap()
    g2_h = nc.dram_tensor("gc_w2", [512, 256], F32, kind="ExternalInput").ap()
    g3_h = nc.dram_tensor("gc_w3", [P, 2], F32, kind="ExternalInput").ap()
    sv_h = nc.dram_tensor("svec", [P, SV], F32, kind="ExternalInput").ap()
    sr_h = nc.dram_tensor("srow", [1, 4], F32, kind="ExternalInput").ap()
    yt = nc.dram_tensor("y_t", [NT, P, NCH * TT], F32, kind="ExternalOutput").ap()

    import os
    KDBG = bool(int(os.environ.get("KDBG", "0")))
    if KDBG:
        dbg_xn = nc.dram_tensor("dbg_xn", [D, T], F32, kind="ExternalOutput").ap()
        dbg_sv = nc.dram_tensor("dbg_sv", [D, T], BF16, kind="ExternalOutput").ap()
        dbg_s = nc.dram_tensor("dbg_s", [D, T], F32, kind="ExternalOutput").ap()
        dbg_of = nc.dram_tensor("dbg_of", [D, T], F32, kind="ExternalOutput").ap()
        dbg_xn_r = dbg_xn.rearrange("(c p) t -> p c t", p=P)
        dbg_sv_r = dbg_sv.rearrange("(c p) t -> p c t", p=P)
        dbg_s_r = dbg_s.rearrange("(c p) t -> p c t", p=P)
        dbg_of_r = dbg_of.rearrange("(c p) t -> p c t", p=P)
        dbg_sm = nc.dram_tensor("dbg_sm", [P, SV + 48], F32, kind="ExternalOutput").ap()
        dbg_w = nc.dram_tensor("dbg_w", [P, NCH * D], BF16, kind="ExternalOutput").ap()
        dbg_w2 = nc.dram_tensor("dbg_w2", [P, NCH * D], BF16, kind="ExternalOutput").ap()
        dbg_dg = nc.dram_tensor("dbg_dg", [P, 40 * P], BF16, kind="ExternalOutput").ap()

    xt_r = xt.rearrange("k p (c t) -> k p c t", c=NCH)
    yt_r = yt.rearrange("k p (c t) -> k p c t", c=NCH)

    sch = Sched()
    E = sch.emit
    M = sch.mark
    sig = {}

    with ExitStack() as ctx:
        e_ = ctx.enter_context
        wtsb = e_(nc.sbuf_tensor("wtsb", [P, NCH, D], BF16))
        w1b = e_(nc.sbuf_tensor("w1b", [P, NCH, D], BF16))
        w2b = e_(nc.sbuf_tensor("w2b", [P, NCH, 512], BF16))
        w3b = e_(nc.sbuf_tensor("w3b", [P, 4, 256], BF16))
        w4b = e_(nc.sbuf_tensor("w4b", [P, 2, 1], BF16))
        gw1b = e_(nc.sbuf_tensor("gw1b", [P, NCH, 512], BF16))
        gw2b = e_(nc.sbuf_tensor("gw2b", [P, 4, 256], BF16))
        gw3b = e_(nc.sbuf_tensor("gw3b", [P, 2, 1], BF16))
        diag = e_(nc.sbuf_tensor("diag", [P, 40, P], BF16))
        ones_sq_b = e_(nc.sbuf_tensor("ones_sq_b", [P, P], BF16))
        ones_r1b = e_(nc.sbuf_tensor("ones_r1b", [1, P], BF16))
        ones_r1f = e_(nc.sbuf_tensor("ones_r1f", [33, P], F32))
        svec = e_(nc.sbuf_tensor("svec_s", [P, SV], F32))
        a_t = e_(nc.sbuf_tensor("a_t", [P, NCH], F32))
        a_s = e_(nc.sbuf_tensor("a_s", [P, NCH], F32))
        oma = e_(nc.sbuf_tensor("oma", [P, NCH], F32))
        bsum = e_(nc.sbuf_tensor("bsum", [P, NCH], F32))
        c0col = e_(nc.sbuf_tensor("c0col", [P, 1], F32))
        c1col = e_(nc.sbuf_tensor("c1col", [P, 1], F32))
        carry = e_(nc.sbuf_tensor("carry", [P, NCH], F32))
        zacc = e_(nc.sbuf_tensor("zacc", [P, NCH * NT], F32))
        gcv = e_(nc.sbuf_tensor("gcv", [P, NCH], F32))
        gcvb = e_(nc.sbuf_tensor("gcvb", [P, NCH], BF16))
        gch1b = e_(nc.sbuf_tensor("gch1b", [P, 4], BF16))
        gch2b = e_(nc.sbuf_tensor("gch2b", [P, 2], BF16))
        csum = e_(nc.sbuf_tensor("csum", [1, NT], F32))
        rows = e_(nc.sbuf_tensor("rows", [33, T], F32))
        rs1 = e_(nc.sbuf_tensor("rs1", [1, 4 * TT], F32))
        rs2 = e_(nc.sbuf_tensor("rs2", [1, 4 * TT], F32))
        rtmp = e_(nc.sbuf_tensor("rtmp", [1, 16], F32))
        t4b = e_(nc.sbuf_tensor("t4b", [1, TT], BF16))
        t4f = e_(nc.sbuf_tensor("t4f", [P, TT], BF16))
        xs2 = [e_(nc.sbuf_tensor(f"xs2_{i}", [P, NCH, TT], F32)) for i in range(3)]
        xn = [e_(nc.sbuf_tensor(f"xn_{i}", [P, NCH, TT], F32)) for i in range(2)]
        xnb = [e_(nc.sbuf_tensor(f"xnb_{i}", [P, NCH, TT + 4], BF16)) for i in range(2)]
        xnc = [e_(nc.sbuf_tensor(f"xnc_{i}", [P, NCH, TT + 4], BF16)) for i in range(2)]
        scanb = e_(nc.sbuf_tensor("scanb", [P, NCH, TT], F32))
        svb = [e_(nc.sbuf_tensor(f"svb_{i}", [P, NCH, TT], BF16)) for i in range(2)]
        shalf = [e_(nc.sbuf_tensor(f"shalf_{i}", [P, NCH, TT], F32)) for i in range(2)]
        of = e_(nc.sbuf_tensor("of", [P, NCH, TT], F32))
        ofB = e_(nc.sbuf_tensor("ofB", [P, NCH, TT], F32))
        h1b = e_(nc.sbuf_tensor("h1b", [P, NCH, TT], BF16))
        h2b = e_(nc.sbuf_tensor("h2b", [P, 4, TT], BF16))
        h3b = e_(nc.sbuf_tensor("h3b", [P, 2, TT], BF16))
        cplb = e_(nc.sbuf_tensor("cplb", [P, TT], F32))
        pb = [e_(nc.psum_tensor(f"pb{i}", [P, 512], F32)) for i in range(8)]
        sems = {n: e_(nc.semaphore(n)) for n in
                ("di", "dw0", "dw1", "dx0", "dx1", "dx2", "dx3", "dx4", "dx5", "dx6",
                 "dr", "dr2", "do0", "do1", "do2", "dd", "t", "v", "s", "g")}
        block = e_(nc.Block())

        E("sync", lambda g: g.dma_start(out=svec[:], in_=sv_h), inc=("di", 16), tag="ld_svec")
        E("sync", lambda g: g.dma_start(out=rtmp[0:1, 0:4], in_=sr_h), inc=("di", 16), tag="ld_srow")
        E("sync", lambda g: g.dma_start(out=cplb[:, 0:2], in_=w4_h),
          inc=("di", 16), tag="ld_w4")
        E("sync", lambda g: g.dma_start(out=cplb[:, 2:4], in_=g3_h),
          inc=("di", 16), tag="ld_g3")
        di_small = M("di")

        E("scalar", lambda g: g.activation(out=a_t[:], in_=svec[:, SV_DL:SV_DL + 8],
                                           func=AF.Tanh, scale=0.5),
          waits=[("di", di_small)], inc=("s", 1), tag="a_tanh")
        s_at = M("s")
        E("vector", lambda g: g.tensor_scalar(out=a_s[:], in0=a_t[:], scalar1=0.5,
                                              scalar2=0.5, op0=ALU.mult, op1=ALU.add),
          waits=[("s", s_at)], tag="a_s")
        E("vector", lambda g: g.tensor_scalar(out=oma[:], in0=a_t[:], scalar1=-0.5,
                                              scalar2=0.5, op0=ALU.mult, op1=ALU.add),
          tag="oma")
        E("vector", lambda g: g.scalar_tensor_tensor(
            out=bsum[:], in0=svec[:, SV_BATT:SV_BATT + 8], scalar=1.0,
            in1=svec[:, SV_CONVB:SV_CONVB + 8], op0=ALU.mult, op1=ALU.add),
          waits=[("di", di_small)], tag="bsum0")
        E("vector", lambda g: g.memset(ones_sq_b[:], 1.0 / D), tag="ones1")
        E("vector", lambda g: g.memset(ones_r1b[:], 1.0), tag="ones2")
        E("vector", lambda g: g.memset(ones_r1f[:], 1.0), tag="ones3")
        E("vector", lambda g: g.memset(carry[:], 0.0), tag="carry0")
        E("vector", lambda g: g.tensor_scalar(out=rtmp[0:1, 4:5], in0=rtmp[0:1, 0:1],
                                              scalar1=0.5, scalar2=None, op0=ALU.mult),
          waits=[("di", di_small)], tag="b4h")
        E("vector", lambda g: g.tensor_scalar(out=rtmp[0:1, 5:6], in0=rtmp[0:1, 1:2],
                                              scalar1=0.5, scalar2=None, op0=ALU.mult),
          inc=("v", 1), tag="gb3h")
        v_prep = M("v")
        if KDBG:
            E("gpsimd", lambda g: g.dma_start(out=dbg_sm[:, SV + 16:SV + 24], in_=bsum[:]),
              waits=[("v", v_prep)], inc=("dd", 16), tag="dbsum_prep")

        for c in range(NCH):
            for k in range(5):
                idx = c * 5 + k
                col = SV_CW + idx
                E("gpsimd",
                  (lambda col_, idx_: lambda g: g.affine_select(
                      out=diag[:, idx_, :],
                      in_=svec[:, col_:col_ + 1].broadcast_to([P, P]),
                      pattern=[[1, P]], compare_op=ALU.is_equal, fill=0.0,
                      base=0, channel_multiplier=-1))(col, idx),
                  waits=[("di", di_small)] if idx == 0 else (),
                  inc=("g", 1) if idx == 39 else None, tag=f"diag{idx}")
        g_diag = M("g")

        E("vector", lambda g: g.tensor_copy(out=w4b[:, :, 0], in_=cplb[:, 0:2]),
          waits=[("di", di_small)], tag="cast_w4")
        E("vector", lambda g: g.tensor_copy(out=gw3b[:, :, 0], in_=cplb[:, 2:4]),
          inc=("v", 1), tag="cast_g3")
        tiny_cast = M("v")

        for j0 in range(5):
            E("sync", (lambda j_: lambda g: g.dma_start(
                out=(xs2 + shalf + xn)[j_][:], in_=xt_r[j_]))(j0),
              inc=(f"dx{j0 % 7}", 16), tag=f"p1ld{j0}")
            sig[("ld1", j0)] = M(f"dx{j0 % 7}")

        wstage = [of, scanb]
        wjobs = [
            (wts_h.rearrange("(c p) m -> p c m", p=P), wtsb, 8, D, True),
            (w1_h.rearrange("(c p) m -> p c m", p=P), w1b, 8, D, False),
            (w2_h.rearrange("(c p) m -> p c m", p=P), w2b, 8, 512, False),
            (w3_h.rearrange("(c p) m -> p c m", p=P), w3b, 4, 256, False),
            (g1_h.rearrange("(c p) m -> p c m", p=P), gw1b, 8, 512, False),
            (g2_h.rearrange("(c p) m -> p c m", p=P), gw2b, 4, 256, False),
        ]
        segs = []
        for (srcw, dst, nch_w, cols, scaled) in wjobs:
            step = max(1, 2048 // cols)
            for c0 in range(0, nch_w, step):
                segs.append((srcw, dst, c0, min(step, nch_w - c0), cols, scaled))
        wl = 0
        last_cast = [tiny_cast] * len(wstage)
        for (srcw, dst, c0, nch_s, cols, scaled) in segs:
            h = wl % len(wstage)
            stg = wstage[h][:].rearrange("p a t -> p (a t)")
            E("sync",
              (lambda srcw_, c0_, n_, cols_, stg_:
               lambda g: g.dma_start(
                   out=stg_[:, 0:n_ * cols_].rearrange("p (c m) -> p c m", c=n_),
                   in_=srcw_[:, c0_:c0_ + n_, :]))(srcw, c0, nch_s, cols, stg),
              waits=[("v", last_cast[h])], inc=(f"dw{wl % 2}", 16), tag=f"wld{wl}")
            di_w = M(f"dw{wl % 2}")
            if scaled:
                for ci in range(nch_s):
                    E("vector", (lambda dst_, c0_, ci_, cols_, stg_:
                                 lambda g: g.tensor_scalar(
                                     out=dst_[:, c0_ + ci_, :],
                                     in0=stg_[:, ci_ * cols_:(ci_ + 1) * cols_],
                                     scalar1=oma[:, c0_ + ci_:c0_ + ci_ + 1],
                                     scalar2=None, op0=ALU.mult))(dst, c0, ci, cols, stg),
                      waits=[(f"dw{wl % 2}", di_w), ("v", v_prep)] if ci == 0 else [],
                      inc=("v", 1) if ci == nch_s - 1 else None, tag=f"wcast{wl}_{ci}")
            else:
                E("vector", (lambda dst_, c0_, n_, cols_, stg_:
                             lambda g: g.tensor_copy(
                                 out=dst_[:, c0_:c0_ + n_, :],
                                 in_=stg_[:, 0:n_ * cols_].rearrange(
                                     "p (c m) -> p c m", c=n_)))(dst, c0, nch_s, cols, stg),
                  waits=[(f"dw{wl % 2}", di_w)], inc=("v", 1), tag=f"wcast{wl}")
            last_cast[h] = M("v")
            wl += 1
        v_weights = M("v")
        if KDBG:
            E("gpsimd", lambda g: g.dma_start(
                out=dbg_w2.rearrange("p (c m) -> p c m", c=NCH), in_=wtsb[:]),
              waits=[("v", v_weights)], inc=("dd", 16), tag="dw2")

        p1x = xs2 + shalf + xn
        NX1 = len(p1x)
        for j in range(NT + 3):
            if j < NT:
                if j >= 5:
                    lw = [("v", sig[("z", j - NX1)])] if j >= NX1 else []
                    E("sync", (lambda j_: lambda g: g.dma_start(
                        out=p1x[j_ % NX1][:], in_=xt_r[j_]))(j),
                      waits=lw, inc=(f"dx{j % 7}", 16), tag=f"p1ld{j}")
                    sig[("ld1", j)] = M(f"dx{j % 7}")
                cw = [(f"dx{j % 7}", sig[("ld1", j)])]
                if j >= 2:
                    cw.append(("t", sig[("sum", j - 2)]))
                E("scalar", (lambda j_: lambda g: g.activation(
                    out=svb[j_ % 2][:], in_=p1x[j_ % NX1][:], func=AF.Copy))(j),
                  waits=cw, inc=("s", 1), tag=f"p1cast{j}")
                sig[("xb", j)] = M("s")
                sqw = [("s", sig[("xb", j)])]
                if j >= 2:
                    sqw.append(("t", sig[("sum", j - 2)]))
                sqb = h1b[:] if j % 2 == 0 else xnb[0][:, :, 2:TT + 2]
                E("vector", (lambda j_, sqb_: lambda g: g.tensor_mul(
                    out=sqb_, in0=svb[j_ % 2][:], in1=svb[j_ % 2][:]))(j, sqb),
                  waits=sqw, inc=("v", 1), tag=f"p1sq{j}")
                sig[("sq", j)] = M("v")
                sig[("sqb", j)] = sqb
            jj = j - 1
            if 0 <= jj < NT:
                sl = slice(jj * TT, (jj + 1) * TT)
                pS = 0 if jj % 2 == 0 else 2
                mmw = [("v", sig[("sq", jj)])]
                if jj == 0:
                    mmw.append(("v", v_prep))
                if jj >= 2:
                    mmw.append(("s", sig[("mq", jj - 2)]))
                for c in range(NCH):
                    E("tensor", (lambda j_, c_, pS_: lambda g: g.matmul(
                        out=pb[pS_][:, 0:TT], lhsT=ones_sq_b[:],
                        rhs=svb[j_ % 2][:, c_, :],
                        start=(c_ == 0), stop=(c_ == NCH - 1)))(jj, c, pS),
                      waits=mmw if c == 0 else (), tag=f"p1mmx{jj}_{c}")
                sqb = sig[("sqb", jj)]
                for c in range(NCH):
                    E("tensor", (lambda c_, pS_, sqb_: lambda g: g.matmul(
                        out=pb[pS_ + 1][:, 0:TT], lhsT=ones_sq_b[:],
                        rhs=sqb_[:, c_, :],
                        start=(c_ == 0), stop=(c_ == NCH - 1)))(c, pS, sqb),
                      inc=("t", 1) if c == NCH - 1 else None, tag=f"p1mmq{jj}_{c}")
                sig[("sum", jj)] = M("t")
                E("scalar", (lambda sl_, pS_: lambda g: g.activation(
                    out=rows[0:1, sl_], in_=pb[pS_][0:1, 0:TT], func=AF.Copy))(sl, pS),
                  waits=[("t", sig[("sum", jj)])], tag=f"p1m{jj}")
                ro = (jj % 4) * TT
                qw = [("v", sig[("z", jj - 4)])] if jj >= 4 else []
                E("scalar", (lambda pS_, ro_: lambda g: g.activation(
                    out=rs1[0:1, ro_:ro_ + TT], in_=pb[pS_ + 1][0:1, 0:TT],
                    func=AF.Copy))(pS, ro),
                  waits=qw, inc=("s", 1), tag=f"p1q{jj}")
                sig[("mq", jj)] = M("s")
            jp = j - 2
            if jp >= 1 and jp % 2 == 1 and jp < NT:
                pair = jp // 2
                TT2 = 2 * TT
                pbase = (pair % 2) * TT2
                psl = slice((jp - 1) * TT, (jp + 1) * TT)
                pR = 4 + pair % 2
                mmw2 = [("s", sig[("mq", jp)])]
                if pair >= 2:
                    mmw2.append(sig[("rrowp", pair - 2)])
                    mmw2.append(("t", sig[("bcp", pair - 2)]))
                E("scalar", (lambda psl_, pb_: lambda g: g.activation(
                    out=rs2[0:1, pb_:pb_ + TT2], in_=rows[0:1, psl_],
                    func=AF.Square))(psl, pbase),
                  waits=mmw2, inc=("s", 1), tag=f"p1mm{pair}")
                sm_sq = M("s")
                E("vector", (lambda pb_: lambda g: g.scalar_tensor_tensor(
                    out=rs1[0:1, pb_:pb_ + TT2], in0=rs1[0:1, pb_:pb_ + TT2],
                    scalar=float(EPS), in1=rs2[0:1, pb_:pb_ + TT2],
                    op0=ALU.add, op1=ALU.subtract))(pbase),
                  waits=[("s", sm_sq)], inc=("v", 1), tag=f"p1var{pair}")
                sig[("var", pair)] = M("v")
                E("scalar", (lambda pb_: lambda g: g.activation(
                    out=rs1[0:1, pb_:pb_ + TT2], in_=rs1[0:1, pb_:pb_ + TT2],
                    func=AF.Ln))(pbase),
                  waits=[("v", sig[("var", pair)])], inc=("s", 1), tag=f"p1ln{pair}")
                sm_ln = M("s")
                E("scalar", (lambda pb_: lambda g: g.activation(
                    out=rs2[0:1, pb_:pb_ + TT2], in_=rs1[0:1, pb_:pb_ + TT2],
                    func=AF.Exp, scale=-0.5))(pbase),
                  waits=[("s", sm_ln)], inc=("s", 1), tag=f"p1exp{pair}")
                sig[("rstdp", pair)] = M("s")
                sig[("rstd", jp - 1)] = sig[("rstd", jp)] = sig[("rstdp", pair)]
                rsem = "dr" if pair % 2 == 0 else "dr2"
                E("gpsimd", (lambda psl_, pb_: lambda g: g.dma_start(
                    out=rows[32:33, psl_], in_=rs2[0:1, pb_:pb_ + TT2]))(psl, pbase),
                  waits=[("s", sig[("rstdp", pair)])], inc=(rsem, 16),
                  tag=f"rrow{pair}")
                sig[("rrowp", pair)] = (rsem, M(rsem))
                sig[("rrow", jp - 1)] = sig[("rrow", jp)] = sig[("rrowp", pair)]
                bw = [("s", sig[("rstdp", pair)])]
                if pair >= 1:
                    bw.append(("v", sig[("z", jp - 2)]))
                E("tensor", (lambda pR_, pb_: lambda g: g.matmul(
                    out=pb[pR_][:, 0:TT2], lhsT=ones_r1f[0:1, :],
                    rhs=rs2[0:1, pb_:pb_ + TT2],
                    start=True, stop=True))(pR, pbase),
                  waits=bw, inc=("t", 1), tag=f"p1bc{pair}")
                sig[("bcp", pair)] = M("t")
                for jz, po in ((jp - 1, 0), (jp, TT)):
                    for c in range(NCH):
                        zw = []
                        if po == 0 and c == 0:
                            zw.append(("t", sig[("bcp", pair)]))
                        if po != 0 and c == 0:
                            zw.append(("v", M("v") if False else sig[("zhalf", pair)]))
                        E("vector", (lambda jz_, c_, pR_, po_: lambda g: g.scalar_tensor_tensor(
                            out=ofB[:, c_, :], in0=p1x[jz_ % NX1][:, c_, :], scalar=1.0,
                            in1=pb[pR_][:, po_:po_ + TT], op0=ALU.mult, op1=ALU.mult,
                            accum_out=zacc[:, c_ * NT + jz_:c_ * NT + jz_ + 1]))(jz, c, pR, po),
                          waits=zw,
                          inc=("v", 1) if (po == 0 and c == NCH - 1) else None,
                          tag=f"p1z{jz}_{c}")
                        if po == 0 and c == NCH - 1:
                            sig[("zhalf", pair)] = M("v")
                E("vector", (lambda psl_, pb_: lambda g: g.tensor_mul(
                    out=rs1[0:1, pb_:pb_ + TT2], in0=rows[0:1, psl_],
                    in1=rs2[0:1, pb_:pb_ + TT2]))(psl, pbase),
                  waits=[("s", sig[("rstdp", pair)])], inc=("v", 1), tag=f"p1cr{pair}")
                vm_cr = M("v")
                E("vector", (lambda pair_, pb_: lambda g: g.reduce_sum(
                    out=csum[0:1, pair_:pair_ + 1], in_=rs1[0:1, pb_:pb_ + TT2],
                    axis=mybir.AxisListType.X))(pair, pbase),
                  waits=[("v", vm_cr)], inc=("v", 1), tag=f"p1cs{pair}")
                sig[("z", jp - 1)] = sig[("z", jp)] = M("v")

        if KDBG:
            E("gpsimd", lambda g: g.dma_start(out=dbg_sm[:, SV + 24:SV + 32], in_=bsum[:]),
              waits=[("v", sig[("z", NT - 1)])], inc=("dd", 16), tag="dbsum_p1")
        E("vector", lambda g: g.reduce_sum(
            out=gcv[:], in_=zacc[:].rearrange("p (c j) -> p c j", j=NT),
            axis=mybir.AxisListType.X),
          waits=[("v", sig[("z", NT - 1)])], inc=("v", 1), tag="zred")
        v_zred = M("v")
        E("vector", lambda g: g.reduce_sum(out=rtmp[0:1, 6:7],
                                           in_=csum[0:1, 0:NT // 2],
                                           axis=mybir.AxisListType.X),
          inc=("v", 1), tag="csred")
        v_gc0 = M("v")
        E("tensor", lambda g: g.matmul(out=pb[7][:, 0:1], lhsT=ones_r1f[0:1, :],
                                       rhs=rtmp[0:1, 6:7], start=True, stop=True),
          waits=[("v", v_gc0)], inc=("t", 1), tag="csbc")
        t_csbc = M("t")
        E("vector", lambda g: g.tensor_scalar(out=gcv[:], in0=gcv[:],
                                              scalar1=pb[7][:, 0:1], scalar2=float(1.0 / T),
                                              op0=ALU.subtract, op1=ALU.mult),
          waits=[("t", t_csbc), ("v", v_zred)], inc=("v", 1), tag="gcv")
        v_gcvop = M("v")
        E("vector", lambda g: g.tensor_copy(out=gcvb[:], in_=gcv[:]),
          waits=[("v", v_gcvop)], inc=("v", 1), tag="gcvb")
        v_gcv = M("v")
        for m in range(4):
            for c in range(NCH):
                E("tensor", (lambda m_, c_: lambda g: g.matmul(
                    out=pb[0][:, m_:m_ + 1], lhsT=gw1b[:, c_, m_ * P:(m_ + 1) * P],
                    rhs=gcvb[:, c_:c_ + 1], start=(c_ == 0), stop=(c_ == NCH - 1)))(m, c),
                  waits=[("v", v_gcv)] if (m == 0 and c == 0) else (),
                  inc=("t", 1) if (m == 3 and c == NCH - 1) else None, tag=f"gch1_{m}_{c}")
        t_g1 = M("t")
        for m in range(4):
            E("scalar", (lambda m_: lambda g: g.activation(
                out=gch1b[:, m_:m_ + 1], in_=pb[0][:, m_:m_ + 1], func=AF.Gelu,
                bias=svec[:, SV_GB1 + m_:SV_GB1 + m_ + 1]))(m),
              waits=[("t", t_g1)] if m == 0 else (),
              inc=("s", 1) if m == 3 else None, tag=f"gce1_{m}")
        s_g1 = M("s")
        for m in range(2):
            for c in range(4):
                E("tensor", (lambda m_, c_: lambda g: g.matmul(
                    out=pb[1][:, m_:m_ + 1], lhsT=gw2b[:, c_, m_ * P:(m_ + 1) * P],
                    rhs=gch1b[:, c_:c_ + 1], start=(c_ == 0), stop=(c_ == 3)))(m, c),
                  waits=[("s", s_g1)] if (m == 0 and c == 0) else (),
                  inc=("t", 1) if (m == 1 and c == 3) else None, tag=f"gch2_{m}_{c}")
        t_g2 = M("t")
        for m in range(2):
            E("scalar", (lambda m_: lambda g: g.activation(
                out=gch2b[:, m_:m_ + 1], in_=pb[1][:, m_:m_ + 1], func=AF.Gelu,
                bias=svec[:, SV_GB2 + m_:SV_GB2 + m_ + 1]))(m),
              waits=[("t", t_g2)] if m == 0 else (),
              inc=("s", 1) if m == 1 else None, tag=f"gce2_{m}")
        s_g2 = M("s")
        for c in range(2):
            E("tensor", (lambda c_: lambda g: g.matmul(
                out=pb[2][0:1, 0:1], lhsT=gw3b[:, c_, :], rhs=gch2b[:, c_:c_ + 1],
                start=(c_ == 0), stop=(c_ == 1)))(c),
              waits=[("s", s_g2)] if c == 0 else (),
              inc=("t", 1) if c == 1 else None, tag=f"gch3_{c}")
        t_g3 = M("t")
        E("scalar", lambda g: g.activation(out=rtmp[0:1, 7:8], in_=pb[2][0:1, 0:1],
                                           func=AF.Tanh, scale=0.5, bias=rtmp[0:1, 5:6]),
          waits=[("t", t_g3)], inc=("s", 1), tag="tgf")
        s_gf = M("s")
        E("vector", lambda g: g.tensor_mul(out=rtmp[0:1, 8:9], in0=rtmp[0:1, 7:8],
                                           in1=rtmp[0:1, 2:3]),
          waits=[("s", s_gf)], inc=("v", 1), tag="c0a")
        v_c0a = M("v")
        E("vector", lambda g: g.tensor_scalar(out=rtmp[0:1, 8:9], in0=rtmp[0:1, 8:9],
                                              scalar1=0.7, scalar2=0.1,
                                              op0=ALU.mult, op1=ALU.add),
          waits=[("v", v_c0a)], tag="c0b")
        E("vector", lambda g: g.tensor_scalar(out=rtmp[0:1, 9:10], in0=rtmp[0:1, 2:3],
                                              scalar1=0.3, scalar2=None, op0=ALU.mult),
          inc=("v", 1), tag="c1")
        v_c01 = M("v")
        E("tensor", lambda g: g.matmul(out=pb[7][:, 0:1], lhsT=ones_r1f[0:1, :],
                                       rhs=rtmp[0:1, 8:9], start=True, stop=True),
          waits=[("v", v_c01)], tag="c0bc")
        E("tensor", lambda g: g.matmul(out=pb[7][:, 1:2], lhsT=ones_r1f[0:1, :],
                                       rhs=rtmp[0:1, 9:10], start=True, stop=True),
          inc=("t", 1), tag="c1bc")
        t_c01 = M("t")
        E("vector", lambda g: g.tensor_copy(out=c0col[:], in_=pb[7][:, 0:1]),
          waits=[("t", t_c01)], tag="c0cp")
        E("vector", lambda g: g.tensor_copy(out=c1col[:], in_=pb[7][:, 1:2]),
          inc=("v", 1), tag="c1cp")
        v_cc = M("v")

        if KDBG:
            E("gpsimd", lambda g: g.dma_start(out=dbg_sm[:, SV + 32:SV + 40], in_=bsum[:]),
              waits=[("v", v_cc)], inc=("dd", 16), tag="dbsum_gc")
        def pfx_ld(k):
            sl = slice(k * TT, (k + 1) * TT)
            if k < 3:
                jlast = max(j for j in range(NT) if j % 7 == k % 3)
                lw = [("v", sig[("z", jlast)])]
            else:
                lw = [(f"do{(k - 3) % 3}", sig[("st", k - 3)])]
            E("sync", (lambda k_: lambda g: g.dma_start(
                out=xs2[k_ % 3][:], in_=xt_r[k_]))(k),
              waits=lw, inc=(f"dx{k % 3}", 16), tag=f"ld{k}")
            sig[("ld", k)] = M(f"dx{k % 3}")
            bw = [("v", v_cc if k == 0 else sig[("xn", k - 1)])]
            if ("bank", 4) in sig:
                bw.append(("s", max(sig[("bank", 4)], sig[("bank", 5)])))
            E("tensor", (lambda sl_: lambda g: g.matmul(
                out=pb[4][:, 0:TT], lhsT=ones_r1f[0:1, :], rhs=rows[0:1, sl_],
                start=True, stop=True))(sl), waits=bw, tag=f"bcm{k}")
            E("tensor", (lambda k_, sl_: lambda g: g.matmul(
                out=pb[5][:, 0:TT], lhsT=ones_r1f[32:33, :], rhs=rows[32:33, sl_],
                start=True, stop=True))(k, sl),
              waits=[sig[("rrow", k)]], inc=("t", 1), tag=f"bcr{k}")
            sig[("bc2", k)] = M("t")

        def pfx_av(k):
            sl = slice(k * TT, (k + 1) * TT)
            avw = [(f"dx{k % 3}", sig[("ld", k)]), ("t", sig[("bc2", k)])]
            if k >= 2:
                avw.append(("v", sig[("upd", k - 2, 2)]))
            elif k == 1:
                avw.append(("v", sig[("car", 0)]))
            E("vector", (lambda k_: lambda g: g.tensor_sub(
                out=scanb[:], in0=xs2[k_ % 3][:],
                in1=pb[4][:, 0:TT].unsqueeze(1).broadcast_to([P, NCH, TT])))(k),
              waits=avw, inc=("v", 1), tag=f"xn1_{k}")
            vm_x1 = M("v")
            E("vector", (lambda k_: lambda g: g.tensor_mul(
                out=xn[k_ % 2][:], in0=scanb[:],
                in1=pb[5][:, 0:TT].unsqueeze(1).broadcast_to([P, NCH, TT])))(k),
              waits=[("v", vm_x1)], inc=("v", 1), tag=f"xn2_{k}")
            sig[("xn", k)] = M("v")
            hw = [("t", sig[("l1", k - 2, 2, NCH - 1)])] if k >= 2 else []
            hw.append(("v", sig[("xn", k)]))
            if k == 0:
                E("vector", lambda g: g.memset(xnc[0][:, :, 0:2], 0.0),
                  waits=hw, tag="hlc0")
            else:
                E("vector", (lambda k_: lambda g: g.tensor_copy(
                    out=xnc[(k_ - 1) % 2][:, :, TT + 2:TT + 4],
                    in_=xn[k_ % 2][:, :, 0:2]))(k), waits=hw, tag=f"hrc{k - 1}")
                E("vector", (lambda k_: lambda g: g.tensor_copy(
                    out=xnc[k_ % 2][:, :, 0:2],
                    in_=xn[(k_ - 1) % 2][:, :, TT - 2:TT]))(k), tag=f"hlc{k}")
            E("vector", (lambda k_: lambda g: g.tensor_copy(
                out=xnc[k_ % 2][:, :, 2:TT + 2], in_=xn[k_ % 2][:]))(k),
              tag=f"xncc{k}")
            E("vector", (lambda k_: lambda g: g.tensor_copy(
                out=xnb[k_ % 2][:, :, 2:TT + 2], in_=xn[k_ % 2][:]))(k),
              inc=("v", 1), tag=f"xnbc{k}")
            sig[("xnb", k)] = M("v")
            if KDBG:
                E("gpsimd", (lambda k_, sl_: lambda g: g.dma_start(
                    out=dbg_xn_r[:, :, sl_], in_=xn[k_ % 2][:]))(k, sl),
                  waits=[("v", sig[("xn", k)])], inc=("dd", 16), tag=f"dxn{k}")
                E("gpsimd", (lambda k_, sl_: lambda g: g.dma_start(
                    out=dbg_sv_r[:, :, sl_], in_=svb[k_ % 2][:]))(k, sl),
                  waits=[("v", sig[("sv", k)])], inc=("dd", 16), tag=f"dsv{k}")

        def pfx_b(k):
            for c in range(NCH):
                E("vector", (lambda k_, c_: lambda g: g.tensor_tensor_scan(
                    out=scanb[:, c_, :], data0=a_s[:, c_:c_ + 1].broadcast_to([P, TT]),
                    data1=xn[k_ % 2][:, c_, :], initial=carry[:, c_:c_ + 1],
                    op0=ALU.mult, op1=ALU.add))(k, c),
                  inc=("v", 1) if c == NCH - 1 else None, tag=f"scan{k}_{c}")
            sig[("scn", k)] = M("v")
            svw = [("v", sig[("scn", k)])]
            if k == 0:
                svw.append(("t", sig[("sum", 14)]))
            elif k == 1:
                svw.append(("t", sig[("sum", 15)]))
            else:
                svw.append(("t", sig[("attnmm", k - 2, 7)]))
            E("vector", (lambda k_: lambda g: g.tensor_copy(
                out=svb[k_ % 2][:], in_=scanb[:]))(k),
              waits=svw, inc=("v", 1), tag=f"svc{k}")
            sig[("sv", k)] = M("v")
            E("vector", lambda g: g.tensor_copy(out=carry[:], in_=scanb[:, :, TT - 1]),
              inc=("v", 1), tag=f"carry{k}")
            sig[("car", k)] = M("v")

        def attn_emit(k, erange):
            for e in erange:
                bank = 2 + e % 4
                w_ = []
                if e == erange[0]:
                    w_ += [("v", sig[("sv", k)]), ("v", sig[("hal", k)]),
                           ("g", g_diag)]
                    if k == 0:
                        w_.append(("v", v_weights))
                if bank in (4, 5):
                    w_.append(("v", sig[("xn", k + 1)] if ("xn", k + 1) in sig
                               else sig[("xn", k)]))
                w_.append(("s", sig.get(("bank", bank), s_gf)))
                for c in range(NCH):
                    E("tensor", (lambda k_, e_, c_, b_: lambda g: g.matmul(
                        out=pb[b_][:, 0:TT], lhsT=wtsb[:, c_, e_ * P:(e_ + 1) * P],
                        rhs=svb[k_ % 2][:, c_, :], start=(c_ == 0), stop=False))(k, e, c, bank),
                      waits=w_ if c == 0 else (), tag=f"aW{k}_{e}_{c}")
                for kk in range(5):
                    E("tensor", (lambda k_, e_, kk_, b_: lambda g: g.matmul(
                        out=pb[b_][:, 0:TT], lhsT=diag[:, e_ * 5 + kk_, :],
                        rhs=xnc[k_ % 2][:, e_, kk_:kk_ + TT],
                        start=False, stop=(kk_ == 4)))(k, e, kk, bank),
                      inc=("t", 1) if kk == 4 else None, tag=f"aC{k}_{e}_{kk}")
                sig[("attnmm", k, e)] = M("t")
                E("scalar", (lambda k_, e_, b_: lambda g: g.activation(
                    out=shalf[k_ % 2][:, e_, :], in_=pb[b_][:, 0:TT],
                    func=AF.Identity,
                    scale=1.0, bias=bsum[:, e_:e_ + 1]))(k, e, bank),
                  waits=([("t", sig[("attnmm", k, e)])]
                         + ([("v", sig[("fin", k - 2)])] if (e == 0 and k >= 2) else [])),
                  inc=("s", 1), tag=f"evY{k}_{e}")
                sig[("evY", k, e)] = M("s")
                sig[("bank", bank)] = M("s")
            if KDBG and erange[-1] == 7:
                sl_d = slice(k * TT, (k + 1) * TT)
                E("gpsimd", (lambda k_, sl_: lambda g: g.dma_start(
                    out=dbg_s_r[:, :, sl_], in_=shalf[k_ % 2][:]))(k, sl_d),
                  waits=[("s", sig[("evY", k, 7)])], inc=("dd", 16), tag=f"ds{k}")

        def lchain(k, c):
            for m in range(NCH):
                bank = m % 2
                w_ = []
                if m == 0:
                    if c == 0:
                        w_.append(("v", sig[("xnb", k)]))
                        if k == 0:
                            w_.append(("v", v_weights))
                    if c > 0:
                        w_.append(("s", sig[("l3e", k, c - 1)]))
                    elif k > 0:
                        w_.append(("s", sig[("l3e", k - 1, 2)]))
                if m >= 2:
                    w_.append(("s", sig[("l1e", k, c, m - 2)]))
                for cc in range(NCH):
                    ccw = list(w_) if cc == 0 else []
                    if c > 0 and m == 0 and cc in (0, 2, 6):
                        ccw.append(sig[("inbp", k, c, cc)])
                    E("tensor", (lambda k_, m_, cc_, b_: lambda g: g.matmul(
                        out=pb[b_][:, 0:TT], lhsT=w1b[:, cc_, m_ * P:(m_ + 1) * P],
                        rhs=xnb[k_ % 2][:, cc_, 2:TT + 2], start=(cc_ == 0),
                        stop=(cc_ == NCH - 1)))(k, m, cc, bank),
                      waits=ccw,
                      inc=("t", 1) if cc == NCH - 1 else None, tag=f"L1_{k}_{c}_{m}_{cc}")
                sig[("l1", k, c, m)] = M("t")
                E("scalar", (lambda m_, b_: lambda g: g.activation(
                    out=h1b[:, m_, :], in_=pb[b_][:, 0:TT], func=AF.Gelu,
                    bias=svec[:, SV_B1 + m_:SV_B1 + m_ + 1]))(m, bank),
                  waits=[("t", sig[("l1", k, c, m)])],
                  inc=("s", 1), tag=f"l1e{k}_{c}_{m}")
                sig[("l1e", k, c, m)] = M("s")
            for m in range(4):
                bank = m % 2
                w_ = []
                if m >= 2:
                    w_.append(("s", sig[("l2e", k, c, m - 2)]))
                for cc in range(NCH):
                    ccw = list(w_) if cc == 0 else []
                    if m == 0 and cc == 0:
                        ccw.append(("s", sig[("l1e", k, c, NCH - 2)]))
                    if m == 0 and cc == NCH - 1:
                        ccw.append(("s", sig[("l1e", k, c, NCH - 1)]))
                    E("tensor", (lambda m_, cc_, b_: lambda g: g.matmul(
                        out=pb[b_][:, 0:TT], lhsT=w2b[:, cc_, m_ * P:(m_ + 1) * P],
                        rhs=h1b[:, cc_, :], start=(cc_ == 0),
                        stop=(cc_ == NCH - 1)))(m, cc, bank),
                      waits=ccw,
                      inc=("t", 1) if cc == NCH - 1 else None, tag=f"L2_{k}_{c}_{m}")
                sig[("l2", k, c, m)] = M("t")
                E("scalar", (lambda m_, b_: lambda g: g.activation(
                    out=h2b[:, m_, :], in_=pb[b_][:, 0:TT], func=AF.Gelu,
                    bias=svec[:, SV_B2 + m_:SV_B2 + m_ + 1]))(m, bank),
                  waits=[("t", sig[("l2", k, c, m)])],
                  inc=("s", 1), tag=f"l2e{k}_{c}_{m}")
                sig[("l2e", k, c, m)] = M("s")
            for m in range(2):
                bank = m % 2
                for cc in range(4):
                    ccw = []
                    if m == 0 and cc == 0:
                        ccw.append(("s", sig[("l2e", k, c, 2)]))
                    if m == 0 and cc == 3:
                        ccw.append(("s", sig[("l2e", k, c, 3)]))
                    E("tensor", (lambda m_, cc_, b_: lambda g: g.matmul(
                        out=pb[b_][:, 0:TT], lhsT=w3b[:, cc_, m_ * P:(m_ + 1) * P],
                        rhs=h2b[:, cc_, :], start=(cc_ == 0), stop=(cc_ == 3)))(m, cc, bank),
                      waits=ccw,
                      inc=("t", 1) if cc == 3 else None, tag=f"L3_{k}_{c}_{m}")
                sig[("l3", k, c, m)] = M("t")
                E("scalar", (lambda m_, b_: lambda g: g.activation(
                    out=h3b[:, m_, :], in_=pb[b_][:, 0:TT], func=AF.Gelu,
                    bias=svec[:, SV_B3 + m_:SV_B3 + m_ + 1]))(m, bank),
                  waits=[("t", sig[("l3", k, c, m)])],
                  inc=("s", 1), tag=f"l3e{k}_{c}_{m}")
                sig[("l3ep", k, c, m)] = M("s")
            sig[("l3e", k, c)] = M("s")
            if c > 0:
                t4w = [("v", sig[("t4r", k, c - 1)])]
            elif k > 0:
                t4w = [("v", sig[("t4r", k - 1, 2)])]
            else:
                t4w = []
            for cc in range(2):
                ccw = (list(t4w) if cc == 0 else []) + [("s", sig[("l3ep", k, c, cc)])]
                E("tensor", (lambda cc_: lambda g: g.matmul(
                    out=pb[6][0:1, 0:TT], lhsT=w4b[:, cc_, :], rhs=h3b[:, cc_, :],
                    start=(cc_ == 0), stop=(cc_ == 1)))(cc),
                  waits=ccw,
                  inc=("t", 1) if cc == 1 else None, tag=f"L4_{k}_{c}")
            sig[("l4", k, c)] = M("t")
            t4rw = [("t", sig[("l4", k, c)])]
            if ("cbc", k, c - 1) in sig:
                t4rw.append(("t", sig[("cbc", k, c - 1)]))
            elif k > 0 and c == 0:
                t4rw.append(("t", sig[("cbc", k - 1, 2)]))
            E("vector", lambda g: g.tensor_scalar(
                out=t4b[:], in0=pb[6][0:1, 0:TT], scalar1=0.5,
                scalar2=rtmp[0:1, 4:5], op0=ALU.mult, op1=ALU.add),
              waits=t4rw, inc=("v", 1), tag=f"t4r{k}_{c}")
            sig[("t4r", k, c)] = M("v")

        def OB(k, c):
            return of if (k + c) % 2 == 0 else ofB

        def cpl_upd(k, c):
            w_ = [("v", sig[("t4r", k, c)])]
            if c > 0:
                w_.append(("v", sig[("upd", k, c - 1)]))
            elif k > 0:
                w_.append(("v", sig[("upd", k - 1, 2)]))
            else:
                w_.append(("v", v_cc))
            E("tensor", lambda g: g.matmul(
                out=pb[7][:, 0:TT], lhsT=ones_r1b[:], rhs=t4b[:],
                start=True, stop=True), waits=w_, inc=("t", 1), tag=f"cbc{k}_{c}")
            sig[("cbc", k, c)] = M("t")
            E("scalar", lambda g: g.activation(
                out=t4f[:], in_=pb[7][:, 0:TT], func=AF.Tanh),
              waits=[("t", sig[("cbc", k, c)])], inc=("s", 1), tag=f"t4_{k}_{c}")
            sig[("t4", k, c)] = M("s")
            E("vector", lambda g: g.tensor_scalar(
                out=cplb[:], in0=t4f[:], scalar1=c1col[:], scalar2=c0col[:],
                op0=ALU.mult, op1=ALU.add),
              waits=[("s", sig[("t4", k, c)])], inc=("v", 1), tag=f"cpl{k}_{c}")
            sig[("cpl", k, c)] = M("v")
            st = xn[k % 2] if c == 0 else OB(k, c - 1)
            ob = OB(k, c)
            cb2 = cplb[:].unsqueeze(1).broadcast_to([P, 2, TT])
            cb4 = cplb[:].unsqueeze(1).broadcast_to([P, 4, TT])
            E("vector", (lambda st_, ob_: lambda g: g.tensor_tensor(
                out=ob_[:, 0:2, :], in0=cb2, in1=st_[:, 4:6, :],
                op=ALU.mult))(st, ob),
              waits=[("v", sig[("cpl", k, c)])], tag=f"twA{k}_{c}")
            E("vector", (lambda st_, ob_: lambda g: g.tensor_tensor(
                out=ob_[:, 2:4, :], in0=cb2, in1=st_[:, 6:8, :],
                op=ALU.mult))(st, ob), tag=f"twA2{k}_{c}")
            E("vector", (lambda st_, ob_: lambda g: g.tensor_tensor(
                out=ob_[:, 4:8, :], in0=cb4, in1=st_[:, 0:4, :],
                op=ALU.mult))(st, ob), inc=("v", 1), tag=f"twB{k}_{c}")
            vm_tw = M("v")
            E("vector", (lambda st_, ob_: lambda g: g.tensor_add(
                out=ob_[:, 0:2, :], in0=st_[:, 0:2, :], in1=ob_[:, 0:2, :]))(st, ob),
              waits=[("v", vm_tw)], inc=("v", 1), tag=f"u1{k}_{c}")
            sig[("u1", k, c)] = M("v")
            E("vector", (lambda st_, ob_: lambda g: g.tensor_sub(
                out=ob_[:, 2:6, :], in0=st_[:, 2:6, :], in1=ob_[:, 2:6, :]))(st, ob),
              inc=("v", 1), tag=f"u2{k}_{c}")
            sig[("u2", k, c)] = M("v")
            E("vector", (lambda st_, ob_: lambda g: g.tensor_add(
                out=ob_[:, 6:8, :], in0=st_[:, 6:8, :], in1=ob_[:, 6:8, :]))(st, ob),
              inc=("v", 1), tag=f"u3{k}_{c}")
            sig[("upd", k, c)] = M("v")

        def inb_emit(k, c):
            tmark = sig[("l1", k, c - 1, NCH - 1)]
            src = OB(k, c - 1)
            for (lo, hi) in [(0, 2), (2, 6), (6, 8)]:
                E("vector", (lambda k_, lo_, hi_, s_: lambda g: g.tensor_copy(
                    out=xnb[k_ % 2][:, lo_:hi_, 2:TT + 2],
                    in_=s_[:, lo_:hi_, :]))(k, lo, hi, src),
                  waits=([("t", tmark), ("v", sig[("upd", k, c - 1)])]
                         if lo == 0 else ()), inc=("v", 1),
                  tag=f"inb{k}_{c}_{lo}")
                sig[("inbp", k, c, lo)] = ("v", M("v"))
            sig[("inb", k, c)] = M("v")

        def tail(k):
            sl = slice(k * TT, (k + 1) * TT)
            fb = OB(k, 2)
            if KDBG:
                E("gpsimd", (lambda k_, sl_, f_: lambda g: g.dma_start(
                    out=dbg_of_r[:, :, sl_], in_=f_[:]))(k, sl, fb),
                  waits=[("v", sig[("upd", k, 2)])], inc=("dd", 16), tag=f"dof{k}")
            tgw = [("v", sig[("upd", k, 2)])]
            if ("xn", k + 2) in sig:
                tgw.append(("v", sig[("xn", k + 2)]))
            E("scalar", (lambda f_: lambda g: g.activation(
                out=scanb[:], in_=f_[:], func=AF.Tanh, scale=0.5))(fb),
              waits=tgw, inc=("s", 1), tag=f"tg{k}")
            sig[("tg", k)] = M("s")
            E("vector", (lambda f_: lambda g: g.scalar_tensor_tensor(
                out=scanb[:], in0=scanb[:], scalar=1.0, in1=f_[:],
                op0=ALU.add, op1=ALU.mult))(fb),
              waits=[("s", sig[("tg", k)])], inc=("v", 1), tag=f"g2{k}")
            vm_g2 = M("v")
            E("vector", (lambda k_: lambda g: g.tensor_mul(
                out=scanb[:], in0=scanb[:], in1=shalf[k_ % 2][:]))(k),
              waits=[("v", vm_g2)], inc=("v", 1), tag=f"pr{k}")
            vm_pr = M("v")
            E("vector", (lambda k_: lambda g: g.scalar_tensor_tensor(
                out=xs2[k_ % 3][:], in0=scanb[:], scalar=0.5, in1=xs2[k_ % 3][:],
                op0=ALU.mult, op1=ALU.add))(k),
              waits=[("v", vm_pr)], inc=("v", 1), tag=f"fin{k}")
            sig[("fin", k)] = M("v")
            E("gpsimd", (lambda k_: lambda g: g.dma_start(
                out=yt_r[k_], in_=xs2[k_ % 3][:]))(k),
              waits=[("v", sig[("fin", k)])], inc=(f"do{k % 3}", 16), tag=f"st{k}")
            sig[("st", k)] = M(f"do{k % 3}")

        pfx_ld(0)
        pfx_av(0)
        pfx_b(0)
        for k in range(1, NT + 1):
            if k <= NT - 1:
                pfx_ld(k)
                pfx_av(k)
                sig[("hal", k - 1)] = sig[("xnb", k)]
            if k > NT - 1:
                E("vector", lambda g: g.memset(
                    xnc[(NT - 1) % 2][:, :, TT + 2:TT + 4], 0.0),
                  inc=("v", 1), tag="hrlastc")
                sig[("hal", NT - 1)] = M("v")
            kk = k - 1
            lchain(kk, 0)
            cpl_upd(kk, 0)
            inb_emit(kk, 1)
            if kk >= 1:
                tail(kk - 1)
            attn_emit(kk, [0, 1, 2, 3])
            lchain(kk, 1)
            cpl_upd(kk, 1)
            inb_emit(kk, 2)
            if k <= NT - 1:
                pfx_b(k)
            attn_emit(kk, [4, 5, 6, 7])
            lchain(kk, 2)
            cpl_upd(kk, 2)
        tail(NT - 1)

        if KDBG:
            E("gpsimd", lambda g: g.dma_start(
                out=dbg_w.rearrange("p (c m) -> p c m", c=NCH), in_=wtsb[:]),
              waits=[("s", M("s")), ("v", M("v"))], inc=("dd", 16), tag="dw")
            E("gpsimd", lambda g: g.dma_start(
                out=dbg_dg.rearrange("p (c m) -> p c m", c=40), in_=diag[:]),
              inc=("dd", 16), tag="ddg")
            E("gpsimd", lambda g: g.dma_start(out=dbg_sm[:, 0:SV], in_=svec[:]),
              waits=[("s", M("s")), ("v", M("v"))], inc=("dd", 16), tag="dsm1")
            E("gpsimd", lambda g: g.dma_start(out=dbg_sm[:, SV:SV + 8], in_=bsum[:]),
              inc=("dd", 16), tag="dsm2")
            E("gpsimd", lambda g: g.dma_start(out=dbg_sm[:, SV + 8:SV + 16], in_=a_s[:]),
              inc=("dd", 16), tag="dsm3")
        E("gpsimd", lambda g: g.engine_nop(),
          waits=[("do0", M("do0")), ("do1", M("do1")), ("do2", M("do2")),
                 ("dd", M("dd"))],
          tag="final_wait")

        sch.simulate()

        def _runner(name):
            def _body(eng):
                for fn, waits, inc, tag in sch.ops[name]:
                    for s_, v_ in waits:
                        eng.wait_ge(sems[s_], v_)
                    inst = fn(eng)
                    if inc is not None:
                        inst.then_inc(sems[inc[0]], inc[1])
            return _body

        block.sync(_runner("sync"))
        block.gpsimd(_runner("gpsimd"))
        block.vector(_runner("vector"))
        block.scalar(_runner("scalar"))
        block.tensor(_runner("tensor"))

    return nc


def kernel(**inputs):
    x = np.asarray(inputs["L0_attn_x"], dtype=np.float32)
    W = np.asarray(inputs["L0_attn_W"], dtype=np.float32)
    svec = np.zeros((P, SV), dtype=np.float32)

    def col_layout(v, nch):
        return np.ascontiguousarray(np.asarray(v, dtype=np.float32).reshape(nch, P).T)

    svec[:, SV_DL:SV_DL + 8] = col_layout(inputs["L0_attn_decay_logit"], 8)
    svec[:, SV_BATT:SV_BATT + 8] = col_layout(inputs["L0_attn_b"], 8)
    svec[:, SV_CONVB:SV_CONVB + 8] = col_layout(inputs["L0_attn_conv_b"], 8)
    svec[:, SV_B1:SV_B1 + 8] = col_layout(inputs["L0_ffn_cn_b1"], 8)
    svec[:, SV_B2:SV_B2 + 4] = col_layout(inputs["L0_ffn_cn_b2"], 4)
    svec[:, SV_B3:SV_B3 + 2] = col_layout(inputs["L0_ffn_cn_b3"], 2)
    svec[:, SV_GB1:SV_GB1 + 4] = col_layout(inputs["L0_ffn_gc_b1"], 4)
    svec[:, SV_GB2:SV_GB2 + 2] = col_layout(inputs["L0_ffn_gc_b2"], 2)
    cw = np.asarray(inputs["L0_attn_conv_w"], np.float32).reshape(D, 5)
    svec[:, SV_CW:SV_CW + 40] = cw.reshape(NCH, P, 5).transpose(1, 0, 2).reshape(P, 40)
    srow = np.zeros((1, 4), dtype=np.float32)
    srow[0, 0] = np.asarray(inputs["L0_ffn_cn_b4"], np.float32).reshape(())
    srow[0, 1] = np.asarray(inputs["L0_ffn_gc_b3"], np.float32).reshape(())
    srow[0, 2] = np.asarray(inputs["L0_ffn_adaptive_range"], np.float32).reshape(())

    common = {
        "w_ret_t": np.ascontiguousarray(W.T),
        "cn_w1": np.ascontiguousarray(np.asarray(inputs["L0_ffn_cn_w1"], np.float32)),
        "cn_w2": np.ascontiguousarray(np.asarray(inputs["L0_ffn_cn_w2"], np.float32)),
        "cn_w3": np.ascontiguousarray(np.asarray(inputs["L0_ffn_cn_w3"], np.float32)),
        "cn_w4": np.ascontiguousarray(np.asarray(inputs["L0_ffn_cn_w4"], np.float32).reshape(2, P).T),
        "gc_w1": np.ascontiguousarray(np.asarray(inputs["L0_ffn_gc_w1"], np.float32)),
        "gc_w2": np.ascontiguousarray(np.asarray(inputs["L0_ffn_gc_w2"], np.float32)),
        "gc_w3": np.ascontiguousarray(np.asarray(inputs["L0_ffn_gc_w3"], np.float32).reshape(2, P).T),
        "svec": svec,
        "srow": srow,
    }
    in_maps = []
    for b in range(N_CORES):
        m = dict(common)
        m["x_t"] = np.ascontiguousarray(
            x[b].T.reshape(NCH, P, NT, TT).transpose(2, 1, 0, 3).reshape(NT, P, NCH * TT))
        in_maps.append(m)

    nc = build_nc()
    res = run_bass_kernel_spmd(nc, in_maps, core_ids=list(range(N_CORES)))
    outs = []
    for r in res.results:
        yt = np.asarray(r["y_t"]).reshape(NT, P, NCH, TT).transpose(2, 1, 0, 3)
        outs.append(yt.reshape(D, T).T)
    out = np.stack(outs, axis=0)
    return np.ascontiguousarray(out).astype(np.float32)


# version 30
# speedup vs baseline: 1.0136x; 1.0136x over previous
import numpy as np
from contextlib import ExitStack

import concourse.bass as bass
import concourse.mybir as mybir
from concourse.bass_utils import run_bass_kernel_spmd

F32 = mybir.dt.float32
BF16 = mybir.dt.bfloat16
AF = mybir.ActivationFunctionType
ALU = mybir.AluOpType

P = 128
D = 1024
NCH = D // P
T = 4096
TT = 256
NT = T // TT
EPS = 1e-5
N_CORES = 8

SV_DL = 0
SV_BATT = 8
SV_CONVB = 16
SV_B1 = 24
SV_B2 = 32
SV_B3 = 36
SV_GB1 = 38
SV_GB2 = 42
SV_CW = 44
SV = 84


class Sched:
    ENGINES = ("sync", "gpsimd", "vector", "scalar", "tensor")

    def __init__(self):
        self.ops = {e: [] for e in self.ENGINES}
        self.cnt = {s: 0 for s in ("di", "dw0", "dw1", "dx0", "dx1", "dx2", "dx3", "dx4", "dx5", "dx6",
                           "dr", "dr2", "do0", "do1", "do2", "dd", "t", "v", "s", "g")}

    def emit(self, eng, fn, waits=(), inc=None, tag=""):
        waits = [(s, v) for (s, v) in waits if v > 0]
        self.ops[eng].append((fn, waits, inc, tag))
        if inc is not None:
            self.cnt[inc[0]] += inc[1]

    def mark(self, sem):
        return self.cnt[sem]

    def simulate(self):
        pos = {e: 0 for e in self.ENGINES}
        val = {s: 0 for s in self.cnt}
        progress = True
        while progress:
            progress = False
            for e in self.ENGINES:
                while pos[e] < len(self.ops[e]):
                    fn, waits, inc, tag = self.ops[e][pos[e]]
                    if all(val[s] >= v for (s, v) in waits):
                        if inc:
                            val[inc[0]] += inc[1]
                        pos[e] += 1
                        progress = True
                    else:
                        break
        stuck = {e: pos[e] for e in self.ENGINES if pos[e] < len(self.ops[e])}
        if stuck:
            lines = []
            for e, p_ in stuck.items():
                fn, waits, inc, tag = self.ops[e][p_]
                lines.append(f"  {e}[{p_}] tag={tag} waits={waits} "
                             f"(now={[(s, val[s]) for s, _ in waits]})")
            raise RuntimeError("DEADLOCK in schedule:\n" + "\n".join(lines))


def build_nc():
    nc = bass.Bass("TRN2", target_bir_lowering=False)

    xt = nc.dram_tensor("x_t", [NT, P, NCH * TT], F32, kind="ExternalInput").ap()
    wts_h = nc.dram_tensor("w_ret_t", [D, D], F32, kind="ExternalInput").ap()
    w1_h = nc.dram_tensor("cn_w1", [D, D], F32, kind="ExternalInput").ap()
    w2_h = nc.dram_tensor("cn_w2", [D, 512], F32, kind="ExternalInput").ap()
    w3_h = nc.dram_tensor("cn_w3", [512, 256], F32, kind="ExternalInput").ap()
    w4_h = nc.dram_tensor("cn_w4", [P, 2], F32, kind="ExternalInput").ap()
    g1_h = nc.dram_tensor("gc_w1", [D, 512], F32, kind="ExternalInput").ap()
    g2_h = nc.dram_tensor("gc_w2", [512, 256], F32, kind="ExternalInput").ap()
    g3_h = nc.dram_tensor("gc_w3", [P, 2], F32, kind="ExternalInput").ap()
    sv_h = nc.dram_tensor("svec", [P, SV], F32, kind="ExternalInput").ap()
    sr_h = nc.dram_tensor("srow", [1, 4], F32, kind="ExternalInput").ap()
    yt = nc.dram_tensor("y_t", [NT, P, NCH * TT], F32, kind="ExternalOutput").ap()

    import os
    KDBG = bool(int(os.environ.get("KDBG", "0")))
    if KDBG:
        dbg_xn = nc.dram_tensor("dbg_xn", [D, T], F32, kind="ExternalOutput").ap()
        dbg_sv = nc.dram_tensor("dbg_sv", [D, T], BF16, kind="ExternalOutput").ap()
        dbg_s = nc.dram_tensor("dbg_s", [D, T], F32, kind="ExternalOutput").ap()
        dbg_of = nc.dram_tensor("dbg_of", [D, T], F32, kind="ExternalOutput").ap()
        dbg_xn_r = dbg_xn.rearrange("(c p) t -> p c t", p=P)
        dbg_sv_r = dbg_sv.rearrange("(c p) t -> p c t", p=P)
        dbg_s_r = dbg_s.rearrange("(c p) t -> p c t", p=P)
        dbg_of_r = dbg_of.rearrange("(c p) t -> p c t", p=P)
        dbg_sm = nc.dram_tensor("dbg_sm", [P, SV + 48], F32, kind="ExternalOutput").ap()
        dbg_w = nc.dram_tensor("dbg_w", [P, NCH * D], BF16, kind="ExternalOutput").ap()
        dbg_w2 = nc.dram_tensor("dbg_w2", [P, NCH * D], BF16, kind="ExternalOutput").ap()
        dbg_dg = nc.dram_tensor("dbg_dg", [P, 40 * P], BF16, kind="ExternalOutput").ap()

    xt_r = xt.rearrange("k p (c t) -> k p c t", c=NCH)
    yt_r = yt.rearrange("k p (c t) -> k p c t", c=NCH)

    sch = Sched()
    E = sch.emit
    M = sch.mark
    sig = {}

    with ExitStack() as ctx:
        e_ = ctx.enter_context
        wtsb = e_(nc.sbuf_tensor("wtsb", [P, NCH, D], BF16))
        w1b = e_(nc.sbuf_tensor("w1b", [P, NCH, D], BF16))
        w2b = e_(nc.sbuf_tensor("w2b", [P, NCH, 512], BF16))
        w3b = e_(nc.sbuf_tensor("w3b", [P, 4, 256], BF16))
        w4b = e_(nc.sbuf_tensor("w4b", [P, 2, 1], BF16))
        gw1b = e_(nc.sbuf_tensor("gw1b", [P, NCH, 512], BF16))
        gw2b = e_(nc.sbuf_tensor("gw2b", [P, 4, 256], BF16))
        gw3b = e_(nc.sbuf_tensor("gw3b", [P, 2, 1], BF16))
        diag = e_(nc.sbuf_tensor("diag", [P, 40, P], BF16))
        ones_sq_b = e_(nc.sbuf_tensor("ones_sq_b", [P, P], BF16))
        ones_r1b = e_(nc.sbuf_tensor("ones_r1b", [1, P], BF16))
        ones_r1f = e_(nc.sbuf_tensor("ones_r1f", [33, P], F32))
        svec = e_(nc.sbuf_tensor("svec_s", [P, SV], F32))
        a_t = e_(nc.sbuf_tensor("a_t", [P, NCH], F32))
        a_s = e_(nc.sbuf_tensor("a_s", [P, NCH], F32))
        oma = e_(nc.sbuf_tensor("oma", [P, NCH], F32))
        bsum = e_(nc.sbuf_tensor("bsum", [P, NCH], F32))
        c0col = e_(nc.sbuf_tensor("c0col", [P, 1], F32))
        c1col = e_(nc.sbuf_tensor("c1col", [P, 1], F32))
        carry = e_(nc.sbuf_tensor("carry", [P, NCH], F32))
        zacc = e_(nc.sbuf_tensor("zacc", [P, NCH * NT], F32))
        gcv = e_(nc.sbuf_tensor("gcv", [P, NCH], F32))
        gcvb = e_(nc.sbuf_tensor("gcvb", [P, NCH], BF16))
        gch1b = e_(nc.sbuf_tensor("gch1b", [P, 4], BF16))
        gch2b = e_(nc.sbuf_tensor("gch2b", [P, 2], BF16))
        csum = e_(nc.sbuf_tensor("csum", [1, NT], F32))
        rows = e_(nc.sbuf_tensor("rows", [33, T], F32))
        rs1 = e_(nc.sbuf_tensor("rs1", [1, 4 * TT], F32))
        rs2 = e_(nc.sbuf_tensor("rs2", [1, 4 * TT], F32))
        rtmp = e_(nc.sbuf_tensor("rtmp", [1, 16], F32))
        t4b = e_(nc.sbuf_tensor("t4b", [1, TT], BF16))
        xs2 = [e_(nc.sbuf_tensor(f"xs2_{i}", [P, NCH, TT], F32)) for i in range(3)]
        xn = [e_(nc.sbuf_tensor(f"xn_{i}", [P, NCH, TT], F32)) for i in range(2)]
        xnb = [e_(nc.sbuf_tensor(f"xnb_{i}", [P, NCH, TT + 4], BF16)) for i in range(2)]
        xnc = [e_(nc.sbuf_tensor(f"xnc_{i}", [P, NCH, TT + 4], BF16)) for i in range(2)]
        scanb = e_(nc.sbuf_tensor("scanb", [P, NCH, TT], F32))
        svb = [e_(nc.sbuf_tensor(f"svb_{i}", [P, NCH, TT], BF16)) for i in range(2)]
        shalf = [e_(nc.sbuf_tensor(f"shalf_{i}", [P, NCH, TT], F32)) for i in range(2)]
        of = e_(nc.sbuf_tensor("of", [P, NCH, TT], F32))
        ofB = e_(nc.sbuf_tensor("ofB", [P, NCH, TT], F32))
        h1b = e_(nc.sbuf_tensor("h1b", [P, NCH, TT], BF16))
        h2b = e_(nc.sbuf_tensor("h2b", [P, 4, TT], BF16))
        h3b = e_(nc.sbuf_tensor("h3b", [P, 2, TT], BF16))
        cplb = e_(nc.sbuf_tensor("cplb", [P, TT], F32))
        pb = [e_(nc.psum_tensor(f"pb{i}", [P, 512], F32)) for i in range(8)]
        sems = {n: e_(nc.semaphore(n)) for n in
                ("di", "dw0", "dw1", "dx0", "dx1", "dx2", "dx3", "dx4", "dx5", "dx6",
                 "dr", "dr2", "do0", "do1", "do2", "dd", "t", "v", "s", "g")}
        block = e_(nc.Block())

        E("sync", lambda g: g.dma_start(out=svec[:], in_=sv_h), inc=("di", 16), tag="ld_svec")
        E("sync", lambda g: g.dma_start(out=rtmp[0:1, 0:4], in_=sr_h), inc=("di", 16), tag="ld_srow")
        E("sync", lambda g: g.dma_start(out=cplb[:, 0:2], in_=w4_h),
          inc=("di", 16), tag="ld_w4")
        E("sync", lambda g: g.dma_start(out=cplb[:, 2:4], in_=g3_h),
          inc=("di", 16), tag="ld_g3")
        di_small = M("di")

        E("scalar", lambda g: g.activation(out=a_t[:], in_=svec[:, SV_DL:SV_DL + 8],
                                           func=AF.Tanh, scale=0.5),
          waits=[("di", di_small)], inc=("s", 1), tag="a_tanh")
        s_at = M("s")
        E("vector", lambda g: g.tensor_scalar(out=a_s[:], in0=a_t[:], scalar1=0.5,
                                              scalar2=0.5, op0=ALU.mult, op1=ALU.add),
          waits=[("s", s_at)], tag="a_s")
        E("vector", lambda g: g.tensor_scalar(out=oma[:], in0=a_t[:], scalar1=-0.5,
                                              scalar2=0.5, op0=ALU.mult, op1=ALU.add),
          tag="oma")
        E("vector", lambda g: g.scalar_tensor_tensor(
            out=bsum[:], in0=svec[:, SV_BATT:SV_BATT + 8], scalar=1.0,
            in1=svec[:, SV_CONVB:SV_CONVB + 8], op0=ALU.mult, op1=ALU.add),
          waits=[("di", di_small)], tag="bsum0")
        E("vector", lambda g: g.memset(ones_sq_b[:], 1.0 / D), tag="ones1")
        E("vector", lambda g: g.memset(ones_r1b[:], 1.0), tag="ones2")
        E("vector", lambda g: g.memset(ones_r1f[:], 1.0), tag="ones3")
        E("vector", lambda g: g.memset(carry[:], 0.0), tag="carry0")
        E("vector", lambda g: g.tensor_scalar(out=rtmp[0:1, 4:5], in0=rtmp[0:1, 0:1],
                                              scalar1=0.5, scalar2=None, op0=ALU.mult),
          waits=[("di", di_small)], tag="b4h")
        E("vector", lambda g: g.tensor_scalar(out=rtmp[0:1, 5:6], in0=rtmp[0:1, 1:2],
                                              scalar1=0.5, scalar2=None, op0=ALU.mult),
          inc=("v", 1), tag="gb3h")
        v_prep = M("v")
        if KDBG:
            E("gpsimd", lambda g: g.dma_start(out=dbg_sm[:, SV + 16:SV + 24], in_=bsum[:]),
              waits=[("v", v_prep)], inc=("dd", 16), tag="dbsum_prep")

        for c in range(NCH):
            for k in range(5):
                idx = c * 5 + k
                col = SV_CW + idx
                E("gpsimd",
                  (lambda col_, idx_: lambda g: g.affine_select(
                      out=diag[:, idx_, :],
                      in_=svec[:, col_:col_ + 1].broadcast_to([P, P]),
                      pattern=[[1, P]], compare_op=ALU.is_equal, fill=0.0,
                      base=0, channel_multiplier=-1))(col, idx),
                  waits=[("di", di_small)] if idx == 0 else (),
                  inc=("g", 1) if idx == 39 else None, tag=f"diag{idx}")
        g_diag = M("g")

        E("vector", lambda g: g.tensor_copy(out=w4b[:, :, 0], in_=cplb[:, 0:2]),
          waits=[("di", di_small)], tag="cast_w4")
        E("vector", lambda g: g.tensor_copy(out=gw3b[:, :, 0], in_=cplb[:, 2:4]),
          inc=("v", 1), tag="cast_g3")
        tiny_cast = M("v")

        for j0 in range(5):
            E("sync", (lambda j_: lambda g: g.dma_start(
                out=(xs2 + shalf + xn)[j_][:], in_=xt_r[j_]))(j0),
              inc=(f"dx{j0 % 7}", 16), tag=f"p1ld{j0}")
            sig[("ld1", j0)] = M(f"dx{j0 % 7}")

        wstage = [of, scanb]
        wjobs = [
            (wts_h.rearrange("(c p) m -> p c m", p=P), wtsb, 8, D, True),
            (w1_h.rearrange("(c p) m -> p c m", p=P), w1b, 8, D, False),
            (w2_h.rearrange("(c p) m -> p c m", p=P), w2b, 8, 512, False),
            (w3_h.rearrange("(c p) m -> p c m", p=P), w3b, 4, 256, False),
            (g1_h.rearrange("(c p) m -> p c m", p=P), gw1b, 8, 512, False),
            (g2_h.rearrange("(c p) m -> p c m", p=P), gw2b, 4, 256, False),
        ]
        segs = []
        for (srcw, dst, nch_w, cols, scaled) in wjobs:
            step = max(1, 2048 // cols)
            for c0 in range(0, nch_w, step):
                segs.append((srcw, dst, c0, min(step, nch_w - c0), cols, scaled))
        wl = 0
        last_cast = [tiny_cast] * len(wstage)
        for (srcw, dst, c0, nch_s, cols, scaled) in segs:
            h = wl % len(wstage)
            stg = wstage[h][:].rearrange("p a t -> p (a t)")
            E("sync",
              (lambda srcw_, c0_, n_, cols_, stg_:
               lambda g: g.dma_start(
                   out=stg_[:, 0:n_ * cols_].rearrange("p (c m) -> p c m", c=n_),
                   in_=srcw_[:, c0_:c0_ + n_, :]))(srcw, c0, nch_s, cols, stg),
              waits=[("v", last_cast[h])], inc=(f"dw{wl % 2}", 16), tag=f"wld{wl}")
            di_w = M(f"dw{wl % 2}")
            if scaled:
                for ci in range(nch_s):
                    E("vector", (lambda dst_, c0_, ci_, cols_, stg_:
                                 lambda g: g.tensor_scalar(
                                     out=dst_[:, c0_ + ci_, :],
                                     in0=stg_[:, ci_ * cols_:(ci_ + 1) * cols_],
                                     scalar1=oma[:, c0_ + ci_:c0_ + ci_ + 1],
                                     scalar2=None, op0=ALU.mult))(dst, c0, ci, cols, stg),
                      waits=[(f"dw{wl % 2}", di_w), ("v", v_prep)] if ci == 0 else [],
                      inc=("v", 1) if ci == nch_s - 1 else None, tag=f"wcast{wl}_{ci}")
            else:
                E("vector", (lambda dst_, c0_, n_, cols_, stg_:
                             lambda g: g.tensor_copy(
                                 out=dst_[:, c0_:c0_ + n_, :],
                                 in_=stg_[:, 0:n_ * cols_].rearrange(
                                     "p (c m) -> p c m", c=n_)))(dst, c0, nch_s, cols, stg),
                  waits=[(f"dw{wl % 2}", di_w)], inc=("v", 1), tag=f"wcast{wl}")
            last_cast[h] = M("v")
            wl += 1
        v_weights = M("v")
        if KDBG:
            E("gpsimd", lambda g: g.dma_start(
                out=dbg_w2.rearrange("p (c m) -> p c m", c=NCH), in_=wtsb[:]),
              waits=[("v", v_weights)], inc=("dd", 16), tag="dw2")

        p1x = xs2 + shalf + xn
        NX1 = len(p1x)
        for j in range(NT + 3):
            if j < NT:
                if j >= 5:
                    lw = [("v", sig[("z", j - NX1)])] if j >= NX1 else []
                    E("sync", (lambda j_: lambda g: g.dma_start(
                        out=p1x[j_ % NX1][:], in_=xt_r[j_]))(j),
                      waits=lw, inc=(f"dx{j % 7}", 16), tag=f"p1ld{j}")
                    sig[("ld1", j)] = M(f"dx{j % 7}")
                cw = [(f"dx{j % 7}", sig[("ld1", j)])]
                if j >= 2:
                    cw.append(("t", sig[("sum", j - 2)]))
                E("vector", (lambda j_: lambda g: g.tensor_copy(
                    out=svb[j_ % 2][:], in_=p1x[j_ % NX1][:]))(j),
                  waits=cw, inc=("v", 1), tag=f"p1cast{j}")
                sig[("xb", j)] = M("v")
                sqw = [("v", sig[("xb", j)])]
                if j >= 2:
                    sqw.append(("t", sig[("sum", j - 2)]))
                sqb = h1b[:] if j % 2 == 0 else xnb[0][:, :, 2:TT + 2]
                E("vector", (lambda j_, sqb_: lambda g: g.tensor_mul(
                    out=sqb_, in0=svb[j_ % 2][:], in1=svb[j_ % 2][:]))(j, sqb),
                  waits=sqw, inc=("v", 1), tag=f"p1sq{j}")
                sig[("sq", j)] = M("v")
                sig[("sqb", j)] = sqb
            jj = j - 1
            if 0 <= jj < NT:
                sl = slice(jj * TT, (jj + 1) * TT)
                pS = 0 if jj % 2 == 0 else 2
                mmw = [("v", sig[("sq", jj)])]
                if jj == 0:
                    mmw.append(("v", v_prep))
                if jj >= 2:
                    mmw.append(("s", sig[("mq", jj - 2)]))
                for c in range(NCH):
                    E("tensor", (lambda j_, c_, pS_: lambda g: g.matmul(
                        out=pb[pS_][:, 0:TT], lhsT=ones_sq_b[:],
                        rhs=svb[j_ % 2][:, c_, :],
                        start=(c_ == 0), stop=(c_ == NCH - 1)))(jj, c, pS),
                      waits=mmw if c == 0 else (), tag=f"p1mmx{jj}_{c}")
                sqb = sig[("sqb", jj)]
                for c in range(NCH):
                    E("tensor", (lambda c_, pS_, sqb_: lambda g: g.matmul(
                        out=pb[pS_ + 1][:, 0:TT], lhsT=ones_sq_b[:],
                        rhs=sqb_[:, c_, :],
                        start=(c_ == 0), stop=(c_ == NCH - 1)))(c, pS, sqb),
                      inc=("t", 1) if c == NCH - 1 else None, tag=f"p1mmq{jj}_{c}")
                sig[("sum", jj)] = M("t")
                E("scalar", (lambda sl_, pS_: lambda g: g.activation(
                    out=rows[0:1, sl_], in_=pb[pS_][0:1, 0:TT], func=AF.Copy))(sl, pS),
                  waits=[("t", sig[("sum", jj)])], tag=f"p1m{jj}")
                ro = (jj % 4) * TT
                qw = [("v", sig[("z", jj - 4)])] if jj >= 4 else []
                E("scalar", (lambda pS_, ro_: lambda g: g.activation(
                    out=rs1[0:1, ro_:ro_ + TT], in_=pb[pS_ + 1][0:1, 0:TT],
                    func=AF.Copy))(pS, ro),
                  waits=qw, inc=("s", 1), tag=f"p1q{jj}")
                sig[("mq", jj)] = M("s")
            jp = j - 2
            if jp >= 1 and jp % 2 == 1 and jp < NT:
                pair = jp // 2
                TT2 = 2 * TT
                pbase = (pair % 2) * TT2
                psl = slice((jp - 1) * TT, (jp + 1) * TT)
                pR = 4 + pair % 2
                mmw2 = [("s", sig[("mq", jp)])]
                if pair >= 2:
                    mmw2.append(sig[("rrowp", pair - 2)])
                    mmw2.append(("t", sig[("bcp", pair - 2)]))
                E("scalar", (lambda psl_, pb_: lambda g: g.activation(
                    out=rs2[0:1, pb_:pb_ + TT2], in_=rows[0:1, psl_],
                    func=AF.Square))(psl, pbase),
                  waits=mmw2, inc=("s", 1), tag=f"p1mm{pair}")
                sm_sq = M("s")
                E("vector", (lambda pb_: lambda g: g.scalar_tensor_tensor(
                    out=rs1[0:1, pb_:pb_ + TT2], in0=rs1[0:1, pb_:pb_ + TT2],
                    scalar=float(EPS), in1=rs2[0:1, pb_:pb_ + TT2],
                    op0=ALU.add, op1=ALU.subtract))(pbase),
                  waits=[("s", sm_sq)], inc=("v", 1), tag=f"p1var{pair}")
                sig[("var", pair)] = M("v")
                E("scalar", (lambda pb_: lambda g: g.activation(
                    out=rs1[0:1, pb_:pb_ + TT2], in_=rs1[0:1, pb_:pb_ + TT2],
                    func=AF.Ln))(pbase),
                  waits=[("v", sig[("var", pair)])], inc=("s", 1), tag=f"p1ln{pair}")
                sm_ln = M("s")
                E("scalar", (lambda pb_: lambda g: g.activation(
                    out=rs2[0:1, pb_:pb_ + TT2], in_=rs1[0:1, pb_:pb_ + TT2],
                    func=AF.Exp, scale=-0.5))(pbase),
                  waits=[("s", sm_ln)], inc=("s", 1), tag=f"p1exp{pair}")
                sig[("rstdp", pair)] = M("s")
                sig[("rstd", jp - 1)] = sig[("rstd", jp)] = sig[("rstdp", pair)]
                rsem = "dr" if pair % 2 == 0 else "dr2"
                E("gpsimd", (lambda psl_, pb_: lambda g: g.dma_start(
                    out=rows[32:33, psl_], in_=rs2[0:1, pb_:pb_ + TT2]))(psl, pbase),
                  waits=[("s", sig[("rstdp", pair)])], inc=(rsem, 16),
                  tag=f"rrow{pair}")
                sig[("rrowp", pair)] = (rsem, M(rsem))
                sig[("rrow", jp - 1)] = sig[("rrow", jp)] = sig[("rrowp", pair)]
                bw = [("s", sig[("rstdp", pair)])]
                if pair >= 1:
                    bw.append(("v", sig[("z", jp - 2)]))
                E("tensor", (lambda pR_, pb_: lambda g: g.matmul(
                    out=pb[pR_][:, 0:TT2], lhsT=ones_r1f[0:1, :],
                    rhs=rs2[0:1, pb_:pb_ + TT2],
                    start=True, stop=True))(pR, pbase),
                  waits=bw, inc=("t", 1), tag=f"p1bc{pair}")
                sig[("bcp", pair)] = M("t")
                for jz, po in ((jp - 1, 0), (jp, TT)):
                    for c in range(NCH):
                        zw = []
                        if po == 0 and c == 0:
                            zw.append(("t", sig[("bcp", pair)]))
                        if po != 0 and c == 0:
                            zw.append(("v", M("v") if False else sig[("zhalf", pair)]))
                        E("vector", (lambda jz_, c_, pR_, po_: lambda g: g.scalar_tensor_tensor(
                            out=xnc[0][:, c_, 0:TT], in0=p1x[jz_ % NX1][:, c_, :], scalar=1.0,
                            in1=pb[pR_][:, po_:po_ + TT], op0=ALU.mult, op1=ALU.mult,
                            accum_out=zacc[:, c_ * NT + jz_:c_ * NT + jz_ + 1]))(jz, c, pR, po),
                          waits=zw,
                          inc=("v", 1) if (po == 0 and c == NCH - 1) else None,
                          tag=f"p1z{jz}_{c}")
                        if po == 0 and c == NCH - 1:
                            sig[("zhalf", pair)] = M("v")
                E("vector", (lambda psl_, pb_: lambda g: g.tensor_mul(
                    out=rs1[0:1, pb_:pb_ + TT2], in0=rows[0:1, psl_],
                    in1=rs2[0:1, pb_:pb_ + TT2]))(psl, pbase),
                  waits=[("s", sig[("rstdp", pair)])], inc=("v", 1), tag=f"p1cr{pair}")
                vm_cr = M("v")
                E("vector", (lambda pair_, pb_: lambda g: g.reduce_sum(
                    out=csum[0:1, pair_:pair_ + 1], in_=rs1[0:1, pb_:pb_ + TT2],
                    axis=mybir.AxisListType.X))(pair, pbase),
                  waits=[("v", vm_cr)], inc=("v", 1), tag=f"p1cs{pair}")
                sig[("z", jp - 1)] = sig[("z", jp)] = M("v")

        if KDBG:
            E("gpsimd", lambda g: g.dma_start(out=dbg_sm[:, SV + 24:SV + 32], in_=bsum[:]),
              waits=[("v", sig[("z", NT - 1)])], inc=("dd", 16), tag="dbsum_p1")
        E("vector", lambda g: g.reduce_sum(
            out=gcv[:], in_=zacc[:].rearrange("p (c j) -> p c j", j=NT),
            axis=mybir.AxisListType.X),
          waits=[("v", sig[("z", NT - 1)])], inc=("v", 1), tag="zred")
        v_zred = M("v")
        E("vector", lambda g: g.reduce_sum(out=rtmp[0:1, 6:7],
                                           in_=csum[0:1, 0:NT // 2],
                                           axis=mybir.AxisListType.X),
          inc=("v", 1), tag="csred")
        v_gc0 = M("v")
        E("tensor", lambda g: g.matmul(out=pb[7][:, 0:1], lhsT=ones_r1f[0:1, :],
                                       rhs=rtmp[0:1, 6:7], start=True, stop=True),
          waits=[("v", v_gc0)], inc=("t", 1), tag="csbc")
        t_csbc = M("t")
        E("vector", lambda g: g.tensor_scalar(out=gcv[:], in0=gcv[:],
                                              scalar1=pb[7][:, 0:1], scalar2=float(1.0 / T),
                                              op0=ALU.subtract, op1=ALU.mult),
          waits=[("t", t_csbc), ("v", v_zred)], inc=("v", 1), tag="gcv")
        v_gcvop = M("v")
        E("vector", lambda g: g.tensor_copy(out=gcvb[:], in_=gcv[:]),
          waits=[("v", v_gcvop)], inc=("v", 1), tag="gcvb")
        v_gcv = M("v")
        for m in range(4):
            for c in range(NCH):
                E("tensor", (lambda m_, c_: lambda g: g.matmul(
                    out=pb[0][:, m_:m_ + 1], lhsT=gw1b[:, c_, m_ * P:(m_ + 1) * P],
                    rhs=gcvb[:, c_:c_ + 1], start=(c_ == 0), stop=(c_ == NCH - 1)))(m, c),
                  waits=[("v", v_gcv)] if (m == 0 and c == 0) else (),
                  inc=("t", 1) if (m == 3 and c == NCH - 1) else None, tag=f"gch1_{m}_{c}")
        t_g1 = M("t")
        for m in range(4):
            E("scalar", (lambda m_: lambda g: g.activation(
                out=gch1b[:, m_:m_ + 1], in_=pb[0][:, m_:m_ + 1], func=AF.Gelu,
                bias=svec[:, SV_GB1 + m_:SV_GB1 + m_ + 1]))(m),
              waits=[("t", t_g1)] if m == 0 else (),
              inc=("s", 1) if m == 3 else None, tag=f"gce1_{m}")
        s_g1 = M("s")
        for m in range(2):
            for c in range(4):
                E("tensor", (lambda m_, c_: lambda g: g.matmul(
                    out=pb[1][:, m_:m_ + 1], lhsT=gw2b[:, c_, m_ * P:(m_ + 1) * P],
                    rhs=gch1b[:, c_:c_ + 1], start=(c_ == 0), stop=(c_ == 3)))(m, c),
                  waits=[("s", s_g1)] if (m == 0 and c == 0) else (),
                  inc=("t", 1) if (m == 1 and c == 3) else None, tag=f"gch2_{m}_{c}")
        t_g2 = M("t")
        for m in range(2):
            E("scalar", (lambda m_: lambda g: g.activation(
                out=gch2b[:, m_:m_ + 1], in_=pb[1][:, m_:m_ + 1], func=AF.Gelu,
                bias=svec[:, SV_GB2 + m_:SV_GB2 + m_ + 1]))(m),
              waits=[("t", t_g2)] if m == 0 else (),
              inc=("s", 1) if m == 1 else None, tag=f"gce2_{m}")
        s_g2 = M("s")
        for c in range(2):
            E("tensor", (lambda c_: lambda g: g.matmul(
                out=pb[2][0:1, 0:1], lhsT=gw3b[:, c_, :], rhs=gch2b[:, c_:c_ + 1],
                start=(c_ == 0), stop=(c_ == 1)))(c),
              waits=[("s", s_g2)] if c == 0 else (),
              inc=("t", 1) if c == 1 else None, tag=f"gch3_{c}")
        t_g3 = M("t")
        E("scalar", lambda g: g.activation(out=rtmp[0:1, 7:8], in_=pb[2][0:1, 0:1],
                                           func=AF.Tanh, scale=0.5, bias=rtmp[0:1, 5:6]),
          waits=[("t", t_g3)], inc=("s", 1), tag="tgf")
        s_gf = M("s")
        E("vector", lambda g: g.tensor_mul(out=rtmp[0:1, 8:9], in0=rtmp[0:1, 7:8],
                                           in1=rtmp[0:1, 2:3]),
          waits=[("s", s_gf)], inc=("v", 1), tag="c0a")
        v_c0a = M("v")
        E("vector", lambda g: g.tensor_scalar(out=rtmp[0:1, 8:9], in0=rtmp[0:1, 8:9],
                                              scalar1=0.7, scalar2=0.1,
                                              op0=ALU.mult, op1=ALU.add),
          waits=[("v", v_c0a)], tag="c0b")
        E("vector", lambda g: g.tensor_scalar(out=rtmp[0:1, 9:10], in0=rtmp[0:1, 2:3],
                                              scalar1=0.3, scalar2=None, op0=ALU.mult),
          inc=("v", 1), tag="c1")
        v_c01 = M("v")
        E("tensor", lambda g: g.matmul(out=pb[7][:, 0:1], lhsT=ones_r1f[0:1, :],
                                       rhs=rtmp[0:1, 8:9], start=True, stop=True),
          waits=[("v", v_c01)], tag="c0bc")
        E("tensor", lambda g: g.matmul(out=pb[7][:, 1:2], lhsT=ones_r1f[0:1, :],
                                       rhs=rtmp[0:1, 9:10], start=True, stop=True),
          inc=("t", 1), tag="c1bc")
        t_c01 = M("t")
        E("vector", lambda g: g.tensor_copy(out=c0col[:], in_=pb[7][:, 0:1]),
          waits=[("t", t_c01)], tag="c0cp")
        E("vector", lambda g: g.tensor_copy(out=c1col[:], in_=pb[7][:, 1:2]),
          inc=("v", 1), tag="c1cp")
        v_cc = M("v")

        if KDBG:
            E("gpsimd", lambda g: g.dma_start(out=dbg_sm[:, SV + 32:SV + 40], in_=bsum[:]),
              waits=[("v", v_cc)], inc=("dd", 16), tag="dbsum_gc")
        def pfx_ld(k):
            sl = slice(k * TT, (k + 1) * TT)
            if k < 3:
                jlast = max(j for j in range(NT) if j % 7 == k % 3)
                lw = [("v", sig[("z", jlast)])]
            else:
                lw = [(f"do{(k - 3) % 3}", sig[("st", k - 3)])]
            E("sync", (lambda k_: lambda g: g.dma_start(
                out=xs2[k_ % 3][:], in_=xt_r[k_]))(k),
              waits=lw, inc=(f"dx{k % 3}", 16), tag=f"ld{k}")
            sig[("ld", k)] = M(f"dx{k % 3}")
            bw = [("v", v_cc if k == 0 else sig[("xn", k - 1)])]
            if ("bank", 4) in sig:
                bw.append(("s", max(sig[("bank", 4)], sig[("bank", 5)])))
            E("tensor", (lambda sl_: lambda g: g.matmul(
                out=pb[4][:, 0:TT], lhsT=ones_r1f[0:1, :], rhs=rows[0:1, sl_],
                start=True, stop=True))(sl), waits=bw, tag=f"bcm{k}")
            E("tensor", (lambda k_, sl_: lambda g: g.matmul(
                out=pb[5][:, 0:TT], lhsT=ones_r1f[32:33, :], rhs=rows[32:33, sl_],
                start=True, stop=True))(k, sl),
              waits=[sig[("rrow", k)]], inc=("t", 1), tag=f"bcr{k}")
            sig[("bc2", k)] = M("t")

        def pfx_av(k):
            sl = slice(k * TT, (k + 1) * TT)
            avw = [(f"dx{k % 3}", sig[("ld", k)]), ("t", sig[("bc2", k)])]
            if k >= 2:
                avw.append(("v", sig[("upd", k - 2, 2)]))
            elif k == 1:
                avw.append(("v", sig[("car", 0)]))
            E("vector", (lambda k_: lambda g: g.tensor_sub(
                out=scanb[:], in0=xs2[k_ % 3][:],
                in1=pb[4][:, 0:TT].unsqueeze(1).broadcast_to([P, NCH, TT])))(k),
              waits=avw, inc=("v", 1), tag=f"xn1_{k}")
            vm_x1 = M("v")
            E("vector", (lambda k_: lambda g: g.tensor_mul(
                out=xn[k_ % 2][:], in0=scanb[:],
                in1=pb[5][:, 0:TT].unsqueeze(1).broadcast_to([P, NCH, TT])))(k),
              waits=[("v", vm_x1)], inc=("v", 1), tag=f"xn2_{k}")
            sig[("xn", k)] = M("v")
            hw = [("t", sig[("l1", k - 2, 2, NCH - 1)])] if k >= 2 else []
            hw.append(("v", sig[("xn", k)]))
            if k == 0:
                E("vector", lambda g: g.memset(xnc[0][:, :, 0:2], 0.0),
                  waits=hw, tag="hlc0")
            else:
                E("vector", (lambda k_: lambda g: g.tensor_copy(
                    out=xnc[(k_ - 1) % 2][:, :, TT + 2:TT + 4],
                    in_=xn[k_ % 2][:, :, 0:2]))(k), waits=hw, tag=f"hrc{k - 1}")
                E("vector", (lambda k_: lambda g: g.tensor_copy(
                    out=xnc[k_ % 2][:, :, 0:2],
                    in_=xn[(k_ - 1) % 2][:, :, TT - 2:TT]))(k), tag=f"hlc{k}")
            E("vector", (lambda k_: lambda g: g.tensor_copy(
                out=xnc[k_ % 2][:, :, 2:TT + 2], in_=xn[k_ % 2][:]))(k),
              tag=f"xncc{k}")
            E("vector", (lambda k_: lambda g: g.tensor_copy(
                out=xnb[k_ % 2][:, :, 2:TT + 2], in_=xn[k_ % 2][:]))(k),
              inc=("v", 1), tag=f"xnbc{k}")
            sig[("xnb", k)] = M("v")
            if KDBG:
                E("gpsimd", (lambda k_, sl_: lambda g: g.dma_start(
                    out=dbg_xn_r[:, :, sl_], in_=xn[k_ % 2][:]))(k, sl),
                  waits=[("v", sig[("xn", k)])], inc=("dd", 16), tag=f"dxn{k}")
                E("gpsimd", (lambda k_, sl_: lambda g: g.dma_start(
                    out=dbg_sv_r[:, :, sl_], in_=svb[k_ % 2][:]))(k, sl),
                  waits=[("v", sig[("sv", k)])], inc=("dd", 16), tag=f"dsv{k}")

        def pfx_b(k):
            for c in range(NCH):
                E("vector", (lambda k_, c_: lambda g: g.tensor_tensor_scan(
                    out=scanb[:, c_, :], data0=a_s[:, c_:c_ + 1].broadcast_to([P, TT]),
                    data1=xn[k_ % 2][:, c_, :], initial=carry[:, c_:c_ + 1],
                    op0=ALU.mult, op1=ALU.add))(k, c),
                  inc=("v", 1) if c == NCH - 1 else None, tag=f"scan{k}_{c}")
            sig[("scn", k)] = M("v")
            svw = [("v", sig[("scn", k)])]
            if k == 0:
                svw.append(("t", sig[("sum", 14)]))
            elif k == 1:
                svw.append(("t", sig[("sum", 15)]))
            else:
                svw.append(("t", sig[("attnmm", k - 2, 7)]))
            E("vector", (lambda k_: lambda g: g.tensor_copy(
                out=svb[k_ % 2][:], in_=scanb[:]))(k),
              waits=svw, inc=("v", 1), tag=f"svc{k}")
            sig[("sv", k)] = M("v")
            E("vector", lambda g: g.tensor_copy(out=carry[:], in_=scanb[:, :, TT - 1]),
              inc=("v", 1), tag=f"carry{k}")
            sig[("car", k)] = M("v")

        def attn_emit(k, erange):
            for e in erange:
                bank = 2 + e % 4
                w_ = []
                if e == erange[0]:
                    w_ += [("v", sig[("sv", k)]), ("v", sig[("hal", k)]),
                           ("g", g_diag)]
                    if k == 0:
                        w_.append(("v", v_weights))
                if bank in (4, 5):
                    w_.append(("v", sig[("xn", k + 1)] if ("xn", k + 1) in sig
                               else sig[("xn", k)]))
                w_.append(("s", sig.get(("bank", bank), s_gf)))
                for c in range(NCH):
                    E("tensor", (lambda k_, e_, c_, b_: lambda g: g.matmul(
                        out=pb[b_][:, 0:TT], lhsT=wtsb[:, c_, e_ * P:(e_ + 1) * P],
                        rhs=svb[k_ % 2][:, c_, :], start=(c_ == 0), stop=False))(k, e, c, bank),
                      waits=w_ if c == 0 else (), tag=f"aW{k}_{e}_{c}")
                for kk in range(5):
                    E("tensor", (lambda k_, e_, kk_, b_: lambda g: g.matmul(
                        out=pb[b_][:, 0:TT], lhsT=diag[:, e_ * 5 + kk_, :],
                        rhs=xnc[k_ % 2][:, e_, kk_:kk_ + TT],
                        start=False, stop=(kk_ == 4)))(k, e, kk, bank),
                      inc=("t", 1) if kk == 4 else None, tag=f"aC{k}_{e}_{kk}")
                sig[("attnmm", k, e)] = M("t")
                E("scalar", (lambda k_, e_, b_: lambda g: g.activation(
                    out=shalf[k_ % 2][:, e_, :], in_=pb[b_][:, 0:TT],
                    func=AF.Identity,
                    scale=1.0, bias=bsum[:, e_:e_ + 1]))(k, e, bank),
                  waits=([("t", sig[("attnmm", k, e)])]
                         + ([("v", sig[("fin", k - 2)])] if (e == 0 and k >= 2) else [])),
                  inc=("s", 1), tag=f"evY{k}_{e}")
                sig[("evY", k, e)] = M("s")
                sig[("bank", bank)] = M("s")
            if KDBG and erange[-1] == 7:
                sl_d = slice(k * TT, (k + 1) * TT)
                E("gpsimd", (lambda k_, sl_: lambda g: g.dma_start(
                    out=dbg_s_r[:, :, sl_], in_=shalf[k_ % 2][:]))(k, sl_d),
                  waits=[("s", sig[("evY", k, 7)])], inc=("dd", 16), tag=f"ds{k}")

        def lchain(k, c):
            for m in range(NCH):
                bank = m % 2
                w_ = []
                if m == 0:
                    if c == 0:
                        w_.append(("v", sig[("xnb", k)]))
                        if k == 0:
                            w_.append(("v", v_weights))
                    if c > 0:
                        w_.append(("s", sig[("l3e", k, c - 1)]))
                    elif k > 0:
                        w_.append(("s", sig[("l3e", k - 1, 2)]))
                if m >= 2:
                    w_.append(("s", sig[("l1e", k, c, m - 2)]))
                for cc in range(NCH):
                    ccw = list(w_) if cc == 0 else []
                    if c > 0 and m == 0 and cc in (0, 2, 6):
                        ccw.append(sig[("inbp", k, c, cc)])
                    E("tensor", (lambda k_, m_, cc_, b_: lambda g: g.matmul(
                        out=pb[b_][:, 0:TT], lhsT=w1b[:, cc_, m_ * P:(m_ + 1) * P],
                        rhs=xnb[k_ % 2][:, cc_, 2:TT + 2], start=(cc_ == 0),
                        stop=(cc_ == NCH - 1)))(k, m, cc, bank),
                      waits=ccw,
                      inc=("t", 1) if cc == NCH - 1 else None, tag=f"L1_{k}_{c}_{m}_{cc}")
                sig[("l1", k, c, m)] = M("t")
                E("scalar", (lambda m_, b_: lambda g: g.activation(
                    out=h1b[:, m_, :], in_=pb[b_][:, 0:TT], func=AF.Gelu,
                    bias=svec[:, SV_B1 + m_:SV_B1 + m_ + 1]))(m, bank),
                  waits=[("t", sig[("l1", k, c, m)])],
                  inc=("s", 1), tag=f"l1e{k}_{c}_{m}")
                sig[("l1e", k, c, m)] = M("s")
            for m in range(4):
                bank = m % 2
                w_ = []
                if m >= 2:
                    w_.append(("s", sig[("l2e", k, c, m - 2)]))
                for cc in range(NCH):
                    ccw = list(w_) if cc == 0 else []
                    if m == 0 and cc == 0:
                        ccw.append(("s", sig[("l1e", k, c, NCH - 2)]))
                    if m == 0 and cc == NCH - 1:
                        ccw.append(("s", sig[("l1e", k, c, NCH - 1)]))
                    E("tensor", (lambda m_, cc_, b_: lambda g: g.matmul(
                        out=pb[b_][:, 0:TT], lhsT=w2b[:, cc_, m_ * P:(m_ + 1) * P],
                        rhs=h1b[:, cc_, :], start=(cc_ == 0),
                        stop=(cc_ == NCH - 1)))(m, cc, bank),
                      waits=ccw,
                      inc=("t", 1) if cc == NCH - 1 else None, tag=f"L2_{k}_{c}_{m}")
                sig[("l2", k, c, m)] = M("t")
                E("scalar", (lambda m_, b_: lambda g: g.activation(
                    out=h2b[:, m_, :], in_=pb[b_][:, 0:TT], func=AF.Gelu,
                    bias=svec[:, SV_B2 + m_:SV_B2 + m_ + 1]))(m, bank),
                  waits=[("t", sig[("l2", k, c, m)])],
                  inc=("s", 1), tag=f"l2e{k}_{c}_{m}")
                sig[("l2e", k, c, m)] = M("s")
            for m in range(2):
                bank = m % 2
                for cc in range(4):
                    ccw = []
                    if m == 0 and cc == 0:
                        ccw.append(("s", sig[("l2e", k, c, 2)]))
                    if m == 0 and cc == 3:
                        ccw.append(("s", sig[("l2e", k, c, 3)]))
                    E("tensor", (lambda m_, cc_, b_: lambda g: g.matmul(
                        out=pb[b_][:, 0:TT], lhsT=w3b[:, cc_, m_ * P:(m_ + 1) * P],
                        rhs=h2b[:, cc_, :], start=(cc_ == 0), stop=(cc_ == 3)))(m, cc, bank),
                      waits=ccw,
                      inc=("t", 1) if cc == 3 else None, tag=f"L3_{k}_{c}_{m}")
                sig[("l3", k, c, m)] = M("t")
                E("scalar", (lambda m_, b_: lambda g: g.activation(
                    out=h3b[:, m_, :], in_=pb[b_][:, 0:TT], func=AF.Gelu,
                    bias=svec[:, SV_B3 + m_:SV_B3 + m_ + 1]))(m, bank),
                  waits=[("t", sig[("l3", k, c, m)])],
                  inc=("s", 1), tag=f"l3e{k}_{c}_{m}")
                sig[("l3ep", k, c, m)] = M("s")
            sig[("l3e", k, c)] = M("s")
            if c > 0:
                t4w = [("s", sig[("t4", k, c - 1)])]
            elif k > 0:
                t4w = [("s", sig[("t4", k - 1, 2)])]
            else:
                t4w = []
            for cc in range(2):
                ccw = (list(t4w) if cc == 0 else []) + [("s", sig[("l3ep", k, c, cc)])]
                E("tensor", (lambda cc_: lambda g: g.matmul(
                    out=pb[6][0:1, 0:TT], lhsT=w4b[:, cc_, :], rhs=h3b[:, cc_, :],
                    start=(cc_ == 0), stop=(cc_ == 1)))(cc),
                  waits=ccw,
                  inc=("t", 1) if cc == 1 else None, tag=f"L4_{k}_{c}")
            sig[("l4", k, c)] = M("t")
            E("scalar", lambda g: g.activation(
                out=t4b[:], in_=pb[6][0:1, 0:TT], func=AF.Tanh, scale=0.5,
                bias=rtmp[0:1, 4:5]),
              waits=[("t", sig[("l4", k, c)])], inc=("s", 1), tag=f"t4_{k}_{c}")
            sig[("t4", k, c)] = M("s")

        def OB(k, c):
            return of if (k + c) % 2 == 0 else ofB

        def cpl_upd(k, c):
            w_ = [("s", sig[("t4", k, c)])]
            if c > 0:
                w_.append(("v", sig[("upd", k, c - 1)]))
            elif k > 0:
                w_.append(("v", sig[("upd", k - 1, 2)]))
            else:
                w_.append(("v", v_cc))
            E("tensor", lambda g: g.matmul(
                out=pb[7][:, 0:TT], lhsT=ones_r1b[:], rhs=t4b[:],
                start=True, stop=True), waits=w_, inc=("t", 1), tag=f"cbc{k}_{c}")
            sig[("cbc", k, c)] = M("t")
            E("vector", lambda g: g.tensor_scalar(
                out=cplb[:], in0=pb[7][:, 0:TT], scalar1=c1col[:], scalar2=c0col[:],
                op0=ALU.mult, op1=ALU.add),
              waits=[("t", sig[("cbc", k, c)])], inc=("v", 1), tag=f"cpl{k}_{c}")
            sig[("cpl", k, c)] = M("v")
            st = xn[k % 2] if c == 0 else OB(k, c - 1)
            ob = OB(k, c)
            cb2 = cplb[:].unsqueeze(1).broadcast_to([P, 2, TT])
            cb4 = cplb[:].unsqueeze(1).broadcast_to([P, 4, TT])
            E("vector", (lambda st_, ob_: lambda g: g.tensor_tensor(
                out=ob_[:, 0:2, :], in0=cb2, in1=st_[:, 4:6, :],
                op=ALU.mult))(st, ob),
              waits=[("v", sig[("cpl", k, c)])], tag=f"twA{k}_{c}")
            E("vector", (lambda st_, ob_: lambda g: g.tensor_tensor(
                out=ob_[:, 2:4, :], in0=cb2, in1=st_[:, 6:8, :],
                op=ALU.mult))(st, ob), tag=f"twA2{k}_{c}")
            E("vector", (lambda st_, ob_: lambda g: g.tensor_tensor(
                out=ob_[:, 4:8, :], in0=cb4, in1=st_[:, 0:4, :],
                op=ALU.mult))(st, ob), inc=("v", 1), tag=f"twB{k}_{c}")
            vm_tw = M("v")
            E("vector", (lambda st_, ob_: lambda g: g.tensor_add(
                out=ob_[:, 0:2, :], in0=st_[:, 0:2, :], in1=ob_[:, 0:2, :]))(st, ob),
              waits=[("v", vm_tw)], inc=("v", 1), tag=f"u1{k}_{c}")
            sig[("u1", k, c)] = M("v")
            E("vector", (lambda st_, ob_: lambda g: g.tensor_sub(
                out=ob_[:, 2:6, :], in0=st_[:, 2:6, :], in1=ob_[:, 2:6, :]))(st, ob),
              inc=("v", 1), tag=f"u2{k}_{c}")
            sig[("u2", k, c)] = M("v")
            E("vector", (lambda st_, ob_: lambda g: g.tensor_add(
                out=ob_[:, 6:8, :], in0=st_[:, 6:8, :], in1=ob_[:, 6:8, :]))(st, ob),
              inc=("v", 1), tag=f"u3{k}_{c}")
            sig[("upd", k, c)] = M("v")

        def inb_emit(k, c):
            tmark = sig[("l1", k, c - 1, NCH - 1)]
            src = OB(k, c - 1)
            for (lo, hi) in [(0, 2), (2, 6), (6, 8)]:
                E("vector", (lambda k_, lo_, hi_, s_: lambda g: g.tensor_copy(
                    out=xnb[k_ % 2][:, lo_:hi_, 2:TT + 2],
                    in_=s_[:, lo_:hi_, :]))(k, lo, hi, src),
                  waits=([("t", tmark), ("v", sig[("upd", k, c - 1)])]
                         if lo == 0 else ()), inc=("v", 1),
                  tag=f"inb{k}_{c}_{lo}")
                sig[("inbp", k, c, lo)] = ("v", M("v"))
            sig[("inb", k, c)] = M("v")

        def tail(k):
            sl = slice(k * TT, (k + 1) * TT)
            fb = OB(k, 2)
            if KDBG:
                E("gpsimd", (lambda k_, sl_, f_: lambda g: g.dma_start(
                    out=dbg_of_r[:, :, sl_], in_=f_[:]))(k, sl, fb),
                  waits=[("v", sig[("upd", k, 2)])], inc=("dd", 16), tag=f"dof{k}")
            tgw = [("v", sig[("upd", k, 2)])]
            if ("xn", k + 2) in sig:
                tgw.append(("v", sig[("xn", k + 2)]))
            E("scalar", (lambda f_: lambda g: g.activation(
                out=scanb[:], in_=f_[:], func=AF.Tanh, scale=0.5))(fb),
              waits=tgw, inc=("s", 1), tag=f"tg{k}")
            sig[("tg", k)] = M("s")
            E("vector", (lambda f_: lambda g: g.scalar_tensor_tensor(
                out=scanb[:], in0=scanb[:], scalar=1.0, in1=f_[:],
                op0=ALU.add, op1=ALU.mult))(fb),
              waits=[("s", sig[("tg", k)])], inc=("v", 1), tag=f"g2{k}")
            vm_g2 = M("v")
            E("vector", (lambda k_: lambda g: g.tensor_mul(
                out=scanb[:], in0=scanb[:], in1=shalf[k_ % 2][:]))(k),
              waits=[("v", vm_g2)], inc=("v", 1), tag=f"pr{k}")
            vm_pr = M("v")
            E("vector", (lambda k_: lambda g: g.scalar_tensor_tensor(
                out=xs2[k_ % 3][:], in0=scanb[:], scalar=0.5, in1=xs2[k_ % 3][:],
                op0=ALU.mult, op1=ALU.add))(k),
              waits=[("v", vm_pr)], inc=("v", 1), tag=f"fin{k}")
            sig[("fin", k)] = M("v")
            E("gpsimd", (lambda k_: lambda g: g.dma_start(
                out=yt_r[k_], in_=xs2[k_ % 3][:]))(k),
              waits=[("v", sig[("fin", k)])], inc=(f"do{k % 3}", 16), tag=f"st{k}")
            sig[("st", k)] = M(f"do{k % 3}")

        pfx_ld(0)
        pfx_av(0)
        pfx_b(0)
        for k in range(1, NT + 1):
            if k <= NT - 1:
                pfx_ld(k)
                pfx_av(k)
                sig[("hal", k - 1)] = sig[("xnb", k)]
            if k > NT - 1:
                E("vector", lambda g: g.memset(
                    xnc[(NT - 1) % 2][:, :, TT + 2:TT + 4], 0.0),
                  inc=("v", 1), tag="hrlastc")
                sig[("hal", NT - 1)] = M("v")
            kk = k - 1
            lchain(kk, 0)
            cpl_upd(kk, 0)
            inb_emit(kk, 1)
            if kk >= 1:
                tail(kk - 1)
            attn_emit(kk, [0, 1, 2, 3])
            lchain(kk, 1)
            cpl_upd(kk, 1)
            inb_emit(kk, 2)
            if k <= NT - 1:
                pfx_b(k)
            attn_emit(kk, [4, 5, 6, 7])
            lchain(kk, 2)
            cpl_upd(kk, 2)
        tail(NT - 1)

        if KDBG:
            E("gpsimd", lambda g: g.dma_start(
                out=dbg_w.rearrange("p (c m) -> p c m", c=NCH), in_=wtsb[:]),
              waits=[("s", M("s")), ("v", M("v"))], inc=("dd", 16), tag="dw")
            E("gpsimd", lambda g: g.dma_start(
                out=dbg_dg.rearrange("p (c m) -> p c m", c=40), in_=diag[:]),
              inc=("dd", 16), tag="ddg")
            E("gpsimd", lambda g: g.dma_start(out=dbg_sm[:, 0:SV], in_=svec[:]),
              waits=[("s", M("s")), ("v", M("v"))], inc=("dd", 16), tag="dsm1")
            E("gpsimd", lambda g: g.dma_start(out=dbg_sm[:, SV:SV + 8], in_=bsum[:]),
              inc=("dd", 16), tag="dsm2")
            E("gpsimd", lambda g: g.dma_start(out=dbg_sm[:, SV + 8:SV + 16], in_=a_s[:]),
              inc=("dd", 16), tag="dsm3")
        E("gpsimd", lambda g: g.engine_nop(),
          waits=[("do0", M("do0")), ("do1", M("do1")), ("do2", M("do2")),
                 ("dd", M("dd"))],
          tag="final_wait")

        sch.simulate()

        def _runner(name):
            def _body(eng):
                for fn, waits, inc, tag in sch.ops[name]:
                    for s_, v_ in waits:
                        eng.wait_ge(sems[s_], v_)
                    inst = fn(eng)
                    if inc is not None:
                        inst.then_inc(sems[inc[0]], inc[1])
            return _body

        block.sync(_runner("sync"))
        block.gpsimd(_runner("gpsimd"))
        block.vector(_runner("vector"))
        block.scalar(_runner("scalar"))
        block.tensor(_runner("tensor"))

    return nc


def kernel(**inputs):
    x = np.asarray(inputs["L0_attn_x"], dtype=np.float32)
    W = np.asarray(inputs["L0_attn_W"], dtype=np.float32)
    svec = np.zeros((P, SV), dtype=np.float32)

    def col_layout(v, nch):
        return np.ascontiguousarray(np.asarray(v, dtype=np.float32).reshape(nch, P).T)

    svec[:, SV_DL:SV_DL + 8] = col_layout(inputs["L0_attn_decay_logit"], 8)
    svec[:, SV_BATT:SV_BATT + 8] = col_layout(inputs["L0_attn_b"], 8)
    svec[:, SV_CONVB:SV_CONVB + 8] = col_layout(inputs["L0_attn_conv_b"], 8)
    svec[:, SV_B1:SV_B1 + 8] = col_layout(inputs["L0_ffn_cn_b1"], 8)
    svec[:, SV_B2:SV_B2 + 4] = col_layout(inputs["L0_ffn_cn_b2"], 4)
    svec[:, SV_B3:SV_B3 + 2] = col_layout(inputs["L0_ffn_cn_b3"], 2)
    svec[:, SV_GB1:SV_GB1 + 4] = col_layout(inputs["L0_ffn_gc_b1"], 4)
    svec[:, SV_GB2:SV_GB2 + 2] = col_layout(inputs["L0_ffn_gc_b2"], 2)
    cw = np.asarray(inputs["L0_attn_conv_w"], np.float32).reshape(D, 5)
    svec[:, SV_CW:SV_CW + 40] = cw.reshape(NCH, P, 5).transpose(1, 0, 2).reshape(P, 40)
    srow = np.zeros((1, 4), dtype=np.float32)
    srow[0, 0] = np.asarray(inputs["L0_ffn_cn_b4"], np.float32).reshape(())
    srow[0, 1] = np.asarray(inputs["L0_ffn_gc_b3"], np.float32).reshape(())
    srow[0, 2] = np.asarray(inputs["L0_ffn_adaptive_range"], np.float32).reshape(())

    common = {
        "w_ret_t": np.ascontiguousarray(W.T),
        "cn_w1": np.ascontiguousarray(np.asarray(inputs["L0_ffn_cn_w1"], np.float32)),
        "cn_w2": np.ascontiguousarray(np.asarray(inputs["L0_ffn_cn_w2"], np.float32)),
        "cn_w3": np.ascontiguousarray(np.asarray(inputs["L0_ffn_cn_w3"], np.float32)),
        "cn_w4": np.ascontiguousarray(np.asarray(inputs["L0_ffn_cn_w4"], np.float32).reshape(2, P).T),
        "gc_w1": np.ascontiguousarray(np.asarray(inputs["L0_ffn_gc_w1"], np.float32)),
        "gc_w2": np.ascontiguousarray(np.asarray(inputs["L0_ffn_gc_w2"], np.float32)),
        "gc_w3": np.ascontiguousarray(np.asarray(inputs["L0_ffn_gc_w3"], np.float32).reshape(2, P).T),
        "svec": svec,
        "srow": srow,
    }
    in_maps = []
    for b in range(N_CORES):
        m = dict(common)
        m["x_t"] = np.ascontiguousarray(
            x[b].T.reshape(NCH, P, NT, TT).transpose(2, 1, 0, 3).reshape(NT, P, NCH * TT))
        in_maps.append(m)

    nc = build_nc()
    res = run_bass_kernel_spmd(nc, in_maps, core_ids=list(range(N_CORES)))
    outs = []
    for r in res.results:
        yt = np.asarray(r["y_t"]).reshape(NT, P, NCH, TT).transpose(2, 1, 0, 3)
        outs.append(yt.reshape(D, T).T)
    out = np.stack(outs, axis=0)
    return np.ascontiguousarray(out).astype(np.float32)


# version 31
# speedup vs baseline: 1.0192x; 1.0055x over previous
import numpy as np
from contextlib import ExitStack

import concourse.bass as bass
import concourse.mybir as mybir
from concourse.bass_utils import run_bass_kernel_spmd

F32 = mybir.dt.float32
BF16 = mybir.dt.bfloat16
AF = mybir.ActivationFunctionType
ALU = mybir.AluOpType

P = 128
D = 1024
NCH = D // P
T = 4096
TT = 256
NT = T // TT
EPS = 1e-5
N_CORES = 8

SV_DL = 0
SV_BATT = 8
SV_CONVB = 16
SV_B1 = 24
SV_B2 = 32
SV_B3 = 36
SV_GB1 = 38
SV_GB2 = 42
SV_CW = 44
SV = 84


class Sched:
    ENGINES = ("sync", "gpsimd", "vector", "scalar", "tensor")

    def __init__(self):
        self.ops = {e: [] for e in self.ENGINES}
        self.cnt = {s: 0 for s in ("di", "dw0", "dw1", "dx0", "dx1", "dx2", "dx3", "dx4", "dx5", "dx6",
                           "dr", "dr2", "do0", "do1", "do2", "dd", "t", "v", "s", "g")}

    def emit(self, eng, fn, waits=(), inc=None, tag=""):
        waits = [(s, v) for (s, v) in waits if v > 0]
        self.ops[eng].append((fn, waits, inc, tag))
        if inc is not None:
            self.cnt[inc[0]] += inc[1]

    def mark(self, sem):
        return self.cnt[sem]

    def simulate(self):
        pos = {e: 0 for e in self.ENGINES}
        val = {s: 0 for s in self.cnt}
        progress = True
        while progress:
            progress = False
            for e in self.ENGINES:
                while pos[e] < len(self.ops[e]):
                    fn, waits, inc, tag = self.ops[e][pos[e]]
                    if all(val[s] >= v for (s, v) in waits):
                        if inc:
                            val[inc[0]] += inc[1]
                        pos[e] += 1
                        progress = True
                    else:
                        break
        stuck = {e: pos[e] for e in self.ENGINES if pos[e] < len(self.ops[e])}
        if stuck:
            lines = []
            for e, p_ in stuck.items():
                fn, waits, inc, tag = self.ops[e][p_]
                lines.append(f"  {e}[{p_}] tag={tag} waits={waits} "
                             f"(now={[(s, val[s]) for s, _ in waits]})")
            raise RuntimeError("DEADLOCK in schedule:\n" + "\n".join(lines))


def build_nc():
    nc = bass.Bass("TRN2", target_bir_lowering=False)

    xt = nc.dram_tensor("x_t", [NT, P, NCH * TT], F32, kind="ExternalInput").ap()
    wts_h = nc.dram_tensor("w_ret_t", [D, D], F32, kind="ExternalInput").ap()
    w1_h = nc.dram_tensor("cn_w1", [D, D], F32, kind="ExternalInput").ap()
    w2_h = nc.dram_tensor("cn_w2", [D, 512], F32, kind="ExternalInput").ap()
    w3_h = nc.dram_tensor("cn_w3", [512, 256], F32, kind="ExternalInput").ap()
    w4_h = nc.dram_tensor("cn_w4", [P, 2], F32, kind="ExternalInput").ap()
    g1_h = nc.dram_tensor("gc_w1", [D, 512], F32, kind="ExternalInput").ap()
    g2_h = nc.dram_tensor("gc_w2", [512, 256], F32, kind="ExternalInput").ap()
    g3_h = nc.dram_tensor("gc_w3", [P, 2], F32, kind="ExternalInput").ap()
    sv_h = nc.dram_tensor("svec", [P, SV], F32, kind="ExternalInput").ap()
    sr_h = nc.dram_tensor("srow", [1, 4], F32, kind="ExternalInput").ap()
    yt = nc.dram_tensor("y_t", [NT, P, NCH * TT], F32, kind="ExternalOutput").ap()

    import os
    KDBG = bool(int(os.environ.get("KDBG", "0")))
    if KDBG:
        dbg_xn = nc.dram_tensor("dbg_xn", [D, T], F32, kind="ExternalOutput").ap()
        dbg_sv = nc.dram_tensor("dbg_sv", [D, T], BF16, kind="ExternalOutput").ap()
        dbg_s = nc.dram_tensor("dbg_s", [D, T], F32, kind="ExternalOutput").ap()
        dbg_of = nc.dram_tensor("dbg_of", [D, T], F32, kind="ExternalOutput").ap()
        dbg_xn_r = dbg_xn.rearrange("(c p) t -> p c t", p=P)
        dbg_sv_r = dbg_sv.rearrange("(c p) t -> p c t", p=P)
        dbg_s_r = dbg_s.rearrange("(c p) t -> p c t", p=P)
        dbg_of_r = dbg_of.rearrange("(c p) t -> p c t", p=P)
        dbg_sm = nc.dram_tensor("dbg_sm", [P, SV + 48], F32, kind="ExternalOutput").ap()
        dbg_w = nc.dram_tensor("dbg_w", [P, NCH * D], BF16, kind="ExternalOutput").ap()
        dbg_w2 = nc.dram_tensor("dbg_w2", [P, NCH * D], BF16, kind="ExternalOutput").ap()
        dbg_dg = nc.dram_tensor("dbg_dg", [P, 40 * P], BF16, kind="ExternalOutput").ap()

    xt_r = xt.rearrange("k p (c t) -> k p c t", c=NCH)
    yt_r = yt.rearrange("k p (c t) -> k p c t", c=NCH)

    sch = Sched()
    E = sch.emit
    M = sch.mark
    sig = {}

    with ExitStack() as ctx:
        e_ = ctx.enter_context
        wtsb = e_(nc.sbuf_tensor("wtsb", [P, NCH, D], BF16))
        w1b = e_(nc.sbuf_tensor("w1b", [P, NCH, D], BF16))
        w2b = e_(nc.sbuf_tensor("w2b", [P, NCH, 512], BF16))
        w3b = e_(nc.sbuf_tensor("w3b", [P, 4, 256], BF16))
        w4b = e_(nc.sbuf_tensor("w4b", [P, 2, 1], BF16))
        gw1b = e_(nc.sbuf_tensor("gw1b", [P, NCH, 512], BF16))
        gw2b = e_(nc.sbuf_tensor("gw2b", [P, 4, 256], BF16))
        gw3b = e_(nc.sbuf_tensor("gw3b", [P, 2, 1], BF16))
        diag = e_(nc.sbuf_tensor("diag", [P, 40, P], BF16))
        ones_sq_b = e_(nc.sbuf_tensor("ones_sq_b", [P, P], BF16))
        ones_r1b = e_(nc.sbuf_tensor("ones_r1b", [1, P], BF16))
        ones_r1f = e_(nc.sbuf_tensor("ones_r1f", [33, P], F32))
        svec = e_(nc.sbuf_tensor("svec_s", [P, SV], F32))
        a_t = e_(nc.sbuf_tensor("a_t", [P, NCH], F32))
        a_s = e_(nc.sbuf_tensor("a_s", [P, NCH], F32))
        oma = e_(nc.sbuf_tensor("oma", [P, NCH], F32))
        bsum = e_(nc.sbuf_tensor("bsum", [P, NCH], F32))
        c0col = e_(nc.sbuf_tensor("c0col", [P, 1], F32))
        c1col = e_(nc.sbuf_tensor("c1col", [P, 1], F32))
        carry = e_(nc.sbuf_tensor("carry", [P, NCH], F32))
        zacc = e_(nc.sbuf_tensor("zacc", [P, NCH * NT], F32))
        gcv = e_(nc.sbuf_tensor("gcv", [P, NCH], F32))
        gcvb = e_(nc.sbuf_tensor("gcvb", [P, NCH], BF16))
        gch1b = e_(nc.sbuf_tensor("gch1b", [P, 4], BF16))
        gch2b = e_(nc.sbuf_tensor("gch2b", [P, 2], BF16))
        csum = e_(nc.sbuf_tensor("csum", [1, NT], F32))
        rows = e_(nc.sbuf_tensor("rows", [33, T], F32))
        rs1 = e_(nc.sbuf_tensor("rs1", [1, 4 * TT], F32))
        rs2 = e_(nc.sbuf_tensor("rs2", [1, 4 * TT], F32))
        rtmp = e_(nc.sbuf_tensor("rtmp", [1, 16], F32))
        t4b = e_(nc.sbuf_tensor("t4b", [1, TT], BF16))
        xs2 = [e_(nc.sbuf_tensor(f"xs2_{i}", [P, NCH, TT], F32)) for i in range(3)]
        xn = [e_(nc.sbuf_tensor(f"xn_{i}", [P, NCH, TT], F32)) for i in range(2)]
        xnb = [e_(nc.sbuf_tensor(f"xnb_{i}", [P, NCH, TT + 4], BF16)) for i in range(2)]
        xnc = [e_(nc.sbuf_tensor(f"xnc_{i}", [P, NCH, TT + 4], BF16)) for i in range(2)]
        scanb = e_(nc.sbuf_tensor("scanb", [P, NCH, TT], F32))
        svb = [e_(nc.sbuf_tensor(f"svb_{i}", [P, NCH, TT], BF16)) for i in range(2)]
        shalf = [e_(nc.sbuf_tensor(f"shalf_{i}", [P, NCH, TT], F32)) for i in range(2)]
        of = e_(nc.sbuf_tensor("of", [P, NCH, TT], F32))
        ofB = e_(nc.sbuf_tensor("ofB", [P, NCH, TT], F32))
        h1b = e_(nc.sbuf_tensor("h1b", [P, NCH, TT], BF16))
        h2b = e_(nc.sbuf_tensor("h2b", [P, 4, TT], BF16))
        h3b = e_(nc.sbuf_tensor("h3b", [P, 2, TT], BF16))
        cplb = e_(nc.sbuf_tensor("cplb", [P, TT], F32))
        pb = [e_(nc.psum_tensor(f"pb{i}", [P, 512], F32)) for i in range(8)]
        sems = {n: e_(nc.semaphore(n)) for n in
                ("di", "dw0", "dw1", "dx0", "dx1", "dx2", "dx3", "dx4", "dx5", "dx6",
                 "dr", "dr2", "do0", "do1", "do2", "dd", "t", "v", "s", "g")}
        block = e_(nc.Block())

        E("sync", lambda g: g.dma_start(out=svec[:], in_=sv_h), inc=("di", 16), tag="ld_svec")
        E("sync", lambda g: g.dma_start(out=rtmp[0:1, 0:4], in_=sr_h), inc=("di", 16), tag="ld_srow")
        E("sync", lambda g: g.dma_start(out=cplb[:, 0:2], in_=w4_h),
          inc=("di", 16), tag="ld_w4")
        E("sync", lambda g: g.dma_start(out=cplb[:, 2:4], in_=g3_h),
          inc=("di", 16), tag="ld_g3")
        di_small = M("di")

        E("scalar", lambda g: g.activation(out=a_t[:], in_=svec[:, SV_DL:SV_DL + 8],
                                           func=AF.Tanh, scale=0.5),
          waits=[("di", di_small)], inc=("s", 1), tag="a_tanh")
        s_at = M("s")
        E("vector", lambda g: g.tensor_scalar(out=a_s[:], in0=a_t[:], scalar1=0.5,
                                              scalar2=0.5, op0=ALU.mult, op1=ALU.add),
          waits=[("s", s_at)], tag="a_s")
        E("vector", lambda g: g.tensor_scalar(out=oma[:], in0=a_t[:], scalar1=-0.5,
                                              scalar2=0.5, op0=ALU.mult, op1=ALU.add),
          tag="oma")
        E("vector", lambda g: g.scalar_tensor_tensor(
            out=bsum[:], in0=svec[:, SV_BATT:SV_BATT + 8], scalar=1.0,
            in1=svec[:, SV_CONVB:SV_CONVB + 8], op0=ALU.mult, op1=ALU.add),
          waits=[("di", di_small)], tag="bsum0")
        E("vector", lambda g: g.memset(ones_sq_b[:], 1.0 / D), tag="ones1")
        E("vector", lambda g: g.memset(ones_r1b[:], 1.0), tag="ones2")
        E("vector", lambda g: g.memset(ones_r1f[:], 1.0), tag="ones3")
        E("vector", lambda g: g.memset(carry[:], 0.0), tag="carry0")
        E("vector", lambda g: g.tensor_scalar(out=rtmp[0:1, 4:5], in0=rtmp[0:1, 0:1],
                                              scalar1=0.5, scalar2=None, op0=ALU.mult),
          waits=[("di", di_small)], tag="b4h")
        E("vector", lambda g: g.tensor_scalar(out=rtmp[0:1, 5:6], in0=rtmp[0:1, 1:2],
                                              scalar1=0.5, scalar2=None, op0=ALU.mult),
          inc=("v", 1), tag="gb3h")
        v_prep = M("v")
        if KDBG:
            E("gpsimd", lambda g: g.dma_start(out=dbg_sm[:, SV + 16:SV + 24], in_=bsum[:]),
              waits=[("v", v_prep)], inc=("dd", 16), tag="dbsum_prep")

        for c in range(NCH):
            for k in range(5):
                idx = c * 5 + k
                col = SV_CW + idx
                E("gpsimd",
                  (lambda col_, idx_: lambda g: g.affine_select(
                      out=diag[:, idx_, :],
                      in_=svec[:, col_:col_ + 1].broadcast_to([P, P]),
                      pattern=[[1, P]], compare_op=ALU.is_equal, fill=0.0,
                      base=0, channel_multiplier=-1))(col, idx),
                  waits=[("di", di_small)] if idx == 0 else (),
                  inc=("g", 1) if idx == 39 else None, tag=f"diag{idx}")
        g_diag = M("g")

        E("vector", lambda g: g.tensor_copy(out=w4b[:, :, 0], in_=cplb[:, 0:2]),
          waits=[("di", di_small)], tag="cast_w4")
        E("vector", lambda g: g.tensor_copy(out=gw3b[:, :, 0], in_=cplb[:, 2:4]),
          inc=("v", 1), tag="cast_g3")
        tiny_cast = M("v")

        for j0 in range(5):
            E("sync", (lambda j_: lambda g: g.dma_start(
                out=(xs2 + shalf + xn)[j_][:], in_=xt_r[j_]))(j0),
              inc=(f"dx{j0 % 7}", 16), tag=f"p1ld{j0}")
            sig[("ld1", j0)] = M(f"dx{j0 % 7}")

        wstage = [of, scanb]
        wjobs = [
            (wts_h.rearrange("(c p) m -> p c m", p=P), wtsb, 8, D, True),
            (w1_h.rearrange("(c p) m -> p c m", p=P), w1b, 8, D, False),
            (w2_h.rearrange("(c p) m -> p c m", p=P), w2b, 8, 512, False),
            (w3_h.rearrange("(c p) m -> p c m", p=P), w3b, 4, 256, False),
            (g1_h.rearrange("(c p) m -> p c m", p=P), gw1b, 8, 512, False),
            (g2_h.rearrange("(c p) m -> p c m", p=P), gw2b, 4, 256, False),
        ]
        segs = []
        for (srcw, dst, nch_w, cols, scaled) in wjobs:
            step = max(1, 2048 // cols)
            for c0 in range(0, nch_w, step):
                segs.append((srcw, dst, c0, min(step, nch_w - c0), cols, scaled))
        wl = 0
        last_cast = [tiny_cast] * len(wstage)
        for (srcw, dst, c0, nch_s, cols, scaled) in segs:
            h = wl % len(wstage)
            stg = wstage[h][:].rearrange("p a t -> p (a t)")
            E("sync",
              (lambda srcw_, c0_, n_, cols_, stg_:
               lambda g: g.dma_start(
                   out=stg_[:, 0:n_ * cols_].rearrange("p (c m) -> p c m", c=n_),
                   in_=srcw_[:, c0_:c0_ + n_, :]))(srcw, c0, nch_s, cols, stg),
              waits=[("v", last_cast[h])], inc=(f"dw{wl % 2}", 16), tag=f"wld{wl}")
            di_w = M(f"dw{wl % 2}")
            if scaled:
                for ci in range(nch_s):
                    E("vector", (lambda dst_, c0_, ci_, cols_, stg_:
                                 lambda g: g.tensor_scalar(
                                     out=dst_[:, c0_ + ci_, :],
                                     in0=stg_[:, ci_ * cols_:(ci_ + 1) * cols_],
                                     scalar1=oma[:, c0_ + ci_:c0_ + ci_ + 1],
                                     scalar2=None, op0=ALU.mult))(dst, c0, ci, cols, stg),
                      waits=[(f"dw{wl % 2}", di_w), ("v", v_prep)] if ci == 0 else [],
                      inc=("v", 1) if ci == nch_s - 1 else None, tag=f"wcast{wl}_{ci}")
            else:
                E("vector", (lambda dst_, c0_, n_, cols_, stg_:
                             lambda g: g.tensor_copy(
                                 out=dst_[:, c0_:c0_ + n_, :],
                                 in_=stg_[:, 0:n_ * cols_].rearrange(
                                     "p (c m) -> p c m", c=n_)))(dst, c0, nch_s, cols, stg),
                  waits=[(f"dw{wl % 2}", di_w)], inc=("v", 1), tag=f"wcast{wl}")
            last_cast[h] = M("v")
            wl += 1
        v_weights = M("v")
        if KDBG:
            E("gpsimd", lambda g: g.dma_start(
                out=dbg_w2.rearrange("p (c m) -> p c m", c=NCH), in_=wtsb[:]),
              waits=[("v", v_weights)], inc=("dd", 16), tag="dw2")

        p1x = xs2 + shalf + xn
        NX1 = len(p1x)
        for j in range(NT + 3):
            if j < NT:
                if j >= 5:
                    lw = [("v", sig[("z", j - NX1)])] if j >= NX1 else []
                    E("sync", (lambda j_: lambda g: g.dma_start(
                        out=p1x[j_ % NX1][:], in_=xt_r[j_]))(j),
                      waits=lw, inc=(f"dx{j % 7}", 16), tag=f"p1ld{j}")
                    sig[("ld1", j)] = M(f"dx{j % 7}")
                cw = [(f"dx{j % 7}", sig[("ld1", j)])]
                if j >= 2:
                    cw.append(("t", sig[("sum", j - 2)]))
                E("scalar", (lambda j_: lambda g: g.activation(
                    out=svb[j_ % 2][:], in_=p1x[j_ % NX1][:], func=AF.Copy))(j),
                  waits=cw, inc=("s", 1), tag=f"p1cast{j}")
                sig[("xb", j)] = M("s")
                sqw = [("s", sig[("xb", j)])]
                if j >= 2:
                    sqw.append(("t", sig[("sum", j - 2)]))
                sqb = h1b[:] if j % 2 == 0 else xnb[0][:, :, 2:TT + 2]
                E("vector", (lambda j_, sqb_: lambda g: g.tensor_mul(
                    out=sqb_, in0=svb[j_ % 2][:], in1=svb[j_ % 2][:]))(j, sqb),
                  waits=sqw, inc=("v", 1), tag=f"p1sq{j}")
                sig[("sq", j)] = M("v")
                sig[("sqb", j)] = sqb
            jj = j - 1
            if 0 <= jj < NT:
                sl = slice(jj * TT, (jj + 1) * TT)
                pS = 0 if jj % 2 == 0 else 2
                mmw = [("v", sig[("sq", jj)])]
                if jj == 0:
                    mmw.append(("v", v_prep))
                if jj >= 2:
                    mmw.append(("s", sig[("mq", jj - 2)]))
                for c in range(NCH):
                    E("tensor", (lambda j_, c_, pS_: lambda g: g.matmul(
                        out=pb[pS_][:, 0:TT], lhsT=ones_sq_b[:],
                        rhs=svb[j_ % 2][:, c_, :],
                        start=(c_ == 0), stop=(c_ == NCH - 1)))(jj, c, pS),
                      waits=mmw if c == 0 else (), tag=f"p1mmx{jj}_{c}")
                sqb = sig[("sqb", jj)]
                for c in range(NCH):
                    E("tensor", (lambda c_, pS_, sqb_: lambda g: g.matmul(
                        out=pb[pS_ + 1][:, 0:TT], lhsT=ones_sq_b[:],
                        rhs=sqb_[:, c_, :],
                        start=(c_ == 0), stop=(c_ == NCH - 1)))(c, pS, sqb),
                      inc=("t", 1) if c == NCH - 1 else None, tag=f"p1mmq{jj}_{c}")
                sig[("sum", jj)] = M("t")
                E("scalar", (lambda sl_, pS_: lambda g: g.activation(
                    out=rows[0:1, sl_], in_=pb[pS_][0:1, 0:TT], func=AF.Copy))(sl, pS),
                  waits=[("t", sig[("sum", jj)])], tag=f"p1m{jj}")
                ro = (jj % 4) * TT
                qw = [("v", sig[("z", jj - 4)])] if jj >= 4 else []
                E("scalar", (lambda pS_, ro_: lambda g: g.activation(
                    out=rs1[0:1, ro_:ro_ + TT], in_=pb[pS_ + 1][0:1, 0:TT],
                    func=AF.Copy))(pS, ro),
                  waits=qw, inc=("s", 1), tag=f"p1q{jj}")
                sig[("mq", jj)] = M("s")
            jp = j - 2
            if jp >= 1 and jp % 2 == 1 and jp < NT:
                pair = jp // 2
                TT2 = 2 * TT
                pbase = (pair % 2) * TT2
                psl = slice((jp - 1) * TT, (jp + 1) * TT)
                pR = 4 + pair % 2
                mmw2 = [("s", sig[("mq", jp)])]
                if pair >= 2:
                    mmw2.append(sig[("rrowp", pair - 2)])
                    mmw2.append(("t", sig[("bcp", pair - 2)]))
                E("scalar", (lambda psl_, pb_: lambda g: g.activation(
                    out=rs2[0:1, pb_:pb_ + TT2], in_=rows[0:1, psl_],
                    func=AF.Square))(psl, pbase),
                  waits=mmw2, inc=("s", 1), tag=f"p1mm{pair}")
                sm_sq = M("s")
                E("vector", (lambda pb_: lambda g: g.scalar_tensor_tensor(
                    out=rs1[0:1, pb_:pb_ + TT2], in0=rs1[0:1, pb_:pb_ + TT2],
                    scalar=float(EPS), in1=rs2[0:1, pb_:pb_ + TT2],
                    op0=ALU.add, op1=ALU.subtract))(pbase),
                  waits=[("s", sm_sq)], inc=("v", 1), tag=f"p1var{pair}")
                sig[("var", pair)] = M("v")
                E("scalar", (lambda pb_: lambda g: g.activation(
                    out=rs1[0:1, pb_:pb_ + TT2], in_=rs1[0:1, pb_:pb_ + TT2],
                    func=AF.Ln))(pbase),
                  waits=[("v", sig[("var", pair)])], inc=("s", 1), tag=f"p1ln{pair}")
                sm_ln = M("s")
                E("scalar", (lambda pb_: lambda g: g.activation(
                    out=rs2[0:1, pb_:pb_ + TT2], in_=rs1[0:1, pb_:pb_ + TT2],
                    func=AF.Exp, scale=-0.5))(pbase),
                  waits=[("s", sm_ln)], inc=("s", 1), tag=f"p1exp{pair}")
                sig[("rstdp", pair)] = M("s")
                sig[("rstd", jp - 1)] = sig[("rstd", jp)] = sig[("rstdp", pair)]
                rsem = "dr" if pair % 2 == 0 else "dr2"
                E("gpsimd", (lambda psl_, pb_: lambda g: g.dma_start(
                    out=rows[32:33, psl_], in_=rs2[0:1, pb_:pb_ + TT2]))(psl, pbase),
                  waits=[("s", sig[("rstdp", pair)])], inc=(rsem, 16),
                  tag=f"rrow{pair}")
                sig[("rrowp", pair)] = (rsem, M(rsem))
                sig[("rrow", jp - 1)] = sig[("rrow", jp)] = sig[("rrowp", pair)]
                bw = [("s", sig[("rstdp", pair)])]
                if pair >= 1:
                    bw.append(("v", sig[("z", jp - 2)]))
                E("tensor", (lambda pR_, pb_: lambda g: g.matmul(
                    out=pb[pR_][:, 0:TT2], lhsT=ones_r1f[0:1, :],
                    rhs=rs2[0:1, pb_:pb_ + TT2],
                    start=True, stop=True))(pR, pbase),
                  waits=bw, inc=("t", 1), tag=f"p1bc{pair}")
                sig[("bcp", pair)] = M("t")
                for jz, po in ((jp - 1, 0), (jp, TT)):
                    for c in range(NCH):
                        zw = []
                        if po == 0 and c == 0:
                            zw.append(("t", sig[("bcp", pair)]))
                        if po != 0 and c == 0:
                            zw.append(("v", M("v") if False else sig[("zhalf", pair)]))
                        E("vector", (lambda jz_, c_, pR_, po_: lambda g: g.scalar_tensor_tensor(
                            out=xnc[0][:, c_, 0:TT], in0=p1x[jz_ % NX1][:, c_, :], scalar=1.0,
                            in1=pb[pR_][:, po_:po_ + TT], op0=ALU.mult, op1=ALU.mult,
                            accum_out=zacc[:, c_ * NT + jz_:c_ * NT + jz_ + 1]))(jz, c, pR, po),
                          waits=zw,
                          inc=("v", 1) if (po == 0 and c == NCH - 1) else None,
                          tag=f"p1z{jz}_{c}")
                        if po == 0 and c == NCH - 1:
                            sig[("zhalf", pair)] = M("v")
                E("vector", (lambda psl_, pb_: lambda g: g.tensor_mul(
                    out=rs1[0:1, pb_:pb_ + TT2], in0=rows[0:1, psl_],
                    in1=rs2[0:1, pb_:pb_ + TT2]))(psl, pbase),
                  waits=[("s", sig[("rstdp", pair)])], inc=("v", 1), tag=f"p1cr{pair}")
                vm_cr = M("v")
                E("vector", (lambda pair_, pb_: lambda g: g.reduce_sum(
                    out=csum[0:1, pair_:pair_ + 1], in_=rs1[0:1, pb_:pb_ + TT2],
                    axis=mybir.AxisListType.X))(pair, pbase),
                  waits=[("v", vm_cr)], inc=("v", 1), tag=f"p1cs{pair}")
                sig[("z", jp - 1)] = sig[("z", jp)] = M("v")

        if KDBG:
            E("gpsimd", lambda g: g.dma_start(out=dbg_sm[:, SV + 24:SV + 32], in_=bsum[:]),
              waits=[("v", sig[("z", NT - 1)])], inc=("dd", 16), tag="dbsum_p1")
        E("vector", lambda g: g.reduce_sum(
            out=gcv[:], in_=zacc[:].rearrange("p (c j) -> p c j", j=NT),
            axis=mybir.AxisListType.X),
          waits=[("v", sig[("z", NT - 1)])], inc=("v", 1), tag="zred")
        v_zred = M("v")
        E("vector", lambda g: g.reduce_sum(out=rtmp[0:1, 6:7],
                                           in_=csum[0:1, 0:NT // 2],
                                           axis=mybir.AxisListType.X),
          inc=("v", 1), tag="csred")
        v_gc0 = M("v")
        E("tensor", lambda g: g.matmul(out=pb[7][:, 0:1], lhsT=ones_r1f[0:1, :],
                                       rhs=rtmp[0:1, 6:7], start=True, stop=True),
          waits=[("v", v_gc0)], inc=("t", 1), tag="csbc")
        t_csbc = M("t")
        E("vector", lambda g: g.tensor_scalar(out=gcv[:], in0=gcv[:],
                                              scalar1=pb[7][:, 0:1], scalar2=float(1.0 / T),
                                              op0=ALU.subtract, op1=ALU.mult),
          waits=[("t", t_csbc), ("v", v_zred)], inc=("v", 1), tag="gcv")
        v_gcvop = M("v")
        E("vector", lambda g: g.tensor_copy(out=gcvb[:], in_=gcv[:]),
          waits=[("v", v_gcvop)], inc=("v", 1), tag="gcvb")
        v_gcv = M("v")
        for m in range(4):
            for c in range(NCH):
                E("tensor", (lambda m_, c_: lambda g: g.matmul(
                    out=pb[0][:, m_:m_ + 1], lhsT=gw1b[:, c_, m_ * P:(m_ + 1) * P],
                    rhs=gcvb[:, c_:c_ + 1], start=(c_ == 0), stop=(c_ == NCH - 1)))(m, c),
                  waits=[("v", v_gcv)] if (m == 0 and c == 0) else (),
                  inc=("t", 1) if (m == 3 and c == NCH - 1) else None, tag=f"gch1_{m}_{c}")
        t_g1 = M("t")
        for m in range(4):
            E("scalar", (lambda m_: lambda g: g.activation(
                out=gch1b[:, m_:m_ + 1], in_=pb[0][:, m_:m_ + 1], func=AF.Gelu,
                bias=svec[:, SV_GB1 + m_:SV_GB1 + m_ + 1]))(m),
              waits=[("t", t_g1)] if m == 0 else (),
              inc=("s", 1) if m == 3 else None, tag=f"gce1_{m}")
        s_g1 = M("s")
        for m in range(2):
            for c in range(4):
                E("tensor", (lambda m_, c_: lambda g: g.matmul(
                    out=pb[1][:, m_:m_ + 1], lhsT=gw2b[:, c_, m_ * P:(m_ + 1) * P],
                    rhs=gch1b[:, c_:c_ + 1], start=(c_ == 0), stop=(c_ == 3)))(m, c),
                  waits=[("s", s_g1)] if (m == 0 and c == 0) else (),
                  inc=("t", 1) if (m == 1 and c == 3) else None, tag=f"gch2_{m}_{c}")
        t_g2 = M("t")
        for m in range(2):
            E("scalar", (lambda m_: lambda g: g.activation(
                out=gch2b[:, m_:m_ + 1], in_=pb[1][:, m_:m_ + 1], func=AF.Gelu,
                bias=svec[:, SV_GB2 + m_:SV_GB2 + m_ + 1]))(m),
              waits=[("t", t_g2)] if m == 0 else (),
              inc=("s", 1) if m == 1 else None, tag=f"gce2_{m}")
        s_g2 = M("s")
        for c in range(2):
            E("tensor", (lambda c_: lambda g: g.matmul(
                out=pb[2][0:1, 0:1], lhsT=gw3b[:, c_, :], rhs=gch2b[:, c_:c_ + 1],
                start=(c_ == 0), stop=(c_ == 1)))(c),
              waits=[("s", s_g2)] if c == 0 else (),
              inc=("t", 1) if c == 1 else None, tag=f"gch3_{c}")
        t_g3 = M("t")
        E("scalar", lambda g: g.activation(out=rtmp[0:1, 7:8], in_=pb[2][0:1, 0:1],
                                           func=AF.Tanh, scale=0.5, bias=rtmp[0:1, 5:6]),
          waits=[("t", t_g3)], inc=("s", 1), tag="tgf")
        s_gf = M("s")
        E("vector", lambda g: g.tensor_mul(out=rtmp[0:1, 8:9], in0=rtmp[0:1, 7:8],
                                           in1=rtmp[0:1, 2:3]),
          waits=[("s", s_gf)], inc=("v", 1), tag="c0a")
        v_c0a = M("v")
        E("vector", lambda g: g.tensor_scalar(out=rtmp[0:1, 8:9], in0=rtmp[0:1, 8:9],
                                              scalar1=0.7, scalar2=0.1,
                                              op0=ALU.mult, op1=ALU.add),
          waits=[("v", v_c0a)], tag="c0b")
        E("vector", lambda g: g.tensor_scalar(out=rtmp[0:1, 9:10], in0=rtmp[0:1, 2:3],
                                              scalar1=0.3, scalar2=None, op0=ALU.mult),
          inc=("v", 1), tag="c1")
        v_c01 = M("v")
        E("tensor", lambda g: g.matmul(out=pb[7][:, 0:1], lhsT=ones_r1f[0:1, :],
                                       rhs=rtmp[0:1, 8:9], start=True, stop=True),
          waits=[("v", v_c01)], tag="c0bc")
        E("tensor", lambda g: g.matmul(out=pb[7][:, 1:2], lhsT=ones_r1f[0:1, :],
                                       rhs=rtmp[0:1, 9:10], start=True, stop=True),
          inc=("t", 1), tag="c1bc")
        t_c01 = M("t")
        E("vector", lambda g: g.tensor_copy(out=c0col[:], in_=pb[7][:, 0:1]),
          waits=[("t", t_c01)], tag="c0cp")
        E("vector", lambda g: g.tensor_copy(out=c1col[:], in_=pb[7][:, 1:2]),
          inc=("v", 1), tag="c1cp")
        v_cc = M("v")

        if KDBG:
            E("gpsimd", lambda g: g.dma_start(out=dbg_sm[:, SV + 32:SV + 40], in_=bsum[:]),
              waits=[("v", v_cc)], inc=("dd", 16), tag="dbsum_gc")
        def pfx_ld(k):
            sl = slice(k * TT, (k + 1) * TT)
            if k < 3:
                jlast = max(j for j in range(NT) if j % 7 == k % 3)
                lw = [("v", sig[("z", jlast)])]
            else:
                lw = [(f"do{(k - 3) % 3}", sig[("st", k - 3)])]
            E("sync", (lambda k_: lambda g: g.dma_start(
                out=xs2[k_ % 3][:], in_=xt_r[k_]))(k),
              waits=lw, inc=(f"dx{k % 3}", 16), tag=f"ld{k}")
            sig[("ld", k)] = M(f"dx{k % 3}")
            bw = [("v", v_cc if k == 0 else sig[("xn", k - 1)])]
            if ("bank", 4) in sig:
                bw.append(("s", max(sig[("bank", 4)], sig[("bank", 5)])))
            E("tensor", (lambda sl_: lambda g: g.matmul(
                out=pb[4][:, 0:TT], lhsT=ones_r1f[0:1, :], rhs=rows[0:1, sl_],
                start=True, stop=True))(sl), waits=bw, tag=f"bcm{k}")
            E("tensor", (lambda k_, sl_: lambda g: g.matmul(
                out=pb[5][:, 0:TT], lhsT=ones_r1f[32:33, :], rhs=rows[32:33, sl_],
                start=True, stop=True))(k, sl),
              waits=[sig[("rrow", k)]], inc=("t", 1), tag=f"bcr{k}")
            sig[("bc2", k)] = M("t")

        def pfx_av(k):
            sl = slice(k * TT, (k + 1) * TT)
            avw = [(f"dx{k % 3}", sig[("ld", k)]), ("t", sig[("bc2", k)])]
            if k >= 2:
                avw.append(("v", sig[("upd", k - 2, 2)]))
            elif k == 1:
                avw.append(("v", sig[("car", 0)]))
            E("vector", (lambda k_: lambda g: g.tensor_sub(
                out=scanb[:], in0=xs2[k_ % 3][:],
                in1=pb[4][:, 0:TT].unsqueeze(1).broadcast_to([P, NCH, TT])))(k),
              waits=avw, inc=("v", 1), tag=f"xn1_{k}")
            vm_x1 = M("v")
            E("vector", (lambda k_: lambda g: g.tensor_mul(
                out=xn[k_ % 2][:], in0=scanb[:],
                in1=pb[5][:, 0:TT].unsqueeze(1).broadcast_to([P, NCH, TT])))(k),
              waits=[("v", vm_x1)], inc=("v", 1), tag=f"xn2_{k}")
            sig[("xn", k)] = M("v")
            hw = [("t", sig[("l1", k - 2, 2, NCH - 1)])] if k >= 2 else []
            hw.append(("v", sig[("xn", k)]))
            if k == 0:
                E("vector", lambda g: g.memset(xnc[0][:, :, 0:2], 0.0),
                  waits=hw, tag="hlc0")
            else:
                E("vector", (lambda k_: lambda g: g.tensor_copy(
                    out=xnc[(k_ - 1) % 2][:, :, TT + 2:TT + 4],
                    in_=xn[k_ % 2][:, :, 0:2]))(k), waits=hw, tag=f"hrc{k - 1}")
                E("vector", (lambda k_: lambda g: g.tensor_copy(
                    out=xnc[k_ % 2][:, :, 0:2],
                    in_=xn[(k_ - 1) % 2][:, :, TT - 2:TT]))(k), tag=f"hlc{k}")
            E("vector", (lambda k_: lambda g: g.tensor_copy(
                out=xnc[k_ % 2][:, :, 2:TT + 2], in_=xn[k_ % 2][:]))(k),
              tag=f"xncc{k}")
            E("vector", (lambda k_: lambda g: g.tensor_copy(
                out=xnb[k_ % 2][:, :, 2:TT + 2], in_=xn[k_ % 2][:]))(k),
              inc=("v", 1), tag=f"xnbc{k}")
            sig[("xnb", k)] = M("v")
            if KDBG:
                E("gpsimd", (lambda k_, sl_: lambda g: g.dma_start(
                    out=dbg_xn_r[:, :, sl_], in_=xn[k_ % 2][:]))(k, sl),
                  waits=[("v", sig[("xn", k)])], inc=("dd", 16), tag=f"dxn{k}")
                E("gpsimd", (lambda k_, sl_: lambda g: g.dma_start(
                    out=dbg_sv_r[:, :, sl_], in_=svb[k_ % 2][:]))(k, sl),
                  waits=[("v", sig[("sv", k)])], inc=("dd", 16), tag=f"dsv{k}")

        def pfx_b(k):
            for c in range(NCH):
                E("vector", (lambda k_, c_: lambda g: g.tensor_tensor_scan(
                    out=scanb[:, c_, :], data0=a_s[:, c_:c_ + 1].broadcast_to([P, TT]),
                    data1=xn[k_ % 2][:, c_, :], initial=carry[:, c_:c_ + 1],
                    op0=ALU.mult, op1=ALU.add))(k, c),
                  inc=("v", 1) if c == NCH - 1 else None, tag=f"scan{k}_{c}")
            sig[("scn", k)] = M("v")
            svw = [("v", sig[("scn", k)])]
            if k == 0:
                svw.append(("t", sig[("sum", 14)]))
            elif k == 1:
                svw.append(("t", sig[("sum", 15)]))
            else:
                svw.append(("t", sig[("attnmm", k - 2, 7)]))
            E("vector", (lambda k_: lambda g: g.tensor_copy(
                out=svb[k_ % 2][:], in_=scanb[:]))(k),
              waits=svw, inc=("v", 1), tag=f"svc{k}")
            sig[("sv", k)] = M("v")
            E("vector", lambda g: g.tensor_copy(out=carry[:], in_=scanb[:, :, TT - 1]),
              inc=("v", 1), tag=f"carry{k}")
            sig[("car", k)] = M("v")

        def attn_emit(k, erange):
            for e in erange:
                bank = 2 + e % 4
                w_ = []
                if e == erange[0]:
                    w_ += [("v", sig[("sv", k)]), ("v", sig[("hal", k)]),
                           ("g", g_diag)]
                    if k == 0:
                        w_.append(("v", v_weights))
                if bank in (4, 5):
                    w_.append(("v", sig[("xn", k + 1)] if ("xn", k + 1) in sig
                               else sig[("xn", k)]))
                w_.append(("s", sig.get(("bank", bank), s_gf)))
                for c in range(NCH):
                    E("tensor", (lambda k_, e_, c_, b_: lambda g: g.matmul(
                        out=pb[b_][:, 0:TT], lhsT=wtsb[:, c_, e_ * P:(e_ + 1) * P],
                        rhs=svb[k_ % 2][:, c_, :], start=(c_ == 0), stop=False))(k, e, c, bank),
                      waits=w_ if c == 0 else (), tag=f"aW{k}_{e}_{c}")
                for kk in range(5):
                    E("tensor", (lambda k_, e_, kk_, b_: lambda g: g.matmul(
                        out=pb[b_][:, 0:TT], lhsT=diag[:, e_ * 5 + kk_, :],
                        rhs=xnc[k_ % 2][:, e_, kk_:kk_ + TT],
                        start=False, stop=(kk_ == 4)))(k, e, kk, bank),
                      inc=("t", 1) if kk == 4 else None, tag=f"aC{k}_{e}_{kk}")
                sig[("attnmm", k, e)] = M("t")
                E("scalar", (lambda k_, e_, b_: lambda g: g.activation(
                    out=shalf[k_ % 2][:, e_, :], in_=pb[b_][:, 0:TT],
                    func=AF.Identity,
                    scale=1.0, bias=bsum[:, e_:e_ + 1]))(k, e, bank),
                  waits=([("t", sig[("attnmm", k, e)])]
                         + ([("v", sig[("fin", k - 2)])] if (e == 0 and k >= 2) else [])),
                  inc=("s", 1), tag=f"evY{k}_{e}")
                sig[("evY", k, e)] = M("s")
                sig[("bank", bank)] = M("s")
            if KDBG and erange[-1] == 7:
                sl_d = slice(k * TT, (k + 1) * TT)
                E("gpsimd", (lambda k_, sl_: lambda g: g.dma_start(
                    out=dbg_s_r[:, :, sl_], in_=shalf[k_ % 2][:]))(k, sl_d),
                  waits=[("s", sig[("evY", k, 7)])], inc=("dd", 16), tag=f"ds{k}")

        def lchain(k, c):
            for m in range(NCH):
                bank = m % 2
                w_ = []
                if m == 0:
                    if c == 0:
                        w_.append(("v", sig[("xnb", k)]))
                        if k == 0:
                            w_.append(("v", v_weights))
                    if c > 0:
                        w_.append(("s", sig[("l3e", k, c - 1)]))
                    elif k > 0:
                        w_.append(("s", sig[("l3e", k - 1, 2)]))
                if m >= 2:
                    w_.append(("s", sig[("l1e", k, c, m - 2)]))
                for cc in range(NCH):
                    ccw = list(w_) if cc == 0 else []
                    if c > 0 and m == 0 and cc in (0, 2, 6):
                        ccw.append(sig[("inbp", k, c, cc)])
                    E("tensor", (lambda k_, m_, cc_, b_: lambda g: g.matmul(
                        out=pb[b_][:, 0:TT], lhsT=w1b[:, cc_, m_ * P:(m_ + 1) * P],
                        rhs=xnb[k_ % 2][:, cc_, 2:TT + 2], start=(cc_ == 0),
                        stop=(cc_ == NCH - 1)))(k, m, cc, bank),
                      waits=ccw,
                      inc=("t", 1) if cc == NCH - 1 else None, tag=f"L1_{k}_{c}_{m}_{cc}")
                sig[("l1", k, c, m)] = M("t")
                E("scalar", (lambda m_, b_: lambda g: g.activation(
                    out=h1b[:, m_, :], in_=pb[b_][:, 0:TT], func=AF.Gelu,
                    bias=svec[:, SV_B1 + m_:SV_B1 + m_ + 1]))(m, bank),
                  waits=[("t", sig[("l1", k, c, m)])],
                  inc=("s", 1), tag=f"l1e{k}_{c}_{m}")
                sig[("l1e", k, c, m)] = M("s")
            for m in range(4):
                bank = m % 2
                w_ = []
                if m >= 2:
                    w_.append(("s", sig[("l2e", k, c, m - 2)]))
                for cc in range(NCH):
                    ccw = list(w_) if cc == 0 else []
                    if m == 0 and cc == 0:
                        ccw.append(("s", sig[("l1e", k, c, NCH - 2)]))
                    if m == 0 and cc == NCH - 1:
                        ccw.append(("s", sig[("l1e", k, c, NCH - 1)]))
                    E("tensor", (lambda m_, cc_, b_: lambda g: g.matmul(
                        out=pb[b_][:, 0:TT], lhsT=w2b[:, cc_, m_ * P:(m_ + 1) * P],
                        rhs=h1b[:, cc_, :], start=(cc_ == 0),
                        stop=(cc_ == NCH - 1)))(m, cc, bank),
                      waits=ccw,
                      inc=("t", 1) if cc == NCH - 1 else None, tag=f"L2_{k}_{c}_{m}")
                sig[("l2", k, c, m)] = M("t")
                E("scalar", (lambda m_, b_: lambda g: g.activation(
                    out=h2b[:, m_, :], in_=pb[b_][:, 0:TT], func=AF.Gelu,
                    bias=svec[:, SV_B2 + m_:SV_B2 + m_ + 1]))(m, bank),
                  waits=[("t", sig[("l2", k, c, m)])],
                  inc=("s", 1), tag=f"l2e{k}_{c}_{m}")
                sig[("l2e", k, c, m)] = M("s")
            for m in range(2):
                bank = m % 2
                for cc in range(4):
                    ccw = []
                    if m == 0 and cc == 0:
                        ccw.append(("s", sig[("l2e", k, c, 2)]))
                    if m == 0 and cc == 3:
                        ccw.append(("s", sig[("l2e", k, c, 3)]))
                    E("tensor", (lambda m_, cc_, b_: lambda g: g.matmul(
                        out=pb[b_][:, 0:TT], lhsT=w3b[:, cc_, m_ * P:(m_ + 1) * P],
                        rhs=h2b[:, cc_, :], start=(cc_ == 0), stop=(cc_ == 3)))(m, cc, bank),
                      waits=ccw,
                      inc=("t", 1) if cc == 3 else None, tag=f"L3_{k}_{c}_{m}")
                sig[("l3", k, c, m)] = M("t")
                E("scalar", (lambda m_, b_: lambda g: g.activation(
                    out=h3b[:, m_, :], in_=pb[b_][:, 0:TT], func=AF.Gelu,
                    bias=svec[:, SV_B3 + m_:SV_B3 + m_ + 1]))(m, bank),
                  waits=[("t", sig[("l3", k, c, m)])],
                  inc=("s", 1), tag=f"l3e{k}_{c}_{m}")
                sig[("l3ep", k, c, m)] = M("s")
            sig[("l3e", k, c)] = M("s")
            if c > 0:
                t4w = [("s", sig[("t4", k, c - 1)])]
            elif k > 0:
                t4w = [("s", sig[("t4", k - 1, 2)])]
            else:
                t4w = []
            for cc in range(2):
                ccw = (list(t4w) if cc == 0 else []) + [("s", sig[("l3ep", k, c, cc)])]
                E("tensor", (lambda cc_: lambda g: g.matmul(
                    out=pb[6][0:1, 0:TT], lhsT=w4b[:, cc_, :], rhs=h3b[:, cc_, :],
                    start=(cc_ == 0), stop=(cc_ == 1)))(cc),
                  waits=ccw,
                  inc=("t", 1) if cc == 1 else None, tag=f"L4_{k}_{c}")
            sig[("l4", k, c)] = M("t")
            E("scalar", lambda g: g.activation(
                out=t4b[:], in_=pb[6][0:1, 0:TT], func=AF.Tanh, scale=0.5,
                bias=rtmp[0:1, 4:5]),
              waits=[("t", sig[("l4", k, c)])], inc=("s", 1), tag=f"t4_{k}_{c}")
            sig[("t4", k, c)] = M("s")

        def OB(k, c):
            return of if (k + c) % 2 == 0 else ofB

        def cpl_upd(k, c):
            w_ = [("s", sig[("t4", k, c)])]
            if c > 0:
                w_.append(("v", sig[("upd", k, c - 1)]))
            elif k > 0:
                w_.append(("v", sig[("upd", k - 1, 2)]))
            else:
                w_.append(("v", v_cc))
            E("tensor", lambda g: g.matmul(
                out=pb[7][:, 0:TT], lhsT=ones_r1b[:], rhs=t4b[:],
                start=True, stop=True), waits=w_, inc=("t", 1), tag=f"cbc{k}_{c}")
            sig[("cbc", k, c)] = M("t")
            E("vector", lambda g: g.tensor_scalar(
                out=cplb[:], in0=pb[7][:, 0:TT], scalar1=c1col[:], scalar2=c0col[:],
                op0=ALU.mult, op1=ALU.add),
              waits=[("t", sig[("cbc", k, c)])], inc=("v", 1), tag=f"cpl{k}_{c}")
            sig[("cpl", k, c)] = M("v")
            st = xn[k % 2] if c == 0 else OB(k, c - 1)
            ob = OB(k, c)
            cb2 = cplb[:].unsqueeze(1).broadcast_to([P, 2, TT])
            cb4 = cplb[:].unsqueeze(1).broadcast_to([P, 4, TT])
            E("vector", (lambda st_, ob_: lambda g: g.tensor_tensor(
                out=ob_[:, 0:2, :], in0=cb2, in1=st_[:, 4:6, :],
                op=ALU.mult))(st, ob),
              waits=[("v", sig[("cpl", k, c)])], tag=f"twA{k}_{c}")
            E("vector", (lambda st_, ob_: lambda g: g.tensor_tensor(
                out=ob_[:, 2:4, :], in0=cb2, in1=st_[:, 6:8, :],
                op=ALU.mult))(st, ob), tag=f"twA2{k}_{c}")
            E("vector", (lambda st_, ob_: lambda g: g.tensor_tensor(
                out=ob_[:, 4:8, :], in0=cb4, in1=st_[:, 0:4, :],
                op=ALU.mult))(st, ob), inc=("v", 1), tag=f"twB{k}_{c}")
            vm_tw = M("v")
            E("vector", (lambda st_, ob_: lambda g: g.tensor_add(
                out=ob_[:, 0:2, :], in0=st_[:, 0:2, :], in1=ob_[:, 0:2, :]))(st, ob),
              waits=[("v", vm_tw)], inc=("v", 1), tag=f"u1{k}_{c}")
            sig[("u1", k, c)] = M("v")
            E("vector", (lambda st_, ob_: lambda g: g.tensor_sub(
                out=ob_[:, 2:6, :], in0=st_[:, 2:6, :], in1=ob_[:, 2:6, :]))(st, ob),
              inc=("v", 1), tag=f"u2{k}_{c}")
            sig[("u2", k, c)] = M("v")
            E("vector", (lambda st_, ob_: lambda g: g.tensor_add(
                out=ob_[:, 6:8, :], in0=st_[:, 6:8, :], in1=ob_[:, 6:8, :]))(st, ob),
              inc=("v", 1), tag=f"u3{k}_{c}")
            sig[("upd", k, c)] = M("v")

        def inb_emit(k, c):
            tmark = sig[("l1", k, c - 1, NCH - 1)]
            src = OB(k, c - 1)
            for (lo, hi) in [(0, 2), (2, 6), (6, 8)]:
                E("vector", (lambda k_, lo_, hi_, s_: lambda g: g.tensor_copy(
                    out=xnb[k_ % 2][:, lo_:hi_, 2:TT + 2],
                    in_=s_[:, lo_:hi_, :]))(k, lo, hi, src),
                  waits=([("t", tmark), ("v", sig[("upd", k, c - 1)])]
                         if lo == 0 else ()), inc=("v", 1),
                  tag=f"inb{k}_{c}_{lo}")
                sig[("inbp", k, c, lo)] = ("v", M("v"))
            sig[("inb", k, c)] = M("v")

        def tail(k):
            sl = slice(k * TT, (k + 1) * TT)
            fb = OB(k, 2)
            if KDBG:
                E("gpsimd", (lambda k_, sl_, f_: lambda g: g.dma_start(
                    out=dbg_of_r[:, :, sl_], in_=f_[:]))(k, sl, fb),
                  waits=[("v", sig[("upd", k, 2)])], inc=("dd", 16), tag=f"dof{k}")
            tgw = [("v", sig[("upd", k, 2)])]
            if ("xn", k + 2) in sig:
                tgw.append(("v", sig[("xn", k + 2)]))
            E("scalar", (lambda f_: lambda g: g.activation(
                out=scanb[:], in_=f_[:], func=AF.Tanh, scale=0.5))(fb),
              waits=tgw, inc=("s", 1), tag=f"tg{k}")
            sig[("tg", k)] = M("s")
            E("vector", (lambda f_: lambda g: g.scalar_tensor_tensor(
                out=scanb[:], in0=scanb[:], scalar=1.0, in1=f_[:],
                op0=ALU.add, op1=ALU.mult))(fb),
              waits=[("s", sig[("tg", k)])], inc=("v", 1), tag=f"g2{k}")
            vm_g2 = M("v")
            E("vector", (lambda k_: lambda g: g.tensor_mul(
                out=scanb[:], in0=scanb[:], in1=shalf[k_ % 2][:]))(k),
              waits=[("v", vm_g2)], inc=("v", 1), tag=f"pr{k}")
            vm_pr = M("v")
            E("vector", (lambda k_: lambda g: g.scalar_tensor_tensor(
                out=xs2[k_ % 3][:], in0=scanb[:], scalar=0.5, in1=xs2[k_ % 3][:],
                op0=ALU.mult, op1=ALU.add))(k),
              waits=[("v", vm_pr)], inc=("v", 1), tag=f"fin{k}")
            sig[("fin", k)] = M("v")
            E("gpsimd", (lambda k_: lambda g: g.dma_start(
                out=yt_r[k_], in_=xs2[k_ % 3][:]))(k),
              waits=[("v", sig[("fin", k)])], inc=(f"do{k % 3}", 16), tag=f"st{k}")
            sig[("st", k)] = M(f"do{k % 3}")

        pfx_ld(0)
        pfx_av(0)
        pfx_b(0)
        for k in range(1, NT + 1):
            if k <= NT - 1:
                pfx_ld(k)
                pfx_av(k)
                sig[("hal", k - 1)] = sig[("xnb", k)]
            if k > NT - 1:
                E("vector", lambda g: g.memset(
                    xnc[(NT - 1) % 2][:, :, TT + 2:TT + 4], 0.0),
                  inc=("v", 1), tag="hrlastc")
                sig[("hal", NT - 1)] = M("v")
            kk = k - 1
            lchain(kk, 0)
            cpl_upd(kk, 0)
            inb_emit(kk, 1)
            if kk >= 1:
                tail(kk - 1)
            attn_emit(kk, [0, 1, 2, 3])
            lchain(kk, 1)
            cpl_upd(kk, 1)
            inb_emit(kk, 2)
            if k <= NT - 1:
                pfx_b(k)
            attn_emit(kk, [4, 5, 6, 7])
            lchain(kk, 2)
            cpl_upd(kk, 2)
        tail(NT - 1)

        if KDBG:
            E("gpsimd", lambda g: g.dma_start(
                out=dbg_w.rearrange("p (c m) -> p c m", c=NCH), in_=wtsb[:]),
              waits=[("s", M("s")), ("v", M("v"))], inc=("dd", 16), tag="dw")
            E("gpsimd", lambda g: g.dma_start(
                out=dbg_dg.rearrange("p (c m) -> p c m", c=40), in_=diag[:]),
              inc=("dd", 16), tag="ddg")
            E("gpsimd", lambda g: g.dma_start(out=dbg_sm[:, 0:SV], in_=svec[:]),
              waits=[("s", M("s")), ("v", M("v"))], inc=("dd", 16), tag="dsm1")
            E("gpsimd", lambda g: g.dma_start(out=dbg_sm[:, SV:SV + 8], in_=bsum[:]),
              inc=("dd", 16), tag="dsm2")
            E("gpsimd", lambda g: g.dma_start(out=dbg_sm[:, SV + 8:SV + 16], in_=a_s[:]),
              inc=("dd", 16), tag="dsm3")
        E("gpsimd", lambda g: g.engine_nop(),
          waits=[("do0", M("do0")), ("do1", M("do1")), ("do2", M("do2")),
                 ("dd", M("dd"))],
          tag="final_wait")

        sch.simulate()

        def _runner(name):
            def _body(eng):
                for fn, waits, inc, tag in sch.ops[name]:
                    for s_, v_ in waits:
                        eng.wait_ge(sems[s_], v_)
                    inst = fn(eng)
                    if inc is not None:
                        inst.then_inc(sems[inc[0]], inc[1])
            return _body

        block.sync(_runner("sync"))
        block.gpsimd(_runner("gpsimd"))
        block.vector(_runner("vector"))
        block.scalar(_runner("scalar"))
        block.tensor(_runner("tensor"))

    return nc


def kernel(**inputs):
    x = np.asarray(inputs["L0_attn_x"], dtype=np.float32)
    W = np.asarray(inputs["L0_attn_W"], dtype=np.float32)
    svec = np.zeros((P, SV), dtype=np.float32)

    def col_layout(v, nch):
        return np.ascontiguousarray(np.asarray(v, dtype=np.float32).reshape(nch, P).T)

    svec[:, SV_DL:SV_DL + 8] = col_layout(inputs["L0_attn_decay_logit"], 8)
    svec[:, SV_BATT:SV_BATT + 8] = col_layout(inputs["L0_attn_b"], 8)
    svec[:, SV_CONVB:SV_CONVB + 8] = col_layout(inputs["L0_attn_conv_b"], 8)
    svec[:, SV_B1:SV_B1 + 8] = col_layout(inputs["L0_ffn_cn_b1"], 8)
    svec[:, SV_B2:SV_B2 + 4] = col_layout(inputs["L0_ffn_cn_b2"], 4)
    svec[:, SV_B3:SV_B3 + 2] = col_layout(inputs["L0_ffn_cn_b3"], 2)
    svec[:, SV_GB1:SV_GB1 + 4] = col_layout(inputs["L0_ffn_gc_b1"], 4)
    svec[:, SV_GB2:SV_GB2 + 2] = col_layout(inputs["L0_ffn_gc_b2"], 2)
    cw = np.asarray(inputs["L0_attn_conv_w"], np.float32).reshape(D, 5)
    svec[:, SV_CW:SV_CW + 40] = cw.reshape(NCH, P, 5).transpose(1, 0, 2).reshape(P, 40)
    srow = np.zeros((1, 4), dtype=np.float32)
    srow[0, 0] = np.asarray(inputs["L0_ffn_cn_b4"], np.float32).reshape(())
    srow[0, 1] = np.asarray(inputs["L0_ffn_gc_b3"], np.float32).reshape(())
    srow[0, 2] = np.asarray(inputs["L0_ffn_adaptive_range"], np.float32).reshape(())

    common = {
        "w_ret_t": np.ascontiguousarray(W.T),
        "cn_w1": np.ascontiguousarray(np.asarray(inputs["L0_ffn_cn_w1"], np.float32)),
        "cn_w2": np.ascontiguousarray(np.asarray(inputs["L0_ffn_cn_w2"], np.float32)),
        "cn_w3": np.ascontiguousarray(np.asarray(inputs["L0_ffn_cn_w3"], np.float32)),
        "cn_w4": np.ascontiguousarray(np.asarray(inputs["L0_ffn_cn_w4"], np.float32).reshape(2, P).T),
        "gc_w1": np.ascontiguousarray(np.asarray(inputs["L0_ffn_gc_w1"], np.float32)),
        "gc_w2": np.ascontiguousarray(np.asarray(inputs["L0_ffn_gc_w2"], np.float32)),
        "gc_w3": np.ascontiguousarray(np.asarray(inputs["L0_ffn_gc_w3"], np.float32).reshape(2, P).T),
        "svec": svec,
        "srow": srow,
    }
    in_maps = []
    for b in range(N_CORES):
        m = dict(common)
        m["x_t"] = np.ascontiguousarray(
            x[b].T.reshape(NCH, P, NT, TT).transpose(2, 1, 0, 3).reshape(NT, P, NCH * TT))
        in_maps.append(m)

    nc = build_nc()
    res = run_bass_kernel_spmd(nc, in_maps, core_ids=list(range(N_CORES)))
    outs = []
    for r in res.results:
        yt = np.asarray(r["y_t"]).reshape(NT, P, NCH, TT).transpose(2, 1, 0, 3)
        outs.append(yt.reshape(D, T).T)
    out = np.stack(outs, axis=0)
    return np.ascontiguousarray(out).astype(np.float32)


# version 32
# speedup vs baseline: 1.0271x; 1.0077x over previous
import numpy as np
from contextlib import ExitStack

import concourse.bass as bass
import concourse.mybir as mybir
from concourse.bass_utils import run_bass_kernel_spmd

F32 = mybir.dt.float32
BF16 = mybir.dt.bfloat16
AF = mybir.ActivationFunctionType
ALU = mybir.AluOpType

P = 128
D = 1024
NCH = D // P
T = 4096
TT = 256
NT = T // TT
EPS = 1e-5
N_CORES = 8

SV_DL = 0
SV_BATT = 8
SV_CONVB = 16
SV_B1 = 24
SV_B2 = 32
SV_B3 = 36
SV_GB1 = 38
SV_GB2 = 42
SV_CW = 44
SV = 84


class Sched:
    ENGINES = ("sync", "gpsimd", "vector", "scalar", "tensor")

    def __init__(self):
        self.ops = {e: [] for e in self.ENGINES}
        self.cnt = {s: 0 for s in ("di", "dw0", "dw1", "dx0", "dx1", "dx2", "dx3", "dx4", "dx5", "dx6",
                           "dr", "do0", "do1", "do2", "dd", "t", "v", "s", "g")}

    def emit(self, eng, fn, waits=(), inc=None, tag=""):
        waits = [(s, v) for (s, v) in waits if v > 0]
        self.ops[eng].append((fn, waits, inc, tag))
        if inc is not None:
            self.cnt[inc[0]] += inc[1]

    def mark(self, sem):
        return self.cnt[sem]

    def simulate(self):
        pos = {e: 0 for e in self.ENGINES}
        val = {s: 0 for s in self.cnt}
        progress = True
        while progress:
            progress = False
            for e in self.ENGINES:
                while pos[e] < len(self.ops[e]):
                    fn, waits, inc, tag = self.ops[e][pos[e]]
                    if all(val[s] >= v for (s, v) in waits):
                        if inc:
                            val[inc[0]] += inc[1]
                        pos[e] += 1
                        progress = True
                    else:
                        break
        stuck = {e: pos[e] for e in self.ENGINES if pos[e] < len(self.ops[e])}
        if stuck:
            lines = []
            for e, p_ in stuck.items():
                fn, waits, inc, tag = self.ops[e][p_]
                lines.append(f"  {e}[{p_}] tag={tag} waits={waits} "
                             f"(now={[(s, val[s]) for s, _ in waits]})")
            raise RuntimeError("DEADLOCK in schedule:\n" + "\n".join(lines))


def build_nc():
    nc = bass.Bass("TRN2", target_bir_lowering=False)

    xt = nc.dram_tensor("x_t", [NT, P, NCH * TT], F32, kind="ExternalInput").ap()
    wts_h = nc.dram_tensor("w_ret_t", [D, D], F32, kind="ExternalInput").ap()
    w1_h = nc.dram_tensor("cn_w1", [D, D], F32, kind="ExternalInput").ap()
    w2_h = nc.dram_tensor("cn_w2", [D, 512], F32, kind="ExternalInput").ap()
    w3_h = nc.dram_tensor("cn_w3", [512, 256], F32, kind="ExternalInput").ap()
    w4_h = nc.dram_tensor("cn_w4", [P, 2], F32, kind="ExternalInput").ap()
    g1_h = nc.dram_tensor("gc_w1", [D, 512], F32, kind="ExternalInput").ap()
    g2_h = nc.dram_tensor("gc_w2", [512, 256], F32, kind="ExternalInput").ap()
    g3_h = nc.dram_tensor("gc_w3", [P, 2], F32, kind="ExternalInput").ap()
    sv_h = nc.dram_tensor("svec", [P, SV], F32, kind="ExternalInput").ap()
    sr_h = nc.dram_tensor("srow", [1, 4], F32, kind="ExternalInput").ap()
    yt = nc.dram_tensor("y_t", [NT, P, NCH * TT], F32, kind="ExternalOutput").ap()

    import os
    KDBG = bool(int(os.environ.get("KDBG", "0")))
    if KDBG:
        dbg_xn = nc.dram_tensor("dbg_xn", [D, T], F32, kind="ExternalOutput").ap()
        dbg_sv = nc.dram_tensor("dbg_sv", [D, T], BF16, kind="ExternalOutput").ap()
        dbg_s = nc.dram_tensor("dbg_s", [D, T], F32, kind="ExternalOutput").ap()
        dbg_of = nc.dram_tensor("dbg_of", [D, T], F32, kind="ExternalOutput").ap()
        dbg_xn_r = dbg_xn.rearrange("(c p) t -> p c t", p=P)
        dbg_sv_r = dbg_sv.rearrange("(c p) t -> p c t", p=P)
        dbg_s_r = dbg_s.rearrange("(c p) t -> p c t", p=P)
        dbg_of_r = dbg_of.rearrange("(c p) t -> p c t", p=P)
        dbg_sm = nc.dram_tensor("dbg_sm", [P, SV + 48], F32, kind="ExternalOutput").ap()
        dbg_w = nc.dram_tensor("dbg_w", [P, NCH * D], BF16, kind="ExternalOutput").ap()
        dbg_w2 = nc.dram_tensor("dbg_w2", [P, NCH * D], BF16, kind="ExternalOutput").ap()
        dbg_dg = nc.dram_tensor("dbg_dg", [P, 40 * P], BF16, kind="ExternalOutput").ap()

    xt_r = xt.rearrange("k p (c t) -> k p c t", c=NCH)
    yt_r = yt.rearrange("k p (c t) -> k p c t", c=NCH)

    sch = Sched()
    E = sch.emit
    M = sch.mark
    sig = {}

    with ExitStack() as ctx:
        e_ = ctx.enter_context
        wtsb = e_(nc.sbuf_tensor("wtsb", [P, NCH, D], BF16))
        w1b = e_(nc.sbuf_tensor("w1b", [P, NCH, D], BF16))
        w2b = e_(nc.sbuf_tensor("w2b", [P, NCH, 512], BF16))
        w3b = e_(nc.sbuf_tensor("w3b", [P, 4, 256], BF16))
        w4b = e_(nc.sbuf_tensor("w4b", [P, 2, 1], BF16))
        gw1b = e_(nc.sbuf_tensor("gw1b", [P, NCH, 512], BF16))
        gw2b = e_(nc.sbuf_tensor("gw2b", [P, 4, 256], BF16))
        gw3b = e_(nc.sbuf_tensor("gw3b", [P, 2, 1], BF16))
        diag = e_(nc.sbuf_tensor("diag", [P, 40, P], BF16))
        ones_sq_b = e_(nc.sbuf_tensor("ones_sq_b", [P, P], BF16))
        ones_r1b = e_(nc.sbuf_tensor("ones_r1b", [1, P], BF16))
        ones_r1f = e_(nc.sbuf_tensor("ones_r1f", [33, P], F32))
        svec = e_(nc.sbuf_tensor("svec_s", [P, SV], F32))
        a_t = e_(nc.sbuf_tensor("a_t", [P, NCH], F32))
        a_s = e_(nc.sbuf_tensor("a_s", [P, NCH], F32))
        oma = e_(nc.sbuf_tensor("oma", [P, NCH], F32))
        bsum = e_(nc.sbuf_tensor("bsum", [P, NCH], F32))
        c0col = e_(nc.sbuf_tensor("c0col", [P, 1], F32))
        c1col = e_(nc.sbuf_tensor("c1col", [P, 1], F32))
        carry = e_(nc.sbuf_tensor("carry", [P, NCH], F32))
        zacc = e_(nc.sbuf_tensor("zacc", [P, NCH * NT], F32))
        gcv = e_(nc.sbuf_tensor("gcv", [P, NCH], F32))
        gcvb = e_(nc.sbuf_tensor("gcvb", [P, NCH], BF16))
        gch1b = e_(nc.sbuf_tensor("gch1b", [P, 4], BF16))
        gch2b = e_(nc.sbuf_tensor("gch2b", [P, 2], BF16))
        csum = e_(nc.sbuf_tensor("csum", [1, NT], F32))
        rows = e_(nc.sbuf_tensor("rows", [33, T], F32))
        rs1 = e_(nc.sbuf_tensor("rs1", [1, 4 * TT], F32))
        rs2 = e_(nc.sbuf_tensor("rs2", [1, 4 * TT], F32))
        rtmp = e_(nc.sbuf_tensor("rtmp", [1, 16], F32))
        t4b = e_(nc.sbuf_tensor("t4b", [1, TT], BF16))
        xs2 = [e_(nc.sbuf_tensor(f"xs2_{i}", [P, NCH, TT], F32)) for i in range(3)]
        xn = [e_(nc.sbuf_tensor(f"xn_{i}", [P, NCH, TT], F32)) for i in range(2)]
        xnb = [e_(nc.sbuf_tensor(f"xnb_{i}", [P, NCH, TT + 4], BF16)) for i in range(2)]
        xnc = [e_(nc.sbuf_tensor(f"xnc_{i}", [P, NCH, TT + 4], BF16)) for i in range(2)]
        scanb = e_(nc.sbuf_tensor("scanb", [P, NCH, TT], F32))
        svb = [e_(nc.sbuf_tensor(f"svb_{i}", [P, NCH, TT], BF16)) for i in range(2)]
        shalf = [e_(nc.sbuf_tensor(f"shalf_{i}", [P, NCH, TT], F32)) for i in range(2)]
        of = e_(nc.sbuf_tensor("of", [P, NCH, TT], F32))
        ofB = e_(nc.sbuf_tensor("ofB", [P, NCH, TT], F32))
        h1b = e_(nc.sbuf_tensor("h1b", [P, NCH, TT], BF16))
        h2b = e_(nc.sbuf_tensor("h2b", [P, 4, TT], BF16))
        h3b = e_(nc.sbuf_tensor("h3b", [P, 2, TT], BF16))
        cplb = e_(nc.sbuf_tensor("cplb", [P, TT], F32))
        pb = [e_(nc.psum_tensor(f"pb{i}", [P, 512], F32)) for i in range(8)]
        sems = {n: e_(nc.semaphore(n)) for n in
                ("di", "dw0", "dw1", "dx0", "dx1", "dx2", "dx3", "dx4", "dx5", "dx6",
                 "dr", "do0", "do1", "do2", "dd", "t", "v", "s", "g")}
        block = e_(nc.Block())

        E("sync", lambda g: g.dma_start(out=svec[:], in_=sv_h), inc=("di", 16), tag="ld_svec")
        E("sync", lambda g: g.dma_start(out=rtmp[0:1, 0:4], in_=sr_h), inc=("di", 16), tag="ld_srow")
        E("sync", lambda g: g.dma_start(out=cplb[:, 0:2], in_=w4_h),
          inc=("di", 16), tag="ld_w4")
        E("sync", lambda g: g.dma_start(out=cplb[:, 2:4], in_=g3_h),
          inc=("di", 16), tag="ld_g3")
        di_small = M("di")

        E("scalar", lambda g: g.activation(out=a_t[:], in_=svec[:, SV_DL:SV_DL + 8],
                                           func=AF.Tanh, scale=0.5),
          waits=[("di", di_small)], inc=("s", 1), tag="a_tanh")
        s_at = M("s")
        E("vector", lambda g: g.tensor_scalar(out=a_s[:], in0=a_t[:], scalar1=0.5,
                                              scalar2=0.5, op0=ALU.mult, op1=ALU.add),
          waits=[("s", s_at)], tag="a_s")
        E("vector", lambda g: g.tensor_scalar(out=oma[:], in0=a_t[:], scalar1=-0.5,
                                              scalar2=0.5, op0=ALU.mult, op1=ALU.add),
          tag="oma")
        E("vector", lambda g: g.scalar_tensor_tensor(
            out=bsum[:], in0=svec[:, SV_BATT:SV_BATT + 8], scalar=1.0,
            in1=svec[:, SV_CONVB:SV_CONVB + 8], op0=ALU.mult, op1=ALU.add),
          waits=[("di", di_small)], tag="bsum0")
        E("vector", lambda g: g.memset(ones_sq_b[:], 1.0 / D), tag="ones1")
        E("vector", lambda g: g.memset(ones_r1b[:], 1.0), tag="ones2")
        E("vector", lambda g: g.memset(ones_r1f[:], 1.0), tag="ones3")
        E("vector", lambda g: g.memset(carry[:], 0.0), tag="carry0")
        E("vector", lambda g: g.tensor_scalar(out=rtmp[0:1, 4:5], in0=rtmp[0:1, 0:1],
                                              scalar1=0.5, scalar2=None, op0=ALU.mult),
          waits=[("di", di_small)], tag="b4h")
        E("vector", lambda g: g.tensor_scalar(out=rtmp[0:1, 5:6], in0=rtmp[0:1, 1:2],
                                              scalar1=0.5, scalar2=None, op0=ALU.mult),
          inc=("v", 1), tag="gb3h")
        v_prep = M("v")
        if KDBG:
            E("gpsimd", lambda g: g.dma_start(out=dbg_sm[:, SV + 16:SV + 24], in_=bsum[:]),
              waits=[("v", v_prep)], inc=("dd", 16), tag="dbsum_prep")

        for c in range(NCH):
            for k in range(5):
                idx = c * 5 + k
                col = SV_CW + idx
                E("gpsimd",
                  (lambda col_, idx_: lambda g: g.affine_select(
                      out=diag[:, idx_, :],
                      in_=svec[:, col_:col_ + 1].broadcast_to([P, P]),
                      pattern=[[1, P]], compare_op=ALU.is_equal, fill=0.0,
                      base=0, channel_multiplier=-1))(col, idx),
                  waits=[("di", di_small)] if idx == 0 else (),
                  inc=("g", 1) if idx == 39 else None, tag=f"diag{idx}")
        g_diag = M("g")

        E("vector", lambda g: g.tensor_copy(out=w4b[:, :, 0], in_=cplb[:, 0:2]),
          waits=[("di", di_small)], tag="cast_w4")
        E("vector", lambda g: g.tensor_copy(out=gw3b[:, :, 0], in_=cplb[:, 2:4]),
          inc=("v", 1), tag="cast_g3")
        tiny_cast = M("v")

        for j0 in range(5):
            E("sync", (lambda j_: lambda g: g.dma_start(
                out=(xs2 + shalf + xn)[j_][:], in_=xt_r[j_]))(j0),
              inc=(f"dx{j0 % 7}", 16), tag=f"p1ld{j0}")
            sig[("ld1", j0)] = M(f"dx{j0 % 7}")

        wstage = [of, scanb]
        wjobs = [
            (wts_h.rearrange("(c p) m -> p c m", p=P), wtsb, 8, D, True),
            (w1_h.rearrange("(c p) m -> p c m", p=P), w1b, 8, D, False),
            (w2_h.rearrange("(c p) m -> p c m", p=P), w2b, 8, 512, False),
            (w3_h.rearrange("(c p) m -> p c m", p=P), w3b, 4, 256, False),
            (g1_h.rearrange("(c p) m -> p c m", p=P), gw1b, 8, 512, False),
            (g2_h.rearrange("(c p) m -> p c m", p=P), gw2b, 4, 256, False),
        ]
        segs = []
        for (srcw, dst, nch_w, cols, scaled) in wjobs:
            step = max(1, 2048 // cols)
            for c0 in range(0, nch_w, step):
                segs.append((srcw, dst, c0, min(step, nch_w - c0), cols, scaled))
        wl = 0
        last_cast = [tiny_cast] * len(wstage)
        for (srcw, dst, c0, nch_s, cols, scaled) in segs:
            h = wl % len(wstage)
            stg = wstage[h][:].rearrange("p a t -> p (a t)")
            E("sync",
              (lambda srcw_, c0_, n_, cols_, stg_:
               lambda g: g.dma_start(
                   out=stg_[:, 0:n_ * cols_].rearrange("p (c m) -> p c m", c=n_),
                   in_=srcw_[:, c0_:c0_ + n_, :]))(srcw, c0, nch_s, cols, stg),
              waits=[("v", last_cast[h])], inc=(f"dw{wl % 2}", 16), tag=f"wld{wl}")
            di_w = M(f"dw{wl % 2}")
            if scaled:
                for ci in range(nch_s):
                    E("vector", (lambda dst_, c0_, ci_, cols_, stg_:
                                 lambda g: g.tensor_scalar(
                                     out=dst_[:, c0_ + ci_, :],
                                     in0=stg_[:, ci_ * cols_:(ci_ + 1) * cols_],
                                     scalar1=oma[:, c0_ + ci_:c0_ + ci_ + 1],
                                     scalar2=None, op0=ALU.mult))(dst, c0, ci, cols, stg),
                      waits=[(f"dw{wl % 2}", di_w), ("v", v_prep)] if ci == 0 else [],
                      inc=("v", 1) if ci == nch_s - 1 else None, tag=f"wcast{wl}_{ci}")
            else:
                E("vector", (lambda dst_, c0_, n_, cols_, stg_:
                             lambda g: g.tensor_copy(
                                 out=dst_[:, c0_:c0_ + n_, :],
                                 in_=stg_[:, 0:n_ * cols_].rearrange(
                                     "p (c m) -> p c m", c=n_)))(dst, c0, nch_s, cols, stg),
                  waits=[(f"dw{wl % 2}", di_w)], inc=("v", 1), tag=f"wcast{wl}")
            last_cast[h] = M("v")
            wl += 1
        v_weights = M("v")
        if KDBG:
            E("gpsimd", lambda g: g.dma_start(
                out=dbg_w2.rearrange("p (c m) -> p c m", c=NCH), in_=wtsb[:]),
              waits=[("v", v_weights)], inc=("dd", 16), tag="dw2")

        p1x = xs2 + shalf + xn
        NX1 = len(p1x)
        for j in range(NT + 3):
            if j < NT:
                if j >= 5:
                    lw = [("v", sig[("z", j - NX1)])] if j >= NX1 else []
                    E("sync", (lambda j_: lambda g: g.dma_start(
                        out=p1x[j_ % NX1][:], in_=xt_r[j_]))(j),
                      waits=lw, inc=(f"dx{j % 7}", 16), tag=f"p1ld{j}")
                    sig[("ld1", j)] = M(f"dx{j % 7}")
                cw = [(f"dx{j % 7}", sig[("ld1", j)])]
                if j >= 2:
                    cw.append(("t", sig[("sum", j - 2)]))
                E("scalar", (lambda j_: lambda g: g.activation(
                    out=svb[j_ % 2][:], in_=p1x[j_ % NX1][:], func=AF.Copy))(j),
                  waits=cw, inc=("s", 1), tag=f"p1cast{j}")
                sig[("xb", j)] = M("s")
                sqw = [("s", sig[("xb", j)])]
                if j >= 2:
                    sqw.append(("t", sig[("sum", j - 2)]))
                sqb = h1b[:] if j % 2 == 0 else xnb[0][:, :, 2:TT + 2]
                E("vector", (lambda j_, sqb_: lambda g: g.tensor_mul(
                    out=sqb_, in0=svb[j_ % 2][:], in1=svb[j_ % 2][:]))(j, sqb),
                  waits=sqw, inc=("v", 1), tag=f"p1sq{j}")
                sig[("sq", j)] = M("v")
                sig[("sqb", j)] = sqb
            jj = j - 1
            if 0 <= jj < NT:
                sl = slice(jj * TT, (jj + 1) * TT)
                pS = 0 if jj % 2 == 0 else 2
                mmw = [("v", sig[("sq", jj)])]
                if jj == 0:
                    mmw.append(("v", v_prep))
                if jj >= 2:
                    mmw.append(("s", sig[("mq", jj - 2)]))
                for c in range(NCH):
                    E("tensor", (lambda j_, c_, pS_: lambda g: g.matmul(
                        out=pb[pS_][:, 0:TT], lhsT=ones_sq_b[:],
                        rhs=svb[j_ % 2][:, c_, :],
                        start=(c_ == 0), stop=(c_ == NCH - 1)))(jj, c, pS),
                      waits=mmw if c == 0 else (), tag=f"p1mmx{jj}_{c}")
                sqb = sig[("sqb", jj)]
                for c in range(NCH):
                    E("tensor", (lambda c_, pS_, sqb_: lambda g: g.matmul(
                        out=pb[pS_ + 1][:, 0:TT], lhsT=ones_sq_b[:],
                        rhs=sqb_[:, c_, :],
                        start=(c_ == 0), stop=(c_ == NCH - 1)))(c, pS, sqb),
                      inc=("t", 1) if c == NCH - 1 else None, tag=f"p1mmq{jj}_{c}")
                sig[("sum", jj)] = M("t")
                E("scalar", (lambda sl_, pS_: lambda g: g.activation(
                    out=rows[0:1, sl_], in_=pb[pS_][0:1, 0:TT], func=AF.Copy))(sl, pS),
                  waits=[("t", sig[("sum", jj)])], tag=f"p1m{jj}")
                ro = (jj % 4) * TT
                qw = [("v", sig[("z", jj - 4)])] if jj >= 4 else []
                E("scalar", (lambda pS_, ro_: lambda g: g.activation(
                    out=rs1[0:1, ro_:ro_ + TT], in_=pb[pS_ + 1][0:1, 0:TT],
                    func=AF.Copy))(pS, ro),
                  waits=qw, inc=("s", 1), tag=f"p1q{jj}")
                sig[("mq", jj)] = M("s")
            jp = j - 2
            if jp >= 1 and jp % 2 == 1 and jp < NT:
                pair = jp // 2
                TT2 = 2 * TT
                pbase = (pair % 2) * TT2
                psl = slice((jp - 1) * TT, (jp + 1) * TT)
                pR = 4 + pair % 4
                mmw2 = [("s", sig[("mq", jp)])]
                if pair >= 2:
                    mmw2.append(("dr", sig[("rrowp", pair - 2)]))
                    mmw2.append(("t", sig[("bcp", pair - 2)]))
                E("scalar", (lambda psl_, pb_: lambda g: g.activation(
                    out=rs2[0:1, pb_:pb_ + TT2], in_=rows[0:1, psl_],
                    func=AF.Square))(psl, pbase),
                  waits=mmw2, inc=("s", 1), tag=f"p1mm{pair}")
                sm_sq = M("s")
                E("vector", (lambda pb_: lambda g: g.scalar_tensor_tensor(
                    out=rs1[0:1, pb_:pb_ + TT2], in0=rs1[0:1, pb_:pb_ + TT2],
                    scalar=float(EPS), in1=rs2[0:1, pb_:pb_ + TT2],
                    op0=ALU.add, op1=ALU.subtract))(pbase),
                  waits=[("s", sm_sq)], inc=("v", 1), tag=f"p1var{pair}")
                sig[("var", pair)] = M("v")
                E("scalar", (lambda pb_: lambda g: g.activation(
                    out=rs1[0:1, pb_:pb_ + TT2], in_=rs1[0:1, pb_:pb_ + TT2],
                    func=AF.Ln))(pbase),
                  waits=[("v", sig[("var", pair)])], inc=("s", 1), tag=f"p1ln{pair}")
                sm_ln = M("s")
                E("scalar", (lambda pb_: lambda g: g.activation(
                    out=rs2[0:1, pb_:pb_ + TT2], in_=rs1[0:1, pb_:pb_ + TT2],
                    func=AF.Exp, scale=-0.5))(pbase),
                  waits=[("s", sm_ln)], inc=("s", 1), tag=f"p1exp{pair}")
                sig[("rstdp", pair)] = M("s")
                sig[("rstd", jp - 1)] = sig[("rstd", jp)] = sig[("rstdp", pair)]
                rrw = [("s", sig[("rstdp", pair)])]
                if pair >= 1:
                    rrw.append(("dr", sig[("rrowp", pair - 1)]))
                E("gpsimd", (lambda psl_, pb_: lambda g: g.dma_start(
                    out=rows[32:33, psl_], in_=rs2[0:1, pb_:pb_ + TT2]))(psl, pbase),
                  waits=rrw, inc=("dr", 16), tag=f"rrow{pair}")
                sig[("rrowp", pair)] = M("dr")
                sig[("rrow", jp - 1)] = sig[("rrow", jp)] = sig[("rrowp", pair)]
                bw = [("s", sig[("rstdp", pair)])]
                if pair >= 4:
                    bw.append(("v", sig[("z", jp - 8)]))
                E("tensor", (lambda pR_, pb_: lambda g: g.matmul(
                    out=pb[pR_][:, 0:TT2], lhsT=ones_r1f[0:1, :],
                    rhs=rs2[0:1, pb_:pb_ + TT2],
                    start=True, stop=True))(pR, pbase),
                  waits=bw, inc=("t", 1), tag=f"p1bc{pair}")
                sig[("bcp", pair)] = M("t")
                for jz, po in ((jp - 1, 0), (jp, TT)):
                    for c in range(NCH):
                        zw = []
                        if po == 0 and c == 0:
                            zw.append(("t", sig[("bcp", pair)]))
                        if po != 0 and c == 0:
                            zw.append(("v", M("v") if False else sig[("zhalf", pair)]))
                        E("vector", (lambda jz_, c_, pR_, po_: lambda g: g.scalar_tensor_tensor(
                            out=ofB[:, c_, :], in0=p1x[jz_ % NX1][:, c_, :], scalar=1.0,
                            in1=pb[pR_][:, po_:po_ + TT], op0=ALU.mult, op1=ALU.mult,
                            accum_out=zacc[:, c_ * NT + jz_:c_ * NT + jz_ + 1]))(jz, c, pR, po),
                          waits=zw,
                          inc=("v", 1) if (po == 0 and c == NCH - 1) else None,
                          tag=f"p1z{jz}_{c}")
                        if po == 0 and c == NCH - 1:
                            sig[("zhalf", pair)] = M("v")
                E("vector", (lambda psl_, pb_: lambda g: g.tensor_mul(
                    out=rs1[0:1, pb_:pb_ + TT2], in0=rows[0:1, psl_],
                    in1=rs2[0:1, pb_:pb_ + TT2]))(psl, pbase),
                  waits=[("s", sig[("rstdp", pair)])], inc=("v", 1), tag=f"p1cr{pair}")
                vm_cr = M("v")
                E("vector", (lambda pair_, pb_: lambda g: g.reduce_sum(
                    out=csum[0:1, pair_:pair_ + 1], in_=rs1[0:1, pb_:pb_ + TT2],
                    axis=mybir.AxisListType.X))(pair, pbase),
                  waits=[("v", vm_cr)], inc=("v", 1), tag=f"p1cs{pair}")
                sig[("z", jp - 1)] = sig[("z", jp)] = M("v")

        if KDBG:
            E("gpsimd", lambda g: g.dma_start(out=dbg_sm[:, SV + 24:SV + 32], in_=bsum[:]),
              waits=[("v", sig[("z", NT - 1)])], inc=("dd", 16), tag="dbsum_p1")
        E("vector", lambda g: g.reduce_sum(
            out=gcv[:], in_=zacc[:].rearrange("p (c j) -> p c j", j=NT),
            axis=mybir.AxisListType.X),
          waits=[("v", sig[("z", NT - 1)])], inc=("v", 1), tag="zred")
        v_zred = M("v")
        E("vector", lambda g: g.reduce_sum(out=rtmp[0:1, 6:7],
                                           in_=csum[0:1, 0:NT // 2],
                                           axis=mybir.AxisListType.X),
          inc=("v", 1), tag="csred")
        v_gc0 = M("v")
        E("tensor", lambda g: g.matmul(out=pb[7][:, 0:1], lhsT=ones_r1f[0:1, :],
                                       rhs=rtmp[0:1, 6:7], start=True, stop=True),
          waits=[("v", v_gc0)], inc=("t", 1), tag="csbc")
        t_csbc = M("t")
        E("vector", lambda g: g.tensor_scalar(out=gcv[:], in0=gcv[:],
                                              scalar1=pb[7][:, 0:1], scalar2=float(1.0 / T),
                                              op0=ALU.subtract, op1=ALU.mult),
          waits=[("t", t_csbc), ("v", v_zred)], inc=("v", 1), tag="gcv")
        v_gcvop = M("v")
        E("vector", lambda g: g.tensor_copy(out=gcvb[:], in_=gcv[:]),
          waits=[("v", v_gcvop)], inc=("v", 1), tag="gcvb")
        v_gcv = M("v")
        for m in range(4):
            for c in range(NCH):
                E("tensor", (lambda m_, c_: lambda g: g.matmul(
                    out=pb[0][:, m_:m_ + 1], lhsT=gw1b[:, c_, m_ * P:(m_ + 1) * P],
                    rhs=gcvb[:, c_:c_ + 1], start=(c_ == 0), stop=(c_ == NCH - 1)))(m, c),
                  waits=[("v", v_gcv)] if (m == 0 and c == 0) else (),
                  inc=("t", 1) if (m == 3 and c == NCH - 1) else None, tag=f"gch1_{m}_{c}")
        t_g1 = M("t")
        for m in range(4):
            E("scalar", (lambda m_: lambda g: g.activation(
                out=gch1b[:, m_:m_ + 1], in_=pb[0][:, m_:m_ + 1], func=AF.Gelu,
                bias=svec[:, SV_GB1 + m_:SV_GB1 + m_ + 1]))(m),
              waits=[("t", t_g1)] if m == 0 else (),
              inc=("s", 1) if m == 3 else None, tag=f"gce1_{m}")
        s_g1 = M("s")
        for m in range(2):
            for c in range(4):
                E("tensor", (lambda m_, c_: lambda g: g.matmul(
                    out=pb[1][:, m_:m_ + 1], lhsT=gw2b[:, c_, m_ * P:(m_ + 1) * P],
                    rhs=gch1b[:, c_:c_ + 1], start=(c_ == 0), stop=(c_ == 3)))(m, c),
                  waits=[("s", s_g1)] if (m == 0 and c == 0) else (),
                  inc=("t", 1) if (m == 1 and c == 3) else None, tag=f"gch2_{m}_{c}")
        t_g2 = M("t")
        for m in range(2):
            E("scalar", (lambda m_: lambda g: g.activation(
                out=gch2b[:, m_:m_ + 1], in_=pb[1][:, m_:m_ + 1], func=AF.Gelu,
                bias=svec[:, SV_GB2 + m_:SV_GB2 + m_ + 1]))(m),
              waits=[("t", t_g2)] if m == 0 else (),
              inc=("s", 1) if m == 1 else None, tag=f"gce2_{m}")
        s_g2 = M("s")
        for c in range(2):
            E("tensor", (lambda c_: lambda g: g.matmul(
                out=pb[2][0:1, 0:1], lhsT=gw3b[:, c_, :], rhs=gch2b[:, c_:c_ + 1],
                start=(c_ == 0), stop=(c_ == 1)))(c),
              waits=[("s", s_g2)] if c == 0 else (),
              inc=("t", 1) if c == 1 else None, tag=f"gch3_{c}")
        t_g3 = M("t")
        E("scalar", lambda g: g.activation(out=rtmp[0:1, 7:8], in_=pb[2][0:1, 0:1],
                                           func=AF.Tanh, scale=0.5, bias=rtmp[0:1, 5:6]),
          waits=[("t", t_g3)], inc=("s", 1), tag="tgf")
        s_gf = M("s")
        E("vector", lambda g: g.tensor_mul(out=rtmp[0:1, 8:9], in0=rtmp[0:1, 7:8],
                                           in1=rtmp[0:1, 2:3]),
          waits=[("s", s_gf)], inc=("v", 1), tag="c0a")
        v_c0a = M("v")
        E("vector", lambda g: g.tensor_scalar(out=rtmp[0:1, 8:9], in0=rtmp[0:1, 8:9],
                                              scalar1=0.7, scalar2=0.1,
                                              op0=ALU.mult, op1=ALU.add),
          waits=[("v", v_c0a)], tag="c0b")
        E("vector", lambda g: g.tensor_scalar(out=rtmp[0:1, 9:10], in0=rtmp[0:1, 2:3],
                                              scalar1=0.3, scalar2=None, op0=ALU.mult),
          inc=("v", 1), tag="c1")
        v_c01 = M("v")
        E("tensor", lambda g: g.matmul(out=pb[7][:, 0:1], lhsT=ones_r1f[0:1, :],
                                       rhs=rtmp[0:1, 8:9], start=True, stop=True),
          waits=[("v", v_c01)], tag="c0bc")
        E("tensor", lambda g: g.matmul(out=pb[7][:, 1:2], lhsT=ones_r1f[0:1, :],
                                       rhs=rtmp[0:1, 9:10], start=True, stop=True),
          inc=("t", 1), tag="c1bc")
        t_c01 = M("t")
        E("vector", lambda g: g.tensor_copy(out=c0col[:], in_=pb[7][:, 0:1]),
          waits=[("t", t_c01)], tag="c0cp")
        E("vector", lambda g: g.tensor_copy(out=c1col[:], in_=pb[7][:, 1:2]),
          inc=("v", 1), tag="c1cp")
        v_cc = M("v")

        if KDBG:
            E("gpsimd", lambda g: g.dma_start(out=dbg_sm[:, SV + 32:SV + 40], in_=bsum[:]),
              waits=[("v", v_cc)], inc=("dd", 16), tag="dbsum_gc")
        def pfx_ld(k):
            sl = slice(k * TT, (k + 1) * TT)
            if k < 3:
                jlast = max(j for j in range(NT) if j % 7 == k % 3)
                lw = [("v", sig[("z", jlast)])]
            else:
                lw = [(f"do{(k - 3) % 3}", sig[("st", k - 3)])]
            E("sync", (lambda k_: lambda g: g.dma_start(
                out=xs2[k_ % 3][:], in_=xt_r[k_]))(k),
              waits=lw, inc=(f"dx{k % 3}", 16), tag=f"ld{k}")
            sig[("ld", k)] = M(f"dx{k % 3}")
            bw = [("v", v_cc if k == 0 else sig[("xn", k - 1)])]
            if ("bank", 4) in sig:
                bw.append(("s", max(sig[("bank", 4)], sig[("bank", 5)])))
            E("tensor", (lambda sl_: lambda g: g.matmul(
                out=pb[4][:, 0:TT], lhsT=ones_r1f[0:1, :], rhs=rows[0:1, sl_],
                start=True, stop=True))(sl), waits=bw, tag=f"bcm{k}")
            E("tensor", (lambda k_, sl_: lambda g: g.matmul(
                out=pb[5][:, 0:TT], lhsT=ones_r1f[32:33, :], rhs=rows[32:33, sl_],
                start=True, stop=True))(k, sl),
              waits=[("dr", sig[("rrow", k)])], inc=("t", 1), tag=f"bcr{k}")
            sig[("bc2", k)] = M("t")

        def pfx_av(k):
            sl = slice(k * TT, (k + 1) * TT)
            avw = [(f"dx{k % 3}", sig[("ld", k)]), ("t", sig[("bc2", k)])]
            if k >= 2:
                avw.append(("v", sig[("upd", k - 2, 2)]))
            elif k == 1:
                avw.append(("v", sig[("car", 0)]))
            E("vector", (lambda k_: lambda g: g.tensor_sub(
                out=scanb[:], in0=xs2[k_ % 3][:],
                in1=pb[4][:, 0:TT].unsqueeze(1).broadcast_to([P, NCH, TT])))(k),
              waits=avw, inc=("v", 1), tag=f"xn1_{k}")
            vm_x1 = M("v")
            E("vector", (lambda k_: lambda g: g.tensor_mul(
                out=xn[k_ % 2][:], in0=scanb[:],
                in1=pb[5][:, 0:TT].unsqueeze(1).broadcast_to([P, NCH, TT])))(k),
              waits=[("v", vm_x1)], inc=("v", 1), tag=f"xn2_{k}")
            sig[("xn", k)] = M("v")
            hw = [("t", sig[("l1", k - 2, 2, NCH - 1)])] if k >= 2 else []
            hw.append(("v", sig[("xn", k)]))
            if k == 0:
                E("vector", lambda g: g.memset(xnc[0][:, :, 0:2], 0.0),
                  waits=hw, tag="hlc0")
            else:
                E("vector", (lambda k_: lambda g: g.tensor_copy(
                    out=xnc[(k_ - 1) % 2][:, :, TT + 2:TT + 4],
                    in_=xn[k_ % 2][:, :, 0:2]))(k), waits=hw, tag=f"hrc{k - 1}")
                E("vector", (lambda k_: lambda g: g.tensor_copy(
                    out=xnc[k_ % 2][:, :, 0:2],
                    in_=xn[(k_ - 1) % 2][:, :, TT - 2:TT]))(k), tag=f"hlc{k}")
            E("vector", (lambda k_: lambda g: g.tensor_copy(
                out=xnc[k_ % 2][:, :, 2:TT + 2], in_=xn[k_ % 2][:]))(k),
              tag=f"xncc{k}")
            E("vector", (lambda k_: lambda g: g.tensor_copy(
                out=xnb[k_ % 2][:, :, 2:TT + 2], in_=xn[k_ % 2][:]))(k),
              inc=("v", 1), tag=f"xnbc{k}")
            sig[("xnb", k)] = M("v")
            if KDBG:
                E("gpsimd", (lambda k_, sl_: lambda g: g.dma_start(
                    out=dbg_xn_r[:, :, sl_], in_=xn[k_ % 2][:]))(k, sl),
                  waits=[("v", sig[("xn", k)])], inc=("dd", 16), tag=f"dxn{k}")
                E("gpsimd", (lambda k_, sl_: lambda g: g.dma_start(
                    out=dbg_sv_r[:, :, sl_], in_=svb[k_ % 2][:]))(k, sl),
                  waits=[("v", sig[("sv", k)])], inc=("dd", 16), tag=f"dsv{k}")

        def pfx_b(k):
            for c in range(NCH):
                E("vector", (lambda k_, c_: lambda g: g.tensor_tensor_scan(
                    out=scanb[:, c_, :], data0=a_s[:, c_:c_ + 1].broadcast_to([P, TT]),
                    data1=xn[k_ % 2][:, c_, :], initial=carry[:, c_:c_ + 1],
                    op0=ALU.mult, op1=ALU.add))(k, c),
                  inc=("v", 1) if c == NCH - 1 else None, tag=f"scan{k}_{c}")
            sig[("scn", k)] = M("v")
            svw = [("v", sig[("scn", k)])]
            if k == 0:
                svw.append(("t", sig[("sum", 14)]))
            elif k == 1:
                svw.append(("t", sig[("sum", 15)]))
            else:
                svw.append(("t", sig[("attnmm", k - 2, 7)]))
            E("vector", (lambda k_: lambda g: g.tensor_copy(
                out=svb[k_ % 2][:], in_=scanb[:]))(k),
              waits=svw, inc=("v", 1), tag=f"svc{k}")
            sig[("sv", k)] = M("v")
            E("vector", lambda g: g.tensor_copy(out=carry[:], in_=scanb[:, :, TT - 1]),
              inc=("v", 1), tag=f"carry{k}")
            sig[("car", k)] = M("v")

        def attn_emit(k, erange):
            for e in erange:
                bank = 2 + e % 4
                w_ = []
                if e == erange[0]:
                    w_ += [("v", sig[("sv", k)]), ("v", sig[("hal", k)]),
                           ("g", g_diag)]
                    if k == 0:
                        w_.append(("v", v_weights))
                if bank in (4, 5):
                    w_.append(("v", sig[("xn", k + 1)] if ("xn", k + 1) in sig
                               else sig[("xn", k)]))
                w_.append(("s", sig.get(("bank", bank), s_gf)))
                for c in range(NCH):
                    E("tensor", (lambda k_, e_, c_, b_: lambda g: g.matmul(
                        out=pb[b_][:, 0:TT], lhsT=wtsb[:, c_, e_ * P:(e_ + 1) * P],
                        rhs=svb[k_ % 2][:, c_, :], start=(c_ == 0), stop=False))(k, e, c, bank),
                      waits=w_ if c == 0 else (), tag=f"aW{k}_{e}_{c}")
                for kk in range(5):
                    E("tensor", (lambda k_, e_, kk_, b_: lambda g: g.matmul(
                        out=pb[b_][:, 0:TT], lhsT=diag[:, e_ * 5 + kk_, :],
                        rhs=xnc[k_ % 2][:, e_, kk_:kk_ + TT],
                        start=False, stop=(kk_ == 4)))(k, e, kk, bank),
                      inc=("t", 1) if kk == 4 else None, tag=f"aC{k}_{e}_{kk}")
                sig[("attnmm", k, e)] = M("t")
                E("scalar", (lambda k_, e_, b_: lambda g: g.activation(
                    out=shalf[k_ % 2][:, e_, :], in_=pb[b_][:, 0:TT],
                    func=AF.Identity,
                    scale=1.0, bias=bsum[:, e_:e_ + 1]))(k, e, bank),
                  waits=([("t", sig[("attnmm", k, e)])]
                         + ([("v", sig[("fin", k - 2)])] if (e == 0 and k >= 2) else [])),
                  inc=("s", 1), tag=f"evY{k}_{e}")
                sig[("evY", k, e)] = M("s")
                sig[("bank", bank)] = M("s")
            if KDBG and erange[-1] == 7:
                sl_d = slice(k * TT, (k + 1) * TT)
                E("gpsimd", (lambda k_, sl_: lambda g: g.dma_start(
                    out=dbg_s_r[:, :, sl_], in_=shalf[k_ % 2][:]))(k, sl_d),
                  waits=[("s", sig[("evY", k, 7)])], inc=("dd", 16), tag=f"ds{k}")

        def lchain(k, c):
            for m in range(NCH):
                bank = m % 2
                w_ = []
                if m == 0:
                    if c == 0:
                        w_.append(("v", sig[("xnb", k)]))
                        if k == 0:
                            w_.append(("v", v_weights))
                    if c > 0:
                        w_.append(("s", sig[("l3e", k, c - 1)]))
                    elif k > 0:
                        w_.append(("s", sig[("l3e", k - 1, 2)]))
                if m >= 2:
                    w_.append(("s", sig[("l1e", k, c, m - 2)]))
                for cc in range(NCH):
                    ccw = list(w_) if cc == 0 else []
                    if c > 0 and m == 0 and cc in (0, 2, 6):
                        ccw.append(sig[("inbp", k, c, cc)])
                    E("tensor", (lambda k_, m_, cc_, b_: lambda g: g.matmul(
                        out=pb[b_][:, 0:TT], lhsT=w1b[:, cc_, m_ * P:(m_ + 1) * P],
                        rhs=xnb[k_ % 2][:, cc_, 2:TT + 2], start=(cc_ == 0),
                        stop=(cc_ == NCH - 1)))(k, m, cc, bank),
                      waits=ccw,
                      inc=("t", 1) if cc == NCH - 1 else None, tag=f"L1_{k}_{c}_{m}_{cc}")
                sig[("l1", k, c, m)] = M("t")
                E("scalar", (lambda m_, b_: lambda g: g.activation(
                    out=h1b[:, m_, :], in_=pb[b_][:, 0:TT], func=AF.Gelu,
                    bias=svec[:, SV_B1 + m_:SV_B1 + m_ + 1]))(m, bank),
                  waits=[("t", sig[("l1", k, c, m)])],
                  inc=("s", 1), tag=f"l1e{k}_{c}_{m}")
                sig[("l1e", k, c, m)] = M("s")
            for m in range(4):
                bank = m % 2
                w_ = []
                if m >= 2:
                    w_.append(("s", sig[("l2e", k, c, m - 2)]))
                for cc in range(NCH):
                    ccw = list(w_) if cc == 0 else []
                    if m == 0 and cc == 0:
                        ccw.append(("s", sig[("l1e", k, c, NCH - 2)]))
                    if m == 0 and cc == NCH - 1:
                        ccw.append(("s", sig[("l1e", k, c, NCH - 1)]))
                    E("tensor", (lambda m_, cc_, b_: lambda g: g.matmul(
                        out=pb[b_][:, 0:TT], lhsT=w2b[:, cc_, m_ * P:(m_ + 1) * P],
                        rhs=h1b[:, cc_, :], start=(cc_ == 0),
                        stop=(cc_ == NCH - 1)))(m, cc, bank),
                      waits=ccw,
                      inc=("t", 1) if cc == NCH - 1 else None, tag=f"L2_{k}_{c}_{m}")
                sig[("l2", k, c, m)] = M("t")
                E("scalar", (lambda m_, b_: lambda g: g.activation(
                    out=h2b[:, m_, :], in_=pb[b_][:, 0:TT], func=AF.Gelu,
                    bias=svec[:, SV_B2 + m_:SV_B2 + m_ + 1]))(m, bank),
                  waits=[("t", sig[("l2", k, c, m)])],
                  inc=("s", 1), tag=f"l2e{k}_{c}_{m}")
                sig[("l2e", k, c, m)] = M("s")
            for m in range(2):
                bank = m % 2
                for cc in range(4):
                    ccw = []
                    if m == 0 and cc == 0:
                        ccw.append(("s", sig[("l2e", k, c, 2)]))
                    if m == 0 and cc == 3:
                        ccw.append(("s", sig[("l2e", k, c, 3)]))
                    E("tensor", (lambda m_, cc_, b_: lambda g: g.matmul(
                        out=pb[b_][:, 0:TT], lhsT=w3b[:, cc_, m_ * P:(m_ + 1) * P],
                        rhs=h2b[:, cc_, :], start=(cc_ == 0), stop=(cc_ == 3)))(m, cc, bank),
                      waits=ccw,
                      inc=("t", 1) if cc == 3 else None, tag=f"L3_{k}_{c}_{m}")
                sig[("l3", k, c, m)] = M("t")
                E("scalar", (lambda m_, b_: lambda g: g.activation(
                    out=h3b[:, m_, :], in_=pb[b_][:, 0:TT], func=AF.Gelu,
                    bias=svec[:, SV_B3 + m_:SV_B3 + m_ + 1]))(m, bank),
                  waits=[("t", sig[("l3", k, c, m)])],
                  inc=("s", 1), tag=f"l3e{k}_{c}_{m}")
                sig[("l3ep", k, c, m)] = M("s")
            sig[("l3e", k, c)] = M("s")
            if c > 0:
                t4w = [("s", sig[("t4", k, c - 1)])]
            elif k > 0:
                t4w = [("s", sig[("t4", k - 1, 2)])]
            else:
                t4w = []
            for cc in range(2):
                ccw = (list(t4w) if cc == 0 else []) + [("s", sig[("l3ep", k, c, cc)])]
                E("tensor", (lambda cc_: lambda g: g.matmul(
                    out=pb[6][0:1, 0:TT], lhsT=w4b[:, cc_, :], rhs=h3b[:, cc_, :],
                    start=(cc_ == 0), stop=(cc_ == 1)))(cc),
                  waits=ccw,
                  inc=("t", 1) if cc == 1 else None, tag=f"L4_{k}_{c}")
            sig[("l4", k, c)] = M("t")
            E("scalar", lambda g: g.activation(
                out=t4b[:], in_=pb[6][0:1, 0:TT], func=AF.Tanh, scale=0.5,
                bias=rtmp[0:1, 4:5]),
              waits=[("t", sig[("l4", k, c)])], inc=("s", 1), tag=f"t4_{k}_{c}")
            sig[("t4", k, c)] = M("s")

        def OB(k, c):
            return of if (k + c) % 2 == 0 else ofB

        def cpl_upd(k, c):
            w_ = [("s", sig[("t4", k, c)])]
            if c > 0:
                w_.append(("v", sig[("upd", k, c - 1)]))
            elif k > 0:
                w_.append(("v", sig[("upd", k - 1, 2)]))
            else:
                w_.append(("v", v_cc))
            E("tensor", lambda g: g.matmul(
                out=pb[7][:, 0:TT], lhsT=ones_r1b[:], rhs=t4b[:],
                start=True, stop=True), waits=w_, inc=("t", 1), tag=f"cbc{k}_{c}")
            sig[("cbc", k, c)] = M("t")
            E("vector", lambda g: g.tensor_scalar(
                out=cplb[:], in0=pb[7][:, 0:TT], scalar1=c1col[:], scalar2=c0col[:],
                op0=ALU.mult, op1=ALU.add),
              waits=[("t", sig[("cbc", k, c)])], inc=("v", 1), tag=f"cpl{k}_{c}")
            sig[("cpl", k, c)] = M("v")
            st = xn[k % 2] if c == 0 else OB(k, c - 1)
            ob = OB(k, c)
            cb2 = cplb[:].unsqueeze(1).broadcast_to([P, 2, TT])
            cb4 = cplb[:].unsqueeze(1).broadcast_to([P, 4, TT])
            E("vector", (lambda st_, ob_: lambda g: g.tensor_tensor(
                out=ob_[:, 0:2, :], in0=cb2, in1=st_[:, 4:6, :],
                op=ALU.mult))(st, ob),
              waits=[("v", sig[("cpl", k, c)])], tag=f"twA{k}_{c}")
            E("vector", (lambda st_, ob_: lambda g: g.tensor_tensor(
                out=ob_[:, 2:4, :], in0=cb2, in1=st_[:, 6:8, :],
                op=ALU.mult))(st, ob), tag=f"twA2{k}_{c}")
            E("vector", (lambda st_, ob_: lambda g: g.tensor_tensor(
                out=ob_[:, 4:8, :], in0=cb4, in1=st_[:, 0:4, :],
                op=ALU.mult))(st, ob), inc=("v", 1), tag=f"twB{k}_{c}")
            vm_tw = M("v")
            E("vector", (lambda st_, ob_: lambda g: g.tensor_add(
                out=ob_[:, 0:2, :], in0=st_[:, 0:2, :], in1=ob_[:, 0:2, :]))(st, ob),
              waits=[("v", vm_tw)], inc=("v", 1), tag=f"u1{k}_{c}")
            sig[("u1", k, c)] = M("v")
            E("vector", (lambda st_, ob_: lambda g: g.tensor_sub(
                out=ob_[:, 2:6, :], in0=st_[:, 2:6, :], in1=ob_[:, 2:6, :]))(st, ob),
              inc=("v", 1), tag=f"u2{k}_{c}")
            sig[("u2", k, c)] = M("v")
            E("vector", (lambda st_, ob_: lambda g: g.tensor_add(
                out=ob_[:, 6:8, :], in0=st_[:, 6:8, :], in1=ob_[:, 6:8, :]))(st, ob),
              inc=("v", 1), tag=f"u3{k}_{c}")
            sig[("upd", k, c)] = M("v")

        def inb_emit(k, c):
            tmark = sig[("l1", k, c - 1, NCH - 1)]
            src = OB(k, c - 1)
            for (lo, hi) in [(0, 2), (2, 6), (6, 8)]:
                E("vector", (lambda k_, lo_, hi_, s_: lambda g: g.tensor_copy(
                    out=xnb[k_ % 2][:, lo_:hi_, 2:TT + 2],
                    in_=s_[:, lo_:hi_, :]))(k, lo, hi, src),
                  waits=([("t", tmark), ("v", sig[("upd", k, c - 1)])]
                         if lo == 0 else ()), inc=("v", 1),
                  tag=f"inb{k}_{c}_{lo}")
                sig[("inbp", k, c, lo)] = ("v", M("v"))
            sig[("inb", k, c)] = M("v")

        def tail(k):
            sl = slice(k * TT, (k + 1) * TT)
            fb = OB(k, 2)
            if KDBG:
                E("gpsimd", (lambda k_, sl_, f_: lambda g: g.dma_start(
                    out=dbg_of_r[:, :, sl_], in_=f_[:]))(k, sl, fb),
                  waits=[("v", sig[("upd", k, 2)])], inc=("dd", 16), tag=f"dof{k}")
            tgw = [("v", sig[("upd", k, 2)])]
            if ("xn", k + 2) in sig:
                tgw.append(("v", sig[("xn", k + 2)]))
            E("scalar", (lambda f_: lambda g: g.activation(
                out=scanb[:], in_=f_[:], func=AF.Tanh, scale=0.5))(fb),
              waits=tgw, inc=("s", 1), tag=f"tg{k}")
            sig[("tg", k)] = M("s")
            E("vector", (lambda f_: lambda g: g.scalar_tensor_tensor(
                out=scanb[:], in0=scanb[:], scalar=1.0, in1=f_[:],
                op0=ALU.add, op1=ALU.mult))(fb),
              waits=[("s", sig[("tg", k)])], inc=("v", 1), tag=f"g2{k}")
            vm_g2 = M("v")
            E("vector", (lambda k_: lambda g: g.tensor_mul(
                out=scanb[:], in0=scanb[:], in1=shalf[k_ % 2][:]))(k),
              waits=[("v", vm_g2)], inc=("v", 1), tag=f"pr{k}")
            vm_pr = M("v")
            E("vector", (lambda k_: lambda g: g.scalar_tensor_tensor(
                out=xs2[k_ % 3][:], in0=scanb[:], scalar=0.5, in1=xs2[k_ % 3][:],
                op0=ALU.mult, op1=ALU.add))(k),
              waits=[("v", vm_pr)], inc=("v", 1), tag=f"fin{k}")
            sig[("fin", k)] = M("v")
            E("gpsimd", (lambda k_: lambda g: g.dma_start(
                out=yt_r[k_], in_=xs2[k_ % 3][:]))(k),
              waits=[("v", sig[("fin", k)])], inc=(f"do{k % 3}", 16), tag=f"st{k}")
            sig[("st", k)] = M(f"do{k % 3}")

        pfx_ld(0)
        pfx_av(0)
        pfx_b(0)
        for k in range(1, NT + 1):
            if k <= NT - 1:
                pfx_ld(k)
                pfx_av(k)
                sig[("hal", k - 1)] = sig[("xnb", k)]
            if k > NT - 1:
                E("vector", lambda g: g.memset(
                    xnc[(NT - 1) % 2][:, :, TT + 2:TT + 4], 0.0),
                  inc=("v", 1), tag="hrlastc")
                sig[("hal", NT - 1)] = M("v")
            kk = k - 1
            lchain(kk, 0)
            cpl_upd(kk, 0)
            inb_emit(kk, 1)
            if kk >= 1:
                tail(kk - 1)
            attn_emit(kk, [0, 1, 2, 3])
            lchain(kk, 1)
            cpl_upd(kk, 1)
            inb_emit(kk, 2)
            if k <= NT - 1:
                pfx_b(k)
            attn_emit(kk, [4, 5, 6, 7])
            lchain(kk, 2)
            cpl_upd(kk, 2)
        tail(NT - 1)

        if KDBG:
            E("gpsimd", lambda g: g.dma_start(
                out=dbg_w.rearrange("p (c m) -> p c m", c=NCH), in_=wtsb[:]),
              waits=[("s", M("s")), ("v", M("v"))], inc=("dd", 16), tag="dw")
            E("gpsimd", lambda g: g.dma_start(
                out=dbg_dg.rearrange("p (c m) -> p c m", c=40), in_=diag[:]),
              inc=("dd", 16), tag="ddg")
            E("gpsimd", lambda g: g.dma_start(out=dbg_sm[:, 0:SV], in_=svec[:]),
              waits=[("s", M("s")), ("v", M("v"))], inc=("dd", 16), tag="dsm1")
            E("gpsimd", lambda g: g.dma_start(out=dbg_sm[:, SV:SV + 8], in_=bsum[:]),
              inc=("dd", 16), tag="dsm2")
            E("gpsimd", lambda g: g.dma_start(out=dbg_sm[:, SV + 8:SV + 16], in_=a_s[:]),
              inc=("dd", 16), tag="dsm3")
        E("gpsimd", lambda g: g.engine_nop(),
          waits=[("do0", M("do0")), ("do1", M("do1")), ("do2", M("do2")),
                 ("dd", M("dd"))],
          tag="final_wait")

        sch.simulate()

        def _runner(name):
            def _body(eng):
                for fn, waits, inc, tag in sch.ops[name]:
                    for s_, v_ in waits:
                        eng.wait_ge(sems[s_], v_)
                    inst = fn(eng)
                    if inc is not None:
                        inst.then_inc(sems[inc[0]], inc[1])
            return _body

        block.sync(_runner("sync"))
        block.gpsimd(_runner("gpsimd"))
        block.vector(_runner("vector"))
        block.scalar(_runner("scalar"))
        block.tensor(_runner("tensor"))

    return nc


def kernel(**inputs):
    x = np.asarray(inputs["L0_attn_x"], dtype=np.float32)
    W = np.asarray(inputs["L0_attn_W"], dtype=np.float32)
    svec = np.zeros((P, SV), dtype=np.float32)

    def col_layout(v, nch):
        return np.ascontiguousarray(np.asarray(v, dtype=np.float32).reshape(nch, P).T)

    svec[:, SV_DL:SV_DL + 8] = col_layout(inputs["L0_attn_decay_logit"], 8)
    svec[:, SV_BATT:SV_BATT + 8] = col_layout(inputs["L0_attn_b"], 8)
    svec[:, SV_CONVB:SV_CONVB + 8] = col_layout(inputs["L0_attn_conv_b"], 8)
    svec[:, SV_B1:SV_B1 + 8] = col_layout(inputs["L0_ffn_cn_b1"], 8)
    svec[:, SV_B2:SV_B2 + 4] = col_layout(inputs["L0_ffn_cn_b2"], 4)
    svec[:, SV_B3:SV_B3 + 2] = col_layout(inputs["L0_ffn_cn_b3"], 2)
    svec[:, SV_GB1:SV_GB1 + 4] = col_layout(inputs["L0_ffn_gc_b1"], 4)
    svec[:, SV_GB2:SV_GB2 + 2] = col_layout(inputs["L0_ffn_gc_b2"], 2)
    cw = np.asarray(inputs["L0_attn_conv_w"], np.float32).reshape(D, 5)
    svec[:, SV_CW:SV_CW + 40] = cw.reshape(NCH, P, 5).transpose(1, 0, 2).reshape(P, 40)
    srow = np.zeros((1, 4), dtype=np.float32)
    srow[0, 0] = np.asarray(inputs["L0_ffn_cn_b4"], np.float32).reshape(())
    srow[0, 1] = np.asarray(inputs["L0_ffn_gc_b3"], np.float32).reshape(())
    srow[0, 2] = np.asarray(inputs["L0_ffn_adaptive_range"], np.float32).reshape(())

    common = {
        "w_ret_t": np.ascontiguousarray(W.T),
        "cn_w1": np.ascontiguousarray(np.asarray(inputs["L0_ffn_cn_w1"], np.float32)),
        "cn_w2": np.ascontiguousarray(np.asarray(inputs["L0_ffn_cn_w2"], np.float32)),
        "cn_w3": np.ascontiguousarray(np.asarray(inputs["L0_ffn_cn_w3"], np.float32)),
        "cn_w4": np.ascontiguousarray(np.asarray(inputs["L0_ffn_cn_w4"], np.float32).reshape(2, P).T),
        "gc_w1": np.ascontiguousarray(np.asarray(inputs["L0_ffn_gc_w1"], np.float32)),
        "gc_w2": np.ascontiguousarray(np.asarray(inputs["L0_ffn_gc_w2"], np.float32)),
        "gc_w3": np.ascontiguousarray(np.asarray(inputs["L0_ffn_gc_w3"], np.float32).reshape(2, P).T),
        "svec": svec,
        "srow": srow,
    }
    in_maps = []
    for b in range(N_CORES):
        m = dict(common)
        m["x_t"] = np.ascontiguousarray(
            x[b].T.reshape(NCH, P, NT, TT).transpose(2, 1, 0, 3).reshape(NT, P, NCH * TT))
        in_maps.append(m)

    nc = build_nc()
    res = run_bass_kernel_spmd(nc, in_maps, core_ids=list(range(N_CORES)))
    outs = []
    for r in res.results:
        yt = np.asarray(r["y_t"]).reshape(NT, P, NCH, TT).transpose(2, 1, 0, 3)
        outs.append(yt.reshape(D, T).T)
    out = np.stack(outs, axis=0)
    return np.ascontiguousarray(out).astype(np.float32)
